# version 29
# baseline (speedup 1.0000x reference)
import numpy as np
import ml_dtypes
import concourse.bass as bass
import concourse.mybir as mybir
from concourse.bass_utils import run_bass_kernel_spmd
from contextlib import ExitStack

F32 = mybir.dt.float32
BF16 = mybir.dt.bfloat16
FP16 = mybir.dt.float16
ALU = mybir.AluOpType
ACTF = mybir.ActivationFunctionType
AX = mybir.AxisListType
BFD = ml_dtypes.bfloat16

MAGIC = float(np.float32(1.5 * 2 ** 23))
SCALE_ATTN = float(128 ** (-0.5))
EPS_F32 = float(np.finfo(np.float32).eps)
SHRINKS = [float(np.float32(s)) for s in np.linspace(0.4, 1.0, 16)]
ISHR = [float(np.float32(1.0 / np.float64(s))) for s in np.linspace(0.4, 1.0, 16)]

T = 2048
H = 16
D = 128
C = 2048
KC = 16
NCH = 16
QF = 6144
FH = 8192
N_CORES = 8

TAPS = ()


class Sem:
    __slots__ = ("h", "n")

    def __init__(self, h):
        self.h = h
        self.n = 0


class B:

    def __init__(self, nc, es, taps):
        self.nc = nc
        self.es = es
        self.taps = taps
        self.tap_aps = {}
        self.q = {e: [] for e in ("sy", "ve", "sc", "pe", "gp")}
        self.waited = {e: {} for e in ("sy", "ve", "sc", "pe", "gp")}
        self.sems = {}
        for name in ("dw", "dx", "dw1", "dx1", "dq", "dg", "cc", "pe", "ve", "ac",
                     "gpc", "dsh", "dst", "dzq"):
            self.sems[name] = Sem(es.enter_context(nc.semaphore(name)))
        self.selfsem = {"ve": self.sems["ve"], "sc": self.sems["ac"],
                        "gp": self.sems["gpc"]}

    def S(self, name):
        return self.sems[name]

    def op(self, eng, fn, waits=(), inc=None, inck=1):
        ws = []
        if eng in self.selfsem:
            s = self.selfsem[eng]
            if s.n > 0:
                ws.append((s, s.n))
            if inc is None:
                inc, inck = s, 1
        if eng == "sc" and self.sems["ve"].n > 0:
            ws.append((self.sems["ve"], self.sems["ve"].n))
        if eng == "gp":
            for nm in ("ve", "ac"):
                if self.sems[nm].n > 0:
                    ws.append((self.sems[nm], self.sems[nm].n))
        for (s, v) in waits:
            if s is None or v is None or v <= 0:
                continue
            ws.append((s, v))
        ws2 = []
        for (s, v) in ws:
            prev = self.waited[eng].get(id(s), 0)
            if v > prev:
                ws2.append((s, v))
                self.waited[eng][id(s)] = v
        tok = None
        if inc is not None:
            inc.n += inck
            tok = inc.n
        self.q[eng].append((fn, ws2, inc, inck))
        return tok

    def dma(self, eng, sem, dst, src, waits=()):
        return self.op(eng, lambda e, d=dst, s=src: e.dma_start(out=d, in_=s),
                       waits=waits, inc=self.S(sem), inck=16)

    def join(self):
        allsems = [(s, s.n) for s in self.sems.values() if s.n > 0]
        for eng in self.q:
            self.op(eng, lambda e: e.nop(), waits=allsems,
                    inc=None if eng in self.selfsem else None)

    def emit(self, block):
        def mk(key):
            def run(eng):
                for (fn, ws, inc, inck) in self.q[key]:
                    for (s, v) in ws:
                        eng.wait_ge(s.h, v)
                    ins = fn(eng)
                    if inc is not None:
                        if ins is None:
                            ins = eng.nop()
                        ins.then_inc(inc.h, inck)
            return run
        block.sync(mk("sy"))
        block.vector(mk("ve"))
        block.scalar(mk("sc"))
        block.tensor(mk("pe"))
        block.gpsimd(mk("gp"))


def _host_shard(L0_attn_x, L0_attn_qkv_w, L0_attn_qkv_norm_w, L0_attn_proj_w,
                L0_attn_proj_norm_w, L0_attn_norm_w, L0_attn_s, L0_ffn_w_fc,
                L0_ffn_w_proj):
    x = np.asarray(L0_attn_x, dtype=np.float32)
    wq = np.asarray(L0_attn_qkv_w, dtype=np.float32)
    wp = np.asarray(L0_attn_proj_w, dtype=np.float32)
    wf = np.asarray(L0_ffn_w_fc, dtype=np.float32)
    wj = np.asarray(L0_ffn_w_proj, dtype=np.float32)
    g1 = np.asarray(L0_attn_qkv_norm_w, dtype=np.float32).reshape(1, C)
    g2 = np.asarray(L0_attn_norm_w, dtype=np.float32).reshape(1, C)
    g3 = np.asarray(L0_attn_proj_norm_w, dtype=np.float32).reshape(1, C)
    srow = np.asarray(L0_attn_s, dtype=np.float32).reshape(1, H)
    maps = []
    for c in range(N_CORES):
        b, seg = c // 4, c % 4
        t0 = seg * T
        r0 = c * 256
        rj = c * 1024
        cv = np.zeros((1, 8), dtype=np.float32)
        for j in range(3):
            cv[0, j] = 1.0 if j < seg else 0.0
            cv[0, 3 + j] = float(seg - 1 - j) if j < seg else 0.0
        maps.append({
            "xT": np.ascontiguousarray(x[b, t0:t0 + T, :].T),
            "wq_sh": np.ascontiguousarray(wq[:, r0:r0 + 256].T),
            "wp_sh": np.ascontiguousarray(wp[:, r0:r0 + 256].T),
            "wf_sh": np.ascontiguousarray(wf[:, r0:r0 + 256].T),
            "wf_nat": np.ascontiguousarray(wf[:, r0:r0 + 256]),
            "wj_sh": np.ascontiguousarray(wj[:, rj:rj + 1024].T),
            "wj_nat": np.ascontiguousarray(wj[:, rj:rj + 1024]),
            "g1": g1.copy(), "g2": g2.copy(), "g3": g3.copy(),
            "srow": srow.copy(), "cvec": cv,
        })
    return maps


class Builder:
    def __init__(self, taps=()):
        self.taps = tuple(taps)
        nc = bass.Bass("TRN2", target_bir_lowering=False, debug=False)
        self.nc = nc
        self.xT = nc.dram_tensor("xT", [C, T], F32, kind="ExternalInput").ap()
        self.wq_sh = nc.dram_tensor("wq_sh", [256, QF], F32, kind="ExternalInput").ap()
        self.wp_sh = nc.dram_tensor("wp_sh", [256, C], F32, kind="ExternalInput").ap()
        self.wf_sh = nc.dram_tensor("wf_sh", [256, FH], F32, kind="ExternalInput").ap()
        self.wf_nat = nc.dram_tensor("wf_nat", [FH, 256], F32, kind="ExternalInput").ap()
        self.wj_sh = nc.dram_tensor("wj_sh", [1024, C], F32, kind="ExternalInput").ap()
        self.wj_nat = nc.dram_tensor("wj_nat", [C, 1024], F32, kind="ExternalInput").ap()
        self.g1 = nc.dram_tensor("g1", [1, C], F32, kind="ExternalInput").ap()
        self.g2 = nc.dram_tensor("g2", [1, C], F32, kind="ExternalInput").ap()
        self.g3 = nc.dram_tensor("g3", [1, C], F32, kind="ExternalInput").ap()
        self.srow_i = nc.dram_tensor("srow", [1, H], F32, kind="ExternalInput").ap()
        self.cvec_i = nc.dram_tensor("cvec", [1, 8], F32, kind="ExternalInput").ap()
        self.out = nc.dram_tensor("out", [C, T], F32, kind="ExternalOutput").ap()

        dr = lambda n, shp, dt=F32: nc.dram_tensor(n, shp, dt)
        self.d_sums_i = dr("d_sums_i", [1, 2]); self.d_sums_o = dr("d_sums_o", [1, 2])
        self.d_wq_s = dr("d_wq_s", [256, QF], BF16)
        self.d_wp_s = dr("d_wp_s", [256, C], BF16)
        self.d_wf_s = dr("d_wf_s", [256, FH], BF16)
        self.d_wj_s = dr("d_wj_s", [1024, C], FP16)
        self.d_wq = dr("d_wq", [C, QF], BF16)
        self.d_wp = dr("d_wp", [C, C], BF16)
        self.d_wf = dr("d_wf", [C, FH], BF16)
        self.d_wj = dr("d_wj", [FH, C], FP16)
        self.d_st_i = dr("d_st_i", [128, H * D])
        self.d_st_o = dr("d_st_o", [512, H * D])
        self.d_q = dr("d_q", [128, H * T], BF16)
        self.d_k = dr("d_k", [128, H * T], BF16)
        self.d_v = dr("d_v", [128, H * T], BF16)
        self.d_o = dr("d_o", [128, H * T], FP16)
        self.d_z = dr("d_z", [128, H * T], BF16)
        self.d_z1 = dr("d_z1", [128, KC * T])
        self.d_z1b = dr("d_z1b", [128, KC * T], BF16)
        self.d_brow = dr("d_brow", [20, FH])
        self.d_dbg = dr("d_dbg", [8, 2048])
        self.d_dbg2 = dr("d_dbg2", [128, KC * 1024])
        self.d_dbg3 = dr("d_dbg3", [128, 8 * 1024], FP16)
        self.d_rrow = dr("d_rrow", [20, FH])

        it = lambda a, n: nc.inline_tensor(np.ascontiguousarray(a), name=n)
        self.c_identb = it(np.eye(128, dtype=BFD), "c_identb")
        self.c_identf = it(np.eye(128, dtype=np.float32), "c_identf")
        ii, jj = np.meshgrid(np.arange(128), np.arange(128), indexing="ij")
        self.c_diffT = it((jj - ii).astype(np.float32), "c_diffT")
        self.c_causT = it((jj >= ii).astype(np.float32), "c_causT")
        self.c_iota16 = it(np.tile(np.arange(128, dtype=np.float32), (16, 1)), "c_iota16")
        self.c_rev = it(np.tile(128.0 - np.arange(128, dtype=np.float32), (16, 1)), "c_rev")
        self.c_ones_row = it(np.ones((1, 128), dtype=np.float32), "c_ones_row")
        self.c_ones_col = it(np.ones((128, 1), dtype=np.float32), "c_ones_col")
        lt = np.zeros((128, 32 * 128), dtype=np.float32)
        for i in range(16):
            for sl in range(2):
                c0 = (i * 2 + sl) * 128
                lt[0:64, c0 + 8 * i + 2 * sl] = 1
                lt[64:128, c0 + 8 * i + 2 * sl + 1] = 1
        self.c_ab_lhsT = it(np.asarray(lt, dtype=np.float16), "c_ab_lhsT")
        pp = np.zeros((128, 8), dtype=np.float32)
        for p in range(128):
            pp[p, p % 8] = 1
        self.c_par8 = it(pp, "c_par8")
        self.c_par8T = it(np.ascontiguousarray(pp.T), "c_par8T")
        b2 = np.zeros((2, 128), dtype=np.float32)
        b2[0, 0:64] = 1.0
        b2[1, 64:128] = 1.0
        self.c_ind2 = it(b2, "c_ind2")
        sp = np.zeros((128, 4), dtype=np.float32)
        ip_ = np.zeros((128, 1), dtype=np.float32)
        for p in range(128):
            i = p // 8
            sp[p, 0] = SHRINKS[i]
            sp[p, 1] = float(np.float32(SHRINKS[i]) * np.float32(SHRINKS[i]))
            sp[p, 2] = float(np.float32(2.0) * np.float32(SHRINKS[i]))
            sp[p, 3] = float(16 - i)
            ip_[p, 0] = ISHR[i]
        self.c_shrp = it(sp, "c_shrp")
        self.c_ishrp = it(ip_, "c_ishrp")
        rk = np.zeros((128, 512), dtype=np.float32)
        sh = np.zeros((128, 512), dtype=np.float32)
        ih = np.zeros((128, 512), dtype=np.float32)
        for col in range(512):
            i = (col % 128) // 8
            rk[:, col] = float(16 - i)
            sh[:, col] = SHRINKS[i]
            ih[:, col] = ISHR[i]
        self.c_rank512 = it(rk, "c_rank512")
        jk = np.zeros((128, 512), dtype=np.float32)
        for row in range(128):
            if row % 8 >= 4:
                jk[row, :] = 1e9
        self.c_junk512 = it(jk, "c_junk512")
        self.c_shr512 = it(sh, "c_shr512")
        self.c_ishr512 = it(ih, "c_ishr512")

    def build(self):
        nc = self.nc
        es = ExitStack()
        with es:
            b = B(nc, es, self.taps)
            self.b = b
            for tname in self.taps:
                shp, dt = TAP_DEFS[tname]
                b.tap_aps[tname] = nc.dram_tensor(tname, shp, dt,
                                                  kind="ExternalOutput").ap()
            SB = lambda n, shp, dt=F32: es.enter_context(nc.sbuf_tensor(n, shp, dt))
            self.SB = SB
            self.pb = [es.enter_context(nc.psum_tensor(f"pb{i}", [128, 512], F32))
                       for i in range(8)]

            self.s_identb = SB("s_identb", [128, 128], BF16)
            self.s_identf = SB("s_identf", [128, 128], F32)
            self.s_diffT = SB("s_diffT", [128, 128], F32)
            self.s_causT = SB("s_causT", [128, 128], F32)
            self.s_iota16 = SB("s_iota16", [16, 128], F32)
            self.s_rev16 = SB("s_rev16", [16, 128], F32)
            self.s_ones_row = SB("s_ones_row", [1, 128], F32)
            self.s_ones_col = SB("s_ones_col", [128, 1], F32)
            self.s_par8 = SB("s_par8", [128, 8], F32)
            self.s_par8T = SB("s_par8T", [8, 128], F32)
            self.s_ind2 = SB("s_ind2", [2, 128], F32)
            self.s_shrp = SB("s_shrp", [128, 4], F32)
            self.s_ishrp = SB("s_ishrp", [128, 1], F32)
            self.s_g1row = SB("s_g1row", [1, C], F32)
            self.s_srow = SB("s_srow", [1, H], F32)
            self.s_scol = SB("s_scol", [H, 1], F32)
            self.s_cvec = SB("s_cvec", [1, 8], F32)
            self.s_scolb = SB("s_scolb", [128, H], F32)
            self.s_snegb = SB("s_snegb", [128, H], F32)
            self.s_lrev = SB("s_lrev", [128, H], F32)
            self.s_rowa = SB("s_rowa", [1, 2048], F32)
            self.s_rowb = SB("s_rowb", [1, 2048], F32)
            self.s_mcol = SB("s_mcol", [128, 5], F32)

            self.stage_loads()
            with ExitStack() as tes:
                self.stage_ternary(tes)
                self.stage_prologue_s()
            with ExitStack() as qes:
                self.stage_int5(qes)
            b.join()
            with ExitStack() as aes:
                self.s_lt = aes.enter_context(nc.sbuf_tensor("s_lt", [128, H * 128], F32))
                self.s_lTf = aes.enter_context(nc.sbuf_tensor("s_lTf", [128, H * 128], F32))
                self.s_S = aes.enter_context(nc.sbuf_tensor("s_S", [128, H * D], F32))
                self.s_Sb = aes.enter_context(nc.sbuf_tensor("s_Sb", [128, H * D], FP16))
                self.s_P = aes.enter_context(nc.sbuf_tensor("s_P", [128, H * D], F32))
                self.s_sc512 = aes.enter_context(nc.sbuf_tensor("s_sc512", [128, 512], F32))
                self.stage_lambda_tiles()
                with ExitStack() as wes:
                    self.stage_qkv(wes)
                b.join()
                with ExitStack() as ses:
                    self.stage_scan(ses)
                b.join()
                self.stage_states()
                with ExitStack() as a2s:
                    self.stage_corr_rms(a2s)
                b.join()
            with ExitStack() as pjs:
                self.stage_proj_ln1(pjs)
            b.join()
            with ExitStack() as ffs:
                self.stage_ffn(ffs)
            tapsrc = {"t_q": self.d_q, "t_k": self.d_k, "t_v": self.d_v,
                      "t_o": self.d_o, "t_z": self.d_z, "t_z1": self.d_z1,
                      "t_z1b": self.d_z1b, "t_wq": self.d_wq, "t_wp": self.d_wp,
                      "t_wf": self.d_wf, "t_wj": self.d_wj, "t_brow": self.d_brow,
                      "t_rrow": self.d_rrow, "t_sto": self.d_st_o,
                      "t_dbg": self.d_dbg, "t_dbg2": self.d_dbg2,
                      "t_dbg3": self.d_dbg3}
            b.join()
            for tname in self.taps:
                b.dma("sy", "dst", b.tap_aps[tname], tapsrc[tname].ap())
            b.op("sy", lambda e: e.nop(), waits=[(b.S("dst"), b.S("dst").n)])

            block = es.enter_context(nc.Block())
            b.emit(block)
        return nc

    def stage_loads(self):
        b = self.b
        pairs = [
            (self.s_identb, self.c_identb.ap()), (self.s_identf, self.c_identf.ap()),
            (self.s_diffT, self.c_diffT.ap()), (self.s_causT, self.c_causT.ap()),
            (self.s_iota16, self.c_iota16.ap()), (self.s_rev16, self.c_rev.ap()),
            (self.s_ones_row, self.c_ones_row.ap()), (self.s_ones_col, self.c_ones_col.ap()),
            (self.s_par8, self.c_par8.ap()), (self.s_par8T, self.c_par8T.ap()),
            (self.s_ind2, self.c_ind2.ap()), (self.s_shrp, self.c_shrp.ap()),
            (self.s_ishrp, self.c_ishrp.ap()),
            (self.s_g1row, self.g1), (self.s_srow, self.srow_i), (self.s_cvec, self.cvec_i),
        ]
        t = None
        for dst, src in pairs:
            t = b.dma("sy", "dx", dst[:], src)
        self.t_consts = t
        self.t_scol = b.dma("sy", "dx", self.s_scol[:, :], self.s_srow[0:1, :],
                            waits=[(b.S("dx"), t)])
        b.op("ve", lambda e: e.memset(self.s_mcol[:, 4:5], MAGIC))

    def stage_ternary(self, qes):
        nc, b = self.nc, self.b
        SBq = lambda n, shp, dt=F32: qes.enter_context(nc.sbuf_tensor(n, shp, dt))
        s_tw = [SBq(f"s_tw{i}", [128, QF], F32) for i in range(2)]
        s_pw = [SBq(f"s_pw{i}", [128, C], F32) for i in range(2)]
        s_tq = SBq("s_tq", [128, QF], BF16)
        s_sums = SBq("s_sums", [128, 4], F32)
        t_twl = [b.dma("sy", "dw", s_tw[r][:, :], self.wq_sh[r * 128:(r + 1) * 128, :])
                 for r in range(2)]
        t_pwl = [b.dma("sy", "dw", s_pw[r][:, :], self.wp_sh[r * 128:(r + 1) * 128, :])
                 for r in range(2)]
        for r in range(2):
            b.op("ve", lambda e, r=r: e.tensor_reduce(
                s_sums[:, r:r + 1], s_tw[r][:, :], AX.X, ALU.add,
                apply_absolute_value=True), waits=[(b.S("dw"), t_twl[r])])
        for r in range(2):
            b.op("ve", lambda e, r=r: e.tensor_reduce(
                s_sums[:, 2 + r:3 + r], s_pw[r][:, :], AX.X, ALU.add,
                apply_absolute_value=True), waits=[(b.S("dw"), t_pwl[r])])
        tok_ve = b.S("ve").n
        tsum = b.op("pe", lambda e: e.matmul(
            self.pb[0][0:1, 0:4], self.s_ones_col[:, :], s_sums[:, 0:4],
            start=True, stop=True), waits=[(b.S("ve"), tok_ve), (b.S("dx"), self.t_consts)],
            inc=b.S("pe"))
        b.op("ve", lambda e: e.tensor_copy(self.s_rowa[0:1, 12:16], self.pb[0][0:1, 0:4]),
             waits=[(b.S("pe"), tsum)])
        tcmb = b.op("ve", lambda e: e.tensor_tensor(
            self.s_rowa[0:1, 0:2], self.s_rowa[0:1, 12:16:2], self.s_rowa[0:1, 13:16:2],
            ALU.add))
        tst = b.op("gp", lambda e: e.dma_start(out=self.d_sums_i.ap(),
                                               in_=self.s_rowa[0:1, 0:2]),
                   waits=[(b.S("ve"), tcmb)], inc=b.S("dg"), inck=16)
        tar = b.op("gp", lambda e: e.collective_compute(
            "AllReduce", ALU.add, replica_groups=[list(range(8))],
            ins=[self.d_sums_i.ap().opt()], outs=[self.d_sums_o.ap().opt()]),
            waits=[(b.S("dg"), tst)], inc=b.S("cc"))
        tld = b.dma("sy", "dx", self.s_rowa[0:1, 4:6], self.d_sums_o.ap(),
                    waits=[(b.S("cc"), tar)])
        b.op("ve", lambda e: e.tensor_scalar(
            self.s_rowa[0:1, 6:7], self.s_rowa[0:1, 4:5], 1.0 / (QF * C), 1e-5,
            ALU.mult, ALU.max), waits=[(b.S("dx"), tld)])
        b.op("ve", lambda e: e.tensor_scalar(
            self.s_rowa[0:1, 7:8], self.s_rowa[0:1, 5:6], 1.0 / (C * C), 1e-5,
            ALU.mult, ALU.max))
        b.op("ve", lambda e: e.reciprocal(self.s_rowa[0:1, 8:9], self.s_rowa[0:1, 6:7]))
        tow = b.op("ve", lambda e: e.reciprocal(self.s_rowa[0:1, 9:10], self.s_rowa[0:1, 7:8]))
        tmb = b.op("pe", lambda e: e.matmul(
            self.pb[1][:, 0:4], self.s_ones_row[0:1, :], self.s_rowa[0:1, 6:10],
            start=True, stop=True), waits=[(b.S("ve"), tow)], inc=b.S("pe"))
        b.op("ve", lambda e: e.tensor_copy(self.s_mcol[:, 0:4], self.pb[1][:, 0:4]),
             waits=[(b.S("pe"), tmb)])

        def tern(tile, Fw, si, mi, dst, r, tok_load):
            ta = b.op("sc", lambda e: e.activation(
                tile[:, 0:Fw], tile[:, 0:Fw], ACTF.Identity,
                bias=self.s_mcol[:, 4:5], scale=self.s_mcol[:, si:si + 1]),
                waits=[(b.S("ve"), b.S("ve").n), (b.S("dw"), tok_load)])
            b.op("ve", lambda e: e.tensor_scalar(
                tile[:, 0:Fw], tile[:, 0:Fw], MAGIC, 1.0, ALU.subtract, ALU.min),
                waits=[(b.S("ac"), ta)])
            tq = b.op("ve", lambda e: e.tensor_scalar(
                s_tq[:, 0:Fw], tile[:, 0:Fw], -1.0, self.s_mcol[:, mi:mi + 1],
                ALU.max, ALU.mult), waits=[(b.S("dsh"), b.S("dsh").n)])
            return b.op("sy", lambda e: e.dma_start(
                out=dst.ap()[r * 128:(r + 1) * 128, 0:Fw], in_=s_tq[:, 0:Fw]),
                waits=[(b.S("ve"), tq)], inc=b.S("dsh"), inck=16)

        for r in range(2):
            tq1 = tern(s_tw[r], QF, 2, 0, self.d_wq_s, r, t_twl[r])
        self.t_ag_q = b.op("gp", lambda e: e.collective_compute(
            "AllGather", ALU.bypass, replica_groups=[list(range(8))],
            ins=[self.d_wq_s.ap().opt()], outs=[self.d_wq.ap().opt()]),
            waits=[(b.S("dsh"), tq1)], inc=b.S("cc"))
        for r in range(2):
            tp1 = tern(s_pw[r], C, 3, 1, self.d_wp_s, r, t_pwl[r])
        self.t_ag_p = b.op("gp", lambda e: e.collective_compute(
            "AllGather", ALU.bypass, replica_groups=[list(range(8))],
            ins=[self.d_wp_s.ap().opt()], outs=[self.d_wp.ap().opt()]),
            waits=[(b.S("dsh"), tp1)], inc=b.S("cc"))

    def stage_prologue_s(self):
        b, pb = self.b, self.pb
        t1 = b.op("pe", lambda e: e.matmul(
            pb[2][:, 0:16], self.s_ones_row[0:1, :], self.s_srow[0:1, :],
            start=True, stop=True), waits=[(b.S("dx"), self.t_consts)], inc=b.S("pe"))
        b.op("ve", lambda e: e.tensor_copy(self.s_scolb[:, :], pb[2][:, 0:16]),
             waits=[(b.S("pe"), t1)])
        tneg = b.op("ve", lambda e: e.tensor_scalar(
            self.s_snegb[:, :], self.s_scolb[:, :], -1.0, None, ALU.mult))
        t2 = b.op("pe", lambda e: e.matmul(
            pb[2][:, 16:32], self.s_rev16[0:1, :], self.s_srow[0:1, :],
            start=True, stop=True), waits=[(b.S("ve"), tneg)], inc=b.S("pe"))
        tl1 = b.op("ve", lambda e: e.tensor_scalar(
            self.s_lrev[:, :], pb[2][:, 16:32], -1.0, None, ALU.mult),
            waits=[(b.S("pe"), t2)])
        b.op("sc", lambda e: e.activation(self.s_lrev[:, :], self.s_lrev[:, :], ACTF.Exp),
             waits=[(b.S("ve"), tl1)])

    def stage_int5(self, qes):
        nc, b, pb = self.nc, self.b, self.pb
        SBq = lambda n, shp, dt=F32: qes.enter_context(nc.sbuf_tensor(n, shp, dt))
        s_nat = SBq("s_nat", [128, 1024], F32)
        s_natb = SBq("s_natb", [128, 1024], F32)
        s_natj = [s_nat, s_natb]
        s_amf = SBq("s_amf", [128, 256], F32)
        s_amp = SBq("s_amp", [128, 256], F32)
        s_trx = SBq("s_trx", [128, 128], F32)
        s_ab = SBq("s_ab", [128, 32 * 128], FP16)
        t_ab = b.dma("sy", "dx", s_ab[:, :], self.c_ab_lhsT.ap())

        tam = None

        def amax_stream(nch, src, dst, nOc, buf_of, first):
            trds = {}
            tam_l = None
            for ch in range(nch):
                buf = buf_of(ch)
                sem = "dw" if ch % 2 == 0 else "dw1"
                if ch - 2 in trds:
                    w = [(b.S("ve"), trds[ch - 2])]
                else:
                    w = [(b.S("ve"), b.S("ve").n), (b.S("dsh"), b.S("dsh").n),
                         (b.S("ac"), b.S("ac").n)]
                tl = b.dma("sy", sem, buf, src[ch * 128:(ch + 1) * 128, :], waits=w)
                trds[ch] = tam_l = b.op("ve", lambda e, ch=ch, buf=buf: e.tensor_reduce(
                    dst[:, ch:256:nOc], buf.rearrange("p (k b) -> p k b", b=64),
                    AX.X, ALU.max, apply_absolute_value=True),
                    waits=[(b.S(sem), tl)])
            return tam_l

        amax_stream(64, self.wf_nat, s_amf, 64,
                    lambda ch: s_nat[:, (ch % 2) * 256:(ch % 2) * 256 + 256], True)
        tam = amax_stream(16, self.wj_nat, s_amp, 16,
                          lambda ch: s_natj[ch % 2][:, 0:1024], False)
        s_rbf = SBq("s_rbf", [128, 256], F32)
        s_rbp = SBq("s_rbp", [128, 256], F32)
        R15 = float(np.float32(1.0 / 15.0))
        trb = None
        for s_am, s_rb in ((s_amf, s_rbf), (s_amp, s_rbp)):
            b.op("ve", lambda e, s_am=s_am: e.tensor_scalar(
                s_am[:, :], s_am[:, :], 1e-8, R15, ALU.max, ALU.mult),
                waits=[(b.S("ve"), tam)])
            trb = b.op("ve", lambda e, s_am=s_am, s_rb=s_rb: e.reciprocal(
                s_rb[:, :], s_am[:, :]))

        def rows_out(slab, nchunks, nOc, row0, dst):
            tlast = None
            for k in range(nchunks):
                tt = b.op("pe", lambda e, k=k, slab=slab: e.transpose(
                    pb[0][:, 0:128], slab[:, k * 128:(k + 1) * 128], self.s_identf[:, :]),
                    waits=[(b.S("ve"), b.S("ve").n), (b.S("dst"), b.S("dst").n)],
                    inc=b.S("pe"))
                te = b.op("ve", lambda e: e.tensor_copy(s_trx[:, :], pb[0][:, 0:128]),
                          waits=[(b.S("pe"), tt)])
                nblk_local = 128 // nOc
                for loc in range(nblk_local):
                    blk = k * nblk_local + loc
                    tlast = b.op("sy", lambda e, loc=loc, blk=blk, dst=dst, nOc=nOc:
                        e.dma_start(
                            out=dst.ap()[row0 + blk:row0 + blk + 1, 0:nOc * 128],
                            in_=s_trx[loc * nOc:(loc + 1) * nOc, 0:128]),
                        waits=[(b.S("ve"), te)], inc=b.S("dst"), inck=16)
            return tlast
        rows_out(s_amf, 2, 64, 0, self.d_brow)
        rows_out(s_rbf, 2, 64, 0, self.d_rrow)
        rows_out(s_amp, 2, 16, 4, self.d_brow)
        t_rows = rows_out(s_rbp, 2, 16, 4, self.d_rrow)

        s_u = [SBq(f"s_u{i}", [128, 512], F32) for i in range(2)]
        s_v = [SBq(f"s_v{i}", [128, 512], F32) for i in range(2)]
        s_wtp = [[SBq(f"s_wt{p}{i}", [128, 512], F32) for i in range(2)]
                 for p in range(2)]
        s_rp = [SBq(f"s_rq{p}", [128, 512], F32) for p in range(2)]
        s_rvp = [SBq(f"s_rvq{p}", [128, 512], F32) for p in range(2)]
        s_d2p = [SBq(f"s_d2q{p}", [128, 512], F32) for p in range(2)]
        s_hip = [SBq(f"s_hiq{p}", [128, 512], FP16) for p in range(2)]
        s_lop = [SBq(f"s_loq{p}", [128, 512], FP16) for p in range(2)]
        s_r, s_rv, s_r2 = s_rp[0], s_rvp[0], s_d2p[0]
        s_brsp = [[SBq(f"s_brs{p}{i}", [2, 512], F32) for i in range(2)]
                  for p in range(2)]
        s_rrsp = [[SBq(f"s_rrs{p}{i}", [2, 512], F32) for i in range(2)]
                  for p in range(2)]
        s_sls = [SBq(f"s_sls{i}", [2, 512], F32) for i in range(2)]
        s_ils = [SBq(f"s_ils{i}", [2, 512], F32) for i in range(2)]
        s_sls2 = [SBq(f"s_sls2{i}", [2, 512], F32) for i in range(2)]
        s_ils2 = [SBq(f"s_ils2{i}", [2, 512], F32) for i in range(2)]
        s_EN = SBq("s_EN", [128, 32], F32)
        s_E = SBq("s_E", [128, 512], F32)
        s_ET = SBq("s_ET", [128, 512], F32)
        s_mn = SBq("s_mn", [128, 32], F32)
        s_W = SBq("s_W", [128, 512], F32)
        s_t1 = SBq("s_t1", [128, 512], F32)
        s_sm32 = SBq("s_sm32", [128, 32], F32)
        s_im32 = SBq("s_im32", [128, 32], F32)
        s_tr32 = SBq("s_tr32", [32, 128], F32)
        s_rank = SBq("s_rank", [128, 512], F32)
        s_junk = SBq("s_junk", [128, 512], F32)
        s_shr5 = SBq("s_shr5", [128, 512], F32)
        s_ishr5 = SBq("s_ishr5", [128, 512], F32)
        s_dq = SBq("s_dq", [128, 512], BF16)
        s_dqh = SBq("s_dqh", [128, 512], FP16)
        for dst, cst in ((s_rank, self.c_rank512), (s_shr5, self.c_shr512),
                         (s_ishr5, self.c_ishr512), (s_junk, self.c_junk512)):
            t_ab = b.dma("sy", "dx", dst[:, :], cst.ap())

        def pack(rcs, Fw, src, dst, row0, half=False):
            npk = len(rcs)
            tlast = 0

            def issue_loads(fc, war_tok):
                par = fc % 2
                dwn, dxn = ("dw", "dx") if par == 0 else ("dw1", "dx1")
                sl_f = slice(fc * 512, (fc + 1) * 512)
                tw = [b.dma("sy", dwn, s_wtp[par][sl][:, :],
                            src[rcs[sl] * 128:(rcs[sl] + 1) * 128, sl_f],
                            waits=[(b.S("ve"), war_tok)]) for sl in range(npk)]
                trr = None
                for sl in range(npk):
                    b.dma("sy", dxn, s_brsp[par][sl][:, :],
                          self.d_brow.ap()[row0 + 2 * sl:row0 + 2 * sl + 2, sl_f],
                          waits=[(b.S("dst"), t_rows), (b.S("ve"), war_tok),
                                 (b.S("pe"), b.S("pe").n)])
                    trr = b.dma("sy", dxn, s_rrsp[par][sl][:, :],
                                self.d_rrow.ap()[row0 + 2 * sl:row0 + 2 * sl + 2, sl_f])
                return tw, trr

            def chunk(fc, s_wt, s_brs, s_rrs, tw, trr, prevE):
                par = fc % 2
                dwn, dxn = ("dw", "dx") if par == 0 else ("dw1", "dx1")
                sl_f = slice(fc * 512, (fc + 1) * 512)
                tu = []
                for sl in range(npk):
                    tb = b.op("pe", lambda e, sl=sl: e.matmul(
                        pb[2][:, 0:512], self.s_ind2[0:2, :], s_rrs[sl][:, :],
                        start=True, stop=True),
                        waits=[(b.S(dxn), trr), (b.S("ve"), b.S("ve").n)], inc=b.S("pe"))
                    b.op("ve", lambda e, sl=sl: e.tensor_tensor(
                        s_u[sl][:, :], s_wt[sl][:, :], pb[2][:, 0:512], ALU.mult),
                        waits=[(b.S("pe"), tb), (b.S(dwn), tw[sl])])
                    tu.append(b.op("ve", lambda e, sl=sl: e.scalar_tensor_tensor(
                        s_v[sl][:, :], s_u[sl][:, :], -1.0, s_u[sl][:, :],
                        ALU.mult, ALU.max)))
                tA = None
                tca = {0: 0, 1: 0}
                tlo = {0: 0, 1: 0}
                for i in range(16):
                    for sl in range(npk):
                        k = i * npk + sl
                        par = k % 2
                        sr, srv = s_rp[par], s_rvp[par]
                        sd2, shi, slo = s_d2p[par], s_hip[par], s_lop[par]
                        b.op("ve", lambda e, i=i, sl=sl, sr=sr: e.tensor_scalar(
                            sr[:, :], s_v[sl][:, :], ISHR[i], 15.49, ALU.mult, ALU.min),
                            waits=[(b.S("ve"), tu[-1])])
                        b.op("ve", lambda e, sr=sr: e.tensor_scalar(
                            sr[:, :], sr[:, :], MAGIC, MAGIC, ALU.add, ALU.subtract))
                        trv = b.op("ve", lambda e, i=i, sl=sl, sr=sr, srv=srv:
                            e.scalar_tensor_tensor(
                                srv[:, :], sr[:, :], SHRINKS[i], s_v[sl][:, :],
                                ALU.mult, ALU.subtract),
                            waits=[(b.S("ac"), tca[par])])
                        tq1 = b.op("sc", lambda e, srv=srv, sd2=sd2: e.activation(
                            sd2[:, :], srv[:, :], ACTF.Square),
                            waits=[(b.S("gpc"), tlo[par])])
                        tq2 = b.op("sc", lambda e, sd2=sd2, shi=shi: e.activation(
                            shi[:, :], sd2[:, :], ACTF.Copy),
                            waits=[(b.S("pe"), b.S("pe").n)])
                        tca[par] = tq2
                        tl = b.op("gp", lambda e, sd2=sd2, shi=shi, slo=slo:
                            e.tensor_tensor(slo[:, :], sd2[:, :], shi[:, :],
                                            ALU.subtract),
                            waits=[(b.S("pe"), b.S("pe").n)])
                        tlo[par] = tl
                        ls = s_ab[:, (i * 2 + sl) * 128:(i * 2 + sl + 1) * 128]
                        first = (i == 0 and sl == 0)
                        last = (i == 15 and sl == npk - 1)
                        b.op("pe", lambda e, ls=ls, shi=shi, first=first: e.matmul(
                            pb[0][:, 0:512], ls, shi[:, :], start=first, stop=False),
                            waits=[(b.S("ac"), tq2), (b.S("dx"), t_ab),
                                   (b.S("ve"), prevE if first else 0)], inc=b.S("pe"))
                        tA = b.op("pe", lambda e, ls=ls, slo=slo, last=last: e.matmul(
                            pb[0][:, 0:512], ls, slo[:, :], start=False, stop=last),
                            waits=[(b.S("gpc"), tl)], inc=b.S("pe"))
                prevE = b.op("ve", lambda e: e.tensor_tensor(
                    s_E[:, :], pb[0][:, 0:512], s_junk[:, :], ALU.add),
                    waits=[(b.S("pe"), tA)])
                for k in range(4):
                    tt = b.op("pe", lambda e, k=k: e.transpose(
                        pb[5][:, k * 128:(k + 1) * 128], s_E[:, k * 128:(k + 1) * 128],
                        self.s_identf[:, :]),
                        waits=[(b.S("ve"), prevE)], inc=b.S("pe"))
                tev5 = b.op("ve", lambda e: e.tensor_copy(s_ET[:, :], pb[5][:, 0:512]),
                            waits=[(b.S("pe"), tt)])
                for k in range(4):
                    b.op("ve", lambda e, k=k: e.tensor_reduce(
                        s_mn[:, k * 8:(k + 1) * 8],
                        s_ET[:, k * 128:(k + 1) * 128].rearrange(
                            "p (i s) -> p s i", i=16), AX.X, ALU.min))
                for k in range(4):
                    b.op("ve", lambda e, k=k: e.tensor_tensor(
                        s_W[:, k * 128:(k + 1) * 128].rearrange("p (i s) -> p s i", i=16),
                        s_ET[:, k * 128:(k + 1) * 128].rearrange("p (i s) -> p s i", i=16),
                        s_mn[:, k * 8:(k + 1) * 8].unsqueeze(2).broadcast_to([128, 8, 16]),
                        ALU.is_equal))
                b.op("ve", lambda e: e.tensor_tensor(
                    s_W[:, :], s_W[:, :], s_rank[:, :], ALU.mult))
                for k in range(4):
                    b.op("ve", lambda e, k=k: e.tensor_reduce(
                        s_mn[:, k * 8:(k + 1) * 8],
                        s_W[:, k * 128:(k + 1) * 128].rearrange(
                            "p (i s) -> p s i", i=16), AX.X, ALU.max))
                for k in range(4):
                    b.op("ve", lambda e, k=k: e.tensor_tensor(
                        s_W[:, k * 128:(k + 1) * 128].rearrange("p (i s) -> p s i", i=16),
                        s_W[:, k * 128:(k + 1) * 128].rearrange("p (i s) -> p s i", i=16),
                        s_mn[:, k * 8:(k + 1) * 8].unsqueeze(2).broadcast_to([128, 8, 16]),
                        ALU.is_equal))
                b.op("ve", lambda e: e.tensor_tensor(
                    s_t1[:, :], s_W[:, :], s_shr5[:, :], ALU.mult))
                for k in range(4):
                    b.op("ve", lambda e, k=k: e.tensor_reduce(
                        s_sm32[:, k:32:4], s_t1[:, k * 128:(k + 1) * 128].rearrange(
                            "p (i s) -> p s i", i=16), AX.X, ALU.add))
                if not half:
                    b.op("ve", lambda e: e.tensor_tensor(
                        s_t1[:, :], s_W[:, :], s_ishr5[:, :], ALU.mult))
                    for k in range(4):
                        b.op("ve", lambda e, k=k: e.tensor_reduce(
                            s_im32[:, k:32:4], s_t1[:, k * 128:(k + 1) * 128].rearrange(
                                "p (i s) -> p s i", i=16), AX.X, ALU.add))
                tsel = b.S("ve").n
                tts = b.op("pe", lambda e: e.transpose(
                    pb[6][0:32, 0:128], s_sm32[:, 0:32], self.s_identf[:, :]),
                    waits=[(b.S("ve"), tsel)], inc=b.S("pe"))
                tevs = b.op("ve", lambda e: e.tensor_copy(s_tr32[0:32, :],
                                                          pb[6][0:32, 0:128]),
                            waits=[(b.S("pe"), tts)])
                tsd = None
                for sl in range(npk):
                    for j in range(2):
                        tsd = b.dma("sy", dxn, s_sls[sl][j:j + 1, :],
                                    s_tr32[(2 * sl + j) * 4:(2 * sl + j) * 4 + 4, :],
                                    waits=[(b.S("ve"), tevs)])
                if not half:
                    tti = b.op("pe", lambda e: e.transpose(
                        pb[6][0:32, 0:128], s_im32[:, 0:32], self.s_identf[:, :]),
                        waits=[(b.S("ve"), tevs), (b.S(dxn), tsd)], inc=b.S("pe"))
                    tevi = b.op("ve", lambda e: e.tensor_copy(s_tr32[0:32, :],
                                                              pb[6][0:32, 0:128]),
                                waits=[(b.S("pe"), tti)])
                    for sl in range(npk):
                        for j in range(2):
                            tsd = b.dma("sy", dxn, s_ils[sl][j:j + 1, :],
                                        s_tr32[(2 * sl + j) * 4:(2 * sl + j) * 4 + 4, :],
                                        waits=[(b.S("ve"), tevi)])
                if half:
                    TTo = lambda o, a, c, op: (lambda e: e.tensor_tensor(
                        o[:, :], a[:, :], c[:, :], op))
                    b.op("ve", lambda e: e.scalar_tensor_tensor(
                        s_E[:, :], s_W[:, :], 1e9, s_ET[:, :], ALU.mult, ALU.add),
                        waits=[(b.S(dxn), tsd)])
                    for k in range(4):
                        b.op("ve", lambda e, k=k: e.tensor_reduce(
                            s_mn[:, k * 8:(k + 1) * 8],
                            s_E[:, k * 128:(k + 1) * 128].rearrange(
                                "p (i s) -> p s i", i=16), AX.X, ALU.min))
                    for k in range(4):
                        b.op("ve", lambda e, k=k: e.tensor_tensor(
                            s_W[:, k * 128:(k + 1) * 128].rearrange("p (i s) -> p s i", i=16),
                            s_E[:, k * 128:(k + 1) * 128].rearrange("p (i s) -> p s i", i=16),
                            s_mn[:, k * 8:(k + 1) * 8].unsqueeze(2).broadcast_to([128, 8, 16]),
                            ALU.is_equal))
                    b.op("ve", lambda e: e.tensor_tensor(
                        s_W[:, :], s_W[:, :], s_rank[:, :], ALU.mult))
                    for k in range(4):
                        b.op("ve", lambda e, k=k: e.tensor_reduce(
                            s_mn[:, k * 8:(k + 1) * 8],
                            s_W[:, k * 128:(k + 1) * 128].rearrange(
                                "p (i s) -> p s i", i=16), AX.X, ALU.max))
                    for k in range(4):
                        b.op("ve", lambda e, k=k: e.tensor_tensor(
                            s_W[:, k * 128:(k + 1) * 128].rearrange("p (i s) -> p s i", i=16),
                            s_W[:, k * 128:(k + 1) * 128].rearrange("p (i s) -> p s i", i=16),
                            s_mn[:, k * 8:(k + 1) * 8].unsqueeze(2).broadcast_to([128, 8, 16]),
                            ALU.is_equal))
                    b.op("ve", lambda e: e.tensor_tensor(
                        s_t1[:, :], s_W[:, :], s_shr5[:, :], ALU.mult))
                    for k in range(4):
                        b.op("ve", lambda e, k=k: e.tensor_reduce(
                            s_sm32[:, k:32:4], s_t1[:, k * 128:(k + 1) * 128].rearrange(
                                "p (i s) -> p s i", i=16), AX.X, ALU.add))
                    t2sel = b.S("ve").n
                    t2ts = b.op("pe", lambda e: e.transpose(
                        pb[6][0:32, 0:128], s_sm32[:, 0:32], self.s_identf[:, :]),
                        waits=[(b.S("ve"), t2sel), (b.S(dxn), tsd)], inc=b.S("pe"))
                    t2ev = b.op("ve", lambda e: e.tensor_copy(
                        s_tr32[0:32, :], pb[6][0:32, 0:128]), waits=[(b.S("pe"), t2ts)])
                    for sl in range(npk):
                        for j in range(2):
                            tsd = b.dma("sy", dxn, s_sls2[sl][j:j + 1, :],
                                        s_tr32[(2 * sl + j) * 4:(2 * sl + j) * 4 + 4, :],
                                        waits=[(b.S("ve"), t2ev)])
                    for sl in range(npk):
                        b.op("ve", TTo(s_rrs[sl], s_sls[sl], s_sls2[sl], ALU.min),
                             waits=[(b.S(dxn), tsd), (b.S("pe"), b.S("pe").n)])
                        b.op("ve", TTo(s_sls2[sl], s_sls[sl], s_sls2[sl], ALU.max))
                        b.op("ve", lambda e, sl=sl: e.tensor_copy(
                            s_sls[sl][:, :], s_rrs[sl][:, :]))
                    for c, sel_r in ((0, s_sls), (1, s_sls2)):
                        for sl in range(npk):
                            b.op("ve", TTo(s_rrs[sl], s_brs[sl], sel_r[sl], ALU.mult))
                            tbc1 = b.op("pe", lambda e, sl=sl: e.matmul(
                                pb[3][:, 0:512], self.s_ind2[0:2, :], s_rrs[sl][:, :],
                                start=True, stop=True),
                                waits=[(b.S("ve"), b.S("ve").n)], inc=b.S("pe"))
                            b.op("ve", lambda e: e.tensor_copy(
                                s_E[:, :], pb[3][:, 0:512]), waits=[(b.S("pe"), tbc1)])
                            b.op("ve", lambda e: e.reciprocal(s_ET[:, :], s_E[:, :]))
                            b.op("ve", lambda e, sl=sl: e.tensor_tensor(
                                s_r[:, :], s_wt[sl][:, :], s_ET[:, :], ALU.mult))
                            b.op("ve", lambda e: e.tensor_scalar(
                                s_r[:, :], s_r[:, :], MAGIC, MAGIC, ALU.add, ALU.subtract))
                            b.op("ve", lambda e: e.tensor_scalar(
                                s_r[:, :], s_r[:, :], 15.0, -15.0, ALU.min, ALU.max))
                            b.op("ve", lambda e: e.tensor_tensor(
                                s_r2[:, :], s_r[:, :], s_E[:, :], ALU.mult))
                            b.op("ve", lambda e, sl=sl: e.tensor_tensor(
                                s_r2[:, :], s_r2[:, :], s_wt[sl][:, :], ALU.subtract))
                            td2 = b.op("ve", lambda e: e.tensor_tensor(
                                s_r2[:, :], s_r2[:, :], s_r2[:, :], ALU.mult))
                            ttrs = []
                            for q in range(4):
                                ttrs.append(b.op("pe", lambda e, q=q: e.transpose(
                                    pb[5][:, q * 128:(q + 1) * 128],
                                    s_r2[:, q * 128:(q + 1) * 128], self.s_identf[:, :]),
                                    waits=[(b.S("ve"), td2)], inc=b.S("pe")))
                            for q in range(4):
                                c0 = c * 16 + 2 * sl * 4 + q
                                b.op("ve", lambda e, q=q, c0=c0: e.tensor_reduce(
                                    s_EN[:, c0:c0 + 5:4],
                                    pb[5][:, q * 128:(q + 1) * 128].rearrange(
                                        "p (j f) -> p j f", j=2), AX.X, ALU.add),
                                    waits=[(b.S("pe"), ttrs[3])])
                    b.op("ve", lambda e: e.tensor_tensor(
                        s_sm32[:, 0:16], s_EN[:, 0:16], s_EN[:, 16:32], ALU.min))
                    tww = b.op("ve", lambda e: e.tensor_tensor(
                        s_sm32[:, 0:16], s_sm32[:, 0:16], s_EN[:, 0:16], ALU.not_equal))
                    tw2 = b.op("pe", lambda e: e.transpose(
                        pb[6][0:16, 0:128], s_sm32[:, 0:16], self.s_identf[:, :]),
                        waits=[(b.S("ve"), tww), (b.S(dxn), b.S(dxn).n)],
                        inc=b.S("pe"))
                    tew = b.op("ve", lambda e: e.tensor_copy(
                        s_tr32[0:16, :], pb[6][0:16, 0:128]), waits=[(b.S("pe"), tw2)])
                    for sl in range(npk):
                        for j in range(2):
                            tsd = b.dma("sy", dxn, s_ils[sl][j:j + 1, :],
                                        s_tr32[(2 * sl + j) * 4:(2 * sl + j) * 4 + 4, :],
                                        waits=[(b.S("ve"), tew)])
                    for sl in range(npk):
                        b.op("ve", TTo(s_rrs[sl], s_sls2[sl], s_sls[sl], ALU.subtract),
                             waits=[(b.S(dxn), tsd)])
                        b.op("ve", TTo(s_rrs[sl], s_rrs[sl], s_ils[sl], ALU.mult))
                        b.op("ve", TTo(s_sls[sl], s_sls[sl], s_rrs[sl], ALU.add))
                for sl in range(npk):
                    tm2 = b.op("ve", lambda e, sl=sl: e.tensor_tensor(
                        s_rrs[sl][:, :], s_brs[sl][:, :], s_sls[sl][:, :], ALU.mult),
                        waits=[(b.S(dxn), tsd)])
                    tsb = b.op("pe", lambda e, sl=sl: e.matmul(
                        pb[3][:, 0:512], self.s_ind2[0:2, :], s_rrs[sl][:, :],
                        start=True, stop=True),
                        waits=[(b.S("ve"), b.S("ve").n)], inc=b.S("pe"))
                    dq_t = s_dqh if half else s_dq
                    if not half:
                        tib = b.op("pe", lambda e, sl=sl: e.matmul(
                            pb[2][:, 0:512], self.s_ind2[0:2, :], s_ils[sl][:, :],
                            start=True, stop=True),
                            waits=[(b.S("ve"), b.S("ve").n)], inc=b.S("pe"))
                        b.op("ve", lambda e, sl=sl: e.tensor_tensor(
                            s_t1[:, :], s_u[sl][:, :], pb[2][:, 0:512], ALU.mult),
                            waits=[(b.S("pe"), tib)])
                        b.op("ve", lambda e: e.tensor_scalar(
                            s_t1[:, :], s_t1[:, :], MAGIC, MAGIC, ALU.add, ALU.subtract))
                        b.op("ve", lambda e: e.tensor_scalar(
                            s_t1[:, :], s_t1[:, :], 15.0, -15.0, ALU.min, ALU.max))
                        tdq = b.op("ve", lambda e, dq_t=dq_t: e.tensor_tensor(
                            dq_t[:, :], s_t1[:, :], pb[3][:, 0:512], ALU.mult),
                            waits=[(b.S("pe"), tsb), (b.S("dsh"), b.S("dsh").n)])
                    else:
                        TT = lambda o, a, c, op: (lambda e: e.tensor_tensor(
                            o[:, :], a[:, :], c[:, :], op))
                        TS = lambda o, a, s1, s2, op1, op2=None: (
                            (lambda e: e.tensor_scalar(o[:, :], a[:, :], s1, s2,
                                                       op1, op2)) if op2 is not None
                            else (lambda e: e.tensor_scalar(o[:, :], a[:, :], s1,
                                                            s2, op1)))
                        b.op("ve", lambda e: e.tensor_copy(s_E[:, :], pb[3][:, 0:512]),
                             waits=[(b.S("pe"), tsb)])
                        b.op("ve", lambda e: e.reciprocal(s_ET[:, :], s_E[:, :]))
                        b.op("ve", TT(s_r2, s_ET, s_E, ALU.mult))
                        b.op("ve", TS(s_t1, s_E, 4097.0, None, ALU.mult))
                        b.op("ve", TT(s_u[0], s_t1, s_E, ALU.subtract))
                        b.op("ve", TT(s_rv, s_t1, s_u[0], ALU.subtract))
                        b.op("ve", TT(s_W, s_E, s_rv, ALU.subtract))
                        b.op("ve", TS(s_t1, s_ET, 4097.0, None, ALU.mult))
                        b.op("ve", TT(s_u[0], s_t1, s_ET, ALU.subtract))
                        b.op("ve", TT(s_u[1], s_t1, s_u[0], ALU.subtract))
                        b.op("ve", TT(s_u[0], s_ET, s_u[1], ALU.subtract))
                        b.op("ve", TT(s_v[0], s_u[1], s_rv, ALU.mult))
                        b.op("ve", TT(s_v[0], s_v[0], s_r2, ALU.subtract))
                        b.op("ve", TT(s_v[1], s_u[1], s_W, ALU.mult))
                        b.op("ve", TT(s_v[0], s_v[0], s_v[1], ALU.add))
                        b.op("ve", TT(s_v[1], s_u[0], s_rv, ALU.mult))
                        b.op("ve", TT(s_v[0], s_v[0], s_v[1], ALU.add))
                        b.op("ve", TT(s_v[1], s_u[0], s_W, ALU.mult))
                        b.op("ve", TT(s_v[0], s_v[0], s_v[1], ALU.add))
                        b.op("ve", TS(s_v[1], s_r2, -1.0, 1.0, ALU.mult, ALU.add))
                        b.op("ve", TT(s_v[1], s_v[1], s_v[0], ALU.subtract))
                        b.op("ve", TT(s_v[1], s_v[1], s_ET, ALU.mult))
                        b.op("ve", TT(s_ET, s_ET, s_v[1], ALU.add))
                        b.op("ve", TT(s_r, s_wt[sl], s_ET, ALU.mult))
                        b.op("ve", TS(s_r, s_r, MAGIC, MAGIC, ALU.add, ALU.subtract))
                        b.op("ve", TS(s_r, s_r, 15.0, -15.0, ALU.min, ALU.max))
                        tdq = b.op("ve", lambda e, dq_t=dq_t: e.tensor_tensor(
                            dq_t[:, :], s_r[:, :], s_E[:, :], ALU.mult),
                            waits=[(b.S("dsh"), b.S("dsh").n)])
                    rc = rcs[sl]
                    tlast = b.op("sy", lambda e, rc=rc, sl_f=sl_f, dq_t=dq_t: e.dma_start(
                        out=dst.ap()[rc * 128:(rc + 1) * 128, sl_f], in_=dq_t[:, :]),
                        waits=[(b.S("ve"), tdq)], inc=b.S("dsh"), inck=16)
                return tlast, prevE

            nfc = Fw // 512
            war = {0: b.S("ve").n, 1: b.S("ve").n}
            loads = {0: issue_loads(0, b.S("ve").n)}
            prevE = 0
            for fc in range(nfc):
                par = fc % 2
                if fc + 1 < nfc:
                    loads[fc + 1] = issue_loads(fc + 1, war[(fc + 1) % 2])
                tw, trr = loads.pop(fc)
                tlast, prevE = chunk(fc, s_wtp[par], s_brsp[par], s_rrsp[par],
                                     tw, trr, prevE)
                war[par] = b.S("ve").n
            return tlast

        t_fc = pack([0, 1], FH, self.wf_sh, self.d_wf_s, 0)
        self.t_ag_f = b.op("gp", lambda e: e.collective_compute(
            "AllGather", ALU.bypass, replica_groups=[list(range(8))],
            ins=[self.d_wf_s.ap().opt()], outs=[self.d_wf.ap().opt()]),
            waits=[(b.S("dsh"), t_fc)], inc=b.S("cc"))
        pack([0, 1], C, self.wj_sh, self.d_wj_s, 4, half=True)
        pack([2, 3], C, self.wj_sh, self.d_wj_s, 8, half=True)
        pack([4, 5], C, self.wj_sh, self.d_wj_s, 12, half=True)
        t_pj = pack([6, 7], C, self.wj_sh, self.d_wj_s, 16, half=True)
        self.t_ag_j = b.op("gp", lambda e: e.collective_compute(
            "AllGather", ALU.bypass, replica_groups=[list(range(8))],
            ins=[self.d_wj_s.ap().opt()], outs=[self.d_wj.ap().opt()]),
            waits=[(b.S("dsh"), t_pj)], inc=b.S("cc"))

    def stage_lambda_tiles(self):
        nc, b, pb = self.nc, self.b, self.pb
        s_l16 = self.s_rev16
        tb1 = b.op("ve", lambda e: e.tensor_scalar(
            s_l16[:, :], self.s_iota16[:, :], self.s_scol[0:16, 0:1], -1.0,
            ALU.mult, ALU.mult), waits=[(b.S("dx"), self.t_scol)])
        tb2 = b.op("sc", lambda e: e.activation(s_l16[:, :], s_l16[:, :], ACTF.Exp),
                   waits=[(b.S("ve"), tb1)])
        tb3 = b.dma("sy", "dx", self.s_rowb[0:1, 0:2048], s_l16[0:16, :],
                    waits=[(b.S("ac"), tb2)])
        for i in range(4):
            tg = b.op("pe", lambda e, i=i: e.matmul(
                pb[5][:, 0:512], self.s_ones_row[0:1, :],
                self.s_rowb[0:1, i * 512:(i + 1) * 512], start=True, stop=True),
                waits=[(b.S("dx"), tb3), (b.S("ve"), b.S("ve").n)], inc=b.S("pe"))
            b.op("ve", lambda e, i=i: e.tensor_copy(
                self.s_lt[:, i * 512:(i + 1) * 512], pb[5][:, 0:512]),
                waits=[(b.S("pe"), tg)])
        tc1 = b.op("ve", lambda e: e.tensor_scalar(
            self.s_rowa[0:1, 0:16], self.s_srow[0:1, :], -128.0, None, ALU.mult))
        tc2 = b.op("sc", lambda e: e.activation(
            self.s_rowa[0:1, 0:16], self.s_rowa[0:1, 0:16], ACTF.Exp),
            waits=[(b.S("ve"), tc1)])
        tc3 = b.op("ve", lambda e: e.tensor_copy(
            self.s_rowb[0:1, 0:2048].rearrange("o (h t) -> o h t", h=16),
            self.s_rowa[0:1, 0:16].unsqueeze(2).broadcast_to([1, 16, 128])),
            waits=[(b.S("ac"), tc2)])
        for i in range(4):
            tg = b.op("pe", lambda e, i=i: e.matmul(
                pb[5][:, 0:512], self.s_ones_row[0:1, :],
                self.s_rowb[0:1, i * 512:(i + 1) * 512], start=True, stop=True),
                waits=[(b.S("ve"), b.S("ve").n)], inc=b.S("pe"))
            b.op("ve", lambda e, i=i: e.tensor_copy(
                self.s_lTf[:, i * 512:(i + 1) * 512], pb[5][:, 0:512]),
                waits=[(b.S("pe"), tg)])
        b.op("ve", lambda e: e.memset(self.s_S[:, :], 0.0))
        b.op("ve", lambda e: e.memset(self.s_Sb[:, :], 0.0))

    def stage_qkv(self, wes):
        nc, b, pb = self.nc, self.b, self.pb
        SBw = lambda n, shp, dt=F32: wes.enter_context(nc.sbuf_tensor(n, shp, dt))
        s_h = SBw("s_h", [128, KC * T], BF16)
        s_wres = SBw("s_wres", [128, KC * T], BF16)
        s_x = [SBw(f"s_x{i}", [128, 512], F32) for i in range(2)]
        s_xsq = SBw("s_xsq", [128, 512], F32)
        s_rr0 = self.s_rowb
        s_ev = [SBw(f"s_ev{i}", [128, 512], BF16) for i in range(2)]

        tstat = []
        for q4 in range(4):
            tq = None
            for kc in range(KC):
                tx = b.dma("sy", "dx", s_x[kc % 2][:, :],
                           self.xT[kc * 128:(kc + 1) * 128, q4 * 512:(q4 + 1) * 512],
                           waits=[(b.S("ac"), b.S("ac").n), (b.S("pe"), b.S("pe").n)]
                           if kc >= 2 else [])
                ts = b.op("sc", lambda e, kc=kc: e.activation(
                    s_xsq[:, :], s_x[kc % 2][:, :], ACTF.Square),
                    waits=[(b.S("dx"), tx), (b.S("pe"), b.S("pe").n)])
                tq = b.op("pe", lambda e, kc=kc, q4=q4: e.matmul(
                    pb[6][0:1, 0:512], self.s_ones_col[:, :], s_xsq[:, :],
                    start=(kc == 0), stop=(kc == KC - 1)),
                    waits=[(b.S("ac"), ts)] + ([(b.S("ve"), b.S("ve").n)]
                                               if kc == 0 else []), inc=b.S("pe"))
            tr1 = b.op("ve", lambda e, q4=q4: e.tensor_scalar(
                self.s_rowa[0:1, 0:512], pb[6][0:1, 0:512], 1.0 / C, EPS_F32,
                ALU.mult, ALU.add), waits=[(b.S("pe"), tq)])
            b.op("ve", lambda e: e.reciprocal(self.s_rowa[0:1, 512:1024],
                                              self.s_rowa[0:1, 0:512]))
            tstat.append(b.op("sc", lambda e, q4=q4: e.activation(
                s_rr0[0:1, q4 * 512:(q4 + 1) * 512], self.s_rowa[0:1, 512:1024],
                ACTF.Sqrt), waits=[(b.S("ve"), b.S("ve").n)]))

        for kc in range(KC):
            for q4 in range(4):
                tx = b.dma("sy", "dx", s_x[q4 % 2][:, :],
                           self.xT[kc * 128:(kc + 1) * 128, q4 * 512:(q4 + 1) * 512],
                           waits=[(b.S("ve"), b.S("ve").n), (b.S("ac"), b.S("ac").n)])
                tout = b.op("pe", lambda e, kc=kc, q4=q4: e.matmul(
                    pb[7][:, 0:512], self.s_g1row[0:1, kc * 128:(kc + 1) * 128],
                    s_rr0[0:1, q4 * 512:(q4 + 1) * 512], start=True, stop=True),
                    waits=[(b.S("ac"), tstat[-1]), (b.S("ve"), b.S("ve").n)],
                    inc=b.S("pe"))
                b.op("ve", lambda e, kc=kc, q4=q4: e.tensor_tensor(
                    s_h[:, kc * T + q4 * 512: kc * T + (q4 + 1) * 512],
                    s_x[q4 % 2][:, :], pb[7][:, 0:512], ALU.mult),
                    waits=[(b.S("pe"), tout), (b.S("dx"), tx)])

        tev_st = {0: 0, 1: 0}

        def proj_phase(col0, dst):
            tw = None
            for kc in range(KC):
                tw = b.dma("sy", "dw", s_wres[:, kc * T:(kc + 1) * T],
                           self.d_wq.ap()[kc * 128:(kc + 1) * 128, col0:col0 + C],
                           waits=[(b.S("cc"), self.t_ag_q), (b.S("pe"), b.S("pe").n)])
            for m in range(16):
                pset = [pb[(m % 2) * 4 + i] for i in range(4)]
                tmm = None
                for kc in range(KC):
                    for fch in range(4):
                        tmm = b.op("pe", lambda e, m=m, kc=kc, fch=fch, pset=pset:
                            e.matmul(
                                pset[fch][:, 0:512],
                                s_wres[:, kc * T + m * 128: kc * T + (m + 1) * 128],
                                s_h[:, kc * T + fch * 512: kc * T + (fch + 1) * 512],
                                start=(kc == 0), stop=(kc == KC - 1)),
                            waits=[(b.S("dw"), tw), (b.S("ve"), b.S("ve").n)]
                            if kc == 0 and fch == 0 else [], inc=b.S("pe"))
                for fch in range(4):
                    tev = b.op("ve", lambda e, fch=fch, pset=pset: e.tensor_copy(
                        s_ev[fch % 2][:, :], pset[fch][:, 0:512]),
                        waits=[(b.S("pe"), tmm), (b.S("dzq"), tev_st[fch % 2])])
                    tev_st[fch % 2] = b.op("sy", lambda e, m=m, fch=fch: e.dma_start(
                        out=dst.ap()[:, m * T + fch * 512: m * T + (fch + 1) * 512],
                        in_=s_ev[fch % 2][:, :]),
                        waits=[(b.S("ve"), tev)], inc=b.S("dzq"), inck=16)
        proj_phase(0, self.d_q)
        proj_phase(C, self.d_k)
        proj_phase(2 * C, self.d_v)

    def stage_scan(self, ses):
        nc, b, pb = self.nc, self.b, self.pb
        SBs = lambda n, shp, dt=BF16: ses.enter_context(nc.sbuf_tensor(n, shp, dt))
        s_maskT = ses.enter_context(nc.sbuf_tensor("s_maskT", [128, H * 128], F32))
        for h in range(H):
            ta = b.op("ve", lambda e, h=h: e.tensor_scalar(
                s_maskT[:, h * 128:(h + 1) * 128], self.s_diffT[:, :],
                self.s_snegb[:, h:h + 1], 88.0, ALU.mult, ALU.min))
            tb2 = b.op("sc", lambda e, h=h: e.activation(
                s_maskT[:, h * 128:(h + 1) * 128], s_maskT[:, h * 128:(h + 1) * 128],
                ACTF.Exp), waits=[(b.S("ve"), ta)])
            b.op("ve", lambda e, h=h: e.tensor_tensor(
                s_maskT[:, h * 128:(h + 1) * 128], s_maskT[:, h * 128:(h + 1) * 128],
                self.s_causT[:, :], ALU.mult), waits=[(b.S("ac"), tb2)])

        s_qc = [SBs(f"s_qc{i}", [128, 2048]) for i in range(2)]
        s_kc = [SBs(f"s_kc{i}", [128, 2048]) for i in range(2)]
        s_vc = [SBs(f"s_vc{i}", [128, 2048]) for i in range(2)]
        s_qt = ses.enter_context(nc.sbuf_tensor("s_qt", [128, 2048], FP16))
        s_kt = ses.enter_context(nc.sbuf_tensor("s_kt", [128, 512], FP16))
        s_vt = ses.enter_context(nc.sbuf_tensor("s_vt", [128, 2048], FP16))
        s_sm = ses.enter_context(nc.sbuf_tensor("s_sm", [128, 512], FP16))
        s_oc = ses.enter_context(nc.sbuf_tensor("s_oc", [128, 2048], FP16))
        tprev_o = 0
        to_prev = 0

        for c in range(NCH):
            bufc = c % 2
            ap3 = lambda d: d.ap()[:, :].rearrange("p (h t) -> p h t", h=H)[
                :, :, c * 128:(c + 1) * 128]
            tq = b.dma("sy", "dq", s_qc[bufc][:, :].rearrange("p (h t) -> p h t", t=128),
                       ap3(self.d_q), waits=[(b.S("ve"), b.S("ve").n),
                                             (b.S("pe"), b.S("pe").n)])
            tk = b.dma("sy", "dq", s_kc[bufc][:, :].rearrange("p (h t) -> p h t", t=128),
                       ap3(self.d_k))
            tv = b.dma("sy", "dq", s_vc[bufc][:, :].rearrange("p (h t) -> p h t", t=128),
                       ap3(self.d_v))
            tqt = b.op("ve", lambda e, bufc=bufc: e.tensor_tensor(
                s_qt[:, :], s_qc[bufc][:, :], self.s_lt[:, :], ALU.mult),
                waits=[(b.S("dq"), tq), (b.S("pe"), b.S("pe").n)])
            for g4 in range(4):
                pv = pb[g4 % 2][:, 0:512].bitcast(BF16)
                for hh in range(4):
                    h = g4 * 4 + hh
                    tt = b.op("pe", lambda e, h=h, hh=hh, bufc=bufc, pv=pv: e.transpose(
                        pv[:, hh * 128:(hh + 1) * 128],
                        s_vc[bufc][:, h * 128:(h + 1) * 128], self.s_identb[:, :]),
                        waits=[(b.S("dq"), tv), (b.S("ve"), b.S("ve").n)], inc=b.S("pe"))
                tve = b.op("ve", lambda e, g4=g4, pv=pv: e.tensor_copy(
                    s_vt[:, g4 * 512:(g4 + 1) * 512], pv[:, 0:512]),
                    waits=[(b.S("pe"), tt)])
            for g4 in range(4):
                pkv = pb[2][:, 0:512].bitcast(BF16)
                for hh in range(4):
                    h = g4 * 4 + hh
                    ttk = b.op("pe", lambda e, h=h, hh=hh, bufc=bufc, pkv=pkv: e.transpose(
                        pkv[:, hh * 128:(hh + 1) * 128],
                        s_kc[bufc][:, h * 128:(h + 1) * 128], self.s_identb[:, :]),
                        waits=[(b.S("dq"), tk), (b.S("ve"), b.S("ve").n)], inc=b.S("pe"))
                for hh in range(4):
                    h = g4 * 4 + hh
                    b.op("ve", lambda e, h=h, hh=hh, pkv=pkv: e.tensor_scalar(
                        s_kt[:, hh * 128:(hh + 1) * 128], pkv[:, hh * 128:(hh + 1) * 128],
                        self.s_lrev[:, h:h + 1], None, ALU.mult),
                        waits=[(b.S("pe"), ttk)])
                tkt = b.S("ve").n
                for hh in range(4):
                    h = g4 * 4 + hh
                    tsc = b.op("pe", lambda e, h=h, hh=hh, bufc=bufc: e.matmul(
                        pb[3][:, hh * 128:(hh + 1) * 128],
                        s_kc[bufc][:, h * 128:(h + 1) * 128],
                        s_qc[bufc][:, h * 128:(h + 1) * 128],
                        start=True, stop=True),
                        waits=[(b.S("ve"), tkt)], inc=b.S("pe"))
                tms = b.op("ve", lambda e, g4=g4: e.tensor_tensor(
                    s_sm[:, :], pb[3][:, 0:512], s_maskT[:, g4 * 512:(g4 + 1) * 512],
                    ALU.mult), waits=[(b.S("pe"), tsc)])
                po = pb[4 + (g4 % 2)]
                for hh in range(4):
                    h = g4 * 4 + hh
                    tin = b.op("pe", lambda e, h=h, hh=hh, po=po: e.matmul(
                        po[:, hh * 128:(hh + 1) * 128],
                        s_vt[:, (h // 4) * 512 + (h % 4) * 128:
                             (h // 4) * 512 + (h % 4) * 128 + 128],
                        s_sm[:, hh * 128:(hh + 1) * 128], start=True, stop=False),
                        waits=[(b.S("ve"), tms)] + ([(b.S("ac"), b.S("ac").n)]
                                                    if hh == 0 else []),
                        inc=b.S("pe"))
                    tcx = b.op("pe", lambda e, h=h, hh=hh, po=po: e.matmul(
                        po[:, hh * 128:(hh + 1) * 128],
                        self.s_Sb[:, h * 128:(h + 1) * 128],
                        s_qt[:, h * 128:(h + 1) * 128], start=False, stop=True),
                        waits=[(b.S("ve"), tqt), (b.S("ve"), tprev_o)], inc=b.S("pe"))
                tev = b.op("sc", lambda e, g4=g4, po=po: e.activation(
                    s_oc[:, g4 * 512:(g4 + 1) * 512], po[:, 0:512], ACTF.Identity,
                    scale=SCALE_ATTN),
                    waits=[(b.S("pe"), tcx), (b.S("dzq"), to_prev)])
                pk = pb[6 + (g4 % 2)]
                for hh in range(4):
                    h = g4 * 4 + hh
                    tkv = b.op("pe", lambda e, h=h, hh=hh, pk=pk: e.matmul(
                        pk[:, hh * 128:(hh + 1) * 128],
                        s_kt[:, hh * 128:(hh + 1) * 128],
                        s_vt[:, (h // 4) * 512 + (h % 4) * 128:
                             (h // 4) * 512 + (h % 4) * 128 + 128],
                        start=True, stop=True),
                        waits=[(b.S("ve"), b.S("ve").n)] if hh == 0 else [],
                        inc=b.S("pe"))
                sl = slice(g4 * 512, (g4 + 1) * 512)
                b.op("ve", lambda e, sl=sl, pk=pk: e.tensor_tensor(
                    self.s_S[:, sl], self.s_S[:, sl], self.s_lTf[:, sl], ALU.mult),
                    waits=[(b.S("pe"), tkv)])
                b.op("ve", lambda e, sl=sl, pk=pk: e.tensor_tensor(
                    self.s_S[:, sl], self.s_S[:, sl], pk[:, 0:512], ALU.add))
                tprev_o = b.op("ve", lambda e, sl=sl: e.tensor_copy(
                    self.s_Sb[:, sl], self.s_S[:, sl]))
            to_prev = b.op("sy", lambda e, c=c: e.dma_start(
                out=self.d_o.ap()[:, :].rearrange("p (h t) -> p h t", h=H)[
                    :, :, c * 128:(c + 1) * 128],
                in_=s_oc[:, :].rearrange("p (h t) -> p h t", t=128)),
                waits=[(b.S("ac"), b.S("ac").n)], inc=b.S("dzq"), inck=16)

    def stage_states(self):
        nc, b, pb = self.nc, self.b, self.pb
        tso = b.op("gp", lambda e: e.dma_start(out=self.d_st_i.ap(), in_=self.s_S[:, :]),
                   waits=[(b.S("ve"), b.S("ve").n)], inc=b.S("dg"), inck=16)
        tag = b.op("gp", lambda e: e.collective_compute(
            "AllGather", ALU.bypass,
            replica_groups=[[0, 1, 2, 3], [4, 5, 6, 7]],
            ins=[self.d_st_i.ap().opt()], outs=[self.d_st_o.ap().opt()]),
            waits=[(b.S("dg"), tso)], inc=b.S("cc"))
        b.op("ve", lambda e: e.memset(self.s_S[:, :], 0.0),
             waits=[(b.S("cc"), tag), (b.S("dg"), tso)])
        for j in range(3):
            tp = b.dma("sy", "dq", self.s_P[:, :],
                       self.d_st_o.ap()[j * 128:(j + 1) * 128, :],
                       waits=[(b.S("cc"), tag), (b.S("ve"), b.S("ve").n)])
            t1 = b.op("ve", lambda e, j=j: e.tensor_scalar(
                self.s_rowa[0:1, 0:16], self.s_srow[0:1, :],
                self.s_cvec[0:1, 3 + j:4 + j], -2048.0, ALU.mult, ALU.mult))
            t2 = b.op("sc", lambda e: e.activation(
                self.s_rowa[0:1, 0:16], self.s_rowa[0:1, 0:16], ACTF.Exp),
                waits=[(b.S("ve"), t1)])
            b.op("ve", lambda e, j=j: e.tensor_scalar(
                self.s_rowa[0:1, 0:16], self.s_rowa[0:1, 0:16],
                self.s_cvec[0:1, j:j + 1], None, ALU.mult), waits=[(b.S("ac"), t2)])
            t4 = b.op("ve", lambda e: e.tensor_copy(
                self.s_rowb[0:1, 0:2048].rearrange("o (h t) -> o h t", h=16),
                self.s_rowa[0:1, 0:16].unsqueeze(2).broadcast_to([1, 16, 128])))
            for i in range(4):
                sl = slice(i * 512, (i + 1) * 512)
                tg = b.op("pe", lambda e, i=i: e.matmul(
                    pb[5][:, 0:512], self.s_ones_row[0:1, :],
                    self.s_rowb[0:1, i * 512:(i + 1) * 512], start=True, stop=True),
                    waits=[(b.S("ve"), b.S("ve").n), (b.S("dq"), tp)], inc=b.S("pe"))
                b.op("ve", lambda e, sl=sl: e.tensor_tensor(
                    self.s_sc512[:, :], self.s_P[:, sl], pb[5][:, 0:512], ALU.mult),
                    waits=[(b.S("pe"), tg)])
                b.op("ve", lambda e, sl=sl: e.tensor_tensor(
                    self.s_S[:, sl], self.s_S[:, sl], self.s_sc512[:, :], ALU.add))
        b.op("ve", lambda e: e.tensor_copy(self.s_Sb[:, :], self.s_S[:, :]))

    def stage_corr_rms(self, a2s):
        nc, b, pb = self.nc, self.b, self.pb
        SBa = lambda n, shp, dt=F32: a2s.enter_context(nc.sbuf_tensor(n, shp, dt))
        s_qc = [SBa(f"s_aqc{i}", [128, 2048], BF16) for i in range(2)]
        s_oc = [SBa(f"s_aoc{i}", [128, 2048], FP16) for i in range(2)]
        s_qt = SBa("s_aqt", [128, 2048], FP16)
        s_o2 = SBa("s_o2", [128, 2048], F32)
        s_o2sq = SBa("s_o2sq", [128, 2048], F32)
        s_zc = SBa("s_zc", [128, 2048], BF16)
        s_g2row = SBa("s_g2row", [1, C], F32)
        s_g3row = SBa("s_g3row", [1, C], F32)
        s_g23row = SBa("s_g23row", [1, C], F32)
        s_g2sqc = SBa("s_g2sqc", [128, H], F32)
        tg2 = b.dma("sy", "dx", s_g2row[0:1, :], self.g2)
        tg3 = b.dma("sy", "dx", s_g3row[0:1, :], self.g3)
        b.op("ve", lambda e: e.tensor_tensor(
            s_g23row[0:1, :], s_g2row[0:1, :], s_g3row[0:1, :], ALU.mult),
            waits=[(b.S("dx"), tg3)])
        tgc = None
        for h in range(H):
            tgc = b.dma("sy", "dx", s_g2sqc[0:128, h:h + 1],
                        s_g2row[0:1, h * 128:(h + 1) * 128],
                        waits=[(b.S("dx"), tg2)])
        b.op("ve", lambda e: e.tensor_tensor(
            s_g2sqc[:, :], s_g2sqc[:, :], s_g2sqc[:, :], ALU.mult),
            waits=[(b.S("dx"), tgc)])

        tzst_prev = 0
        for c in range(NCH):
            bufc = c % 2
            ap3 = lambda d: d.ap()[:, :].rearrange("p (h t) -> p h t", h=H)[
                :, :, c * 128:(c + 1) * 128]
            tq = b.dma("sy", "dq", s_qc[bufc][:, :].rearrange("p (h t) -> p h t", t=128),
                       ap3(self.d_q), waits=[(b.S("ve"), b.S("ve").n)])
            to = b.dma("sy", "dq", s_oc[bufc][:, :].rearrange("p (h t) -> p h t", t=128),
                       ap3(self.d_o))
            tqt = b.op("ve", lambda e, bufc=bufc: e.tensor_tensor(
                s_qt[:, :], s_qc[bufc][:, :], self.s_lt[:, :], ALU.mult),
                waits=[(b.S("dq"), tq), (b.S("pe"), b.S("pe").n)])
            tcr = None
            for h in range(H):
                tcr = b.op("pe", lambda e, h=h: e.matmul(
                    pb[h // 4][:, (h % 4) * 128:(h % 4 + 1) * 128],
                    self.s_Sb[:, h * 128:(h + 1) * 128],
                    s_qt[:, h * 128:(h + 1) * 128], start=True, stop=True),
                    waits=[(b.S("ve"), tqt)], inc=b.S("pe"))
            for i in range(4):
                b.op("ve", lambda e, i=i, bufc=bufc: e.scalar_tensor_tensor(
                    s_o2[:, i * 512:(i + 1) * 512], pb[i][:, 0:512], SCALE_ATTN,
                    s_oc[bufc][:, i * 512:(i + 1) * 512], ALU.mult, ALU.add),
                    waits=[(b.S("pe"), tcr), (b.S("dq"), to), (b.S("ac"), b.S("ac").n)])
            for i in range(4):
                sl = slice(i * 512, (i + 1) * 512)
                b.op("ve", lambda e, sl=sl: e.tensor_tensor(
                    self.s_S[:, sl], self.s_S[:, sl], self.s_lTf[:, sl], ALU.mult))
            b.op("ve", lambda e: e.tensor_copy(self.s_Sb[:, :], self.s_S[:, :]))
            tsq = b.op("sc", lambda e: e.activation(s_o2sq[:, :], s_o2[:, :], ACTF.Square),
                       waits=[(b.S("ve"), b.S("ve").n)])
            tst = tst2 = None
            for h in range(H):
                tst = b.op("pe", lambda e, h=h: e.matmul(
                    pb[4][0:1, 0:128], self.s_ones_col[:, :],
                    s_o2sq[:, h * 128:(h + 1) * 128], start=(h == 0), stop=(h == H - 1)),
                    waits=[(b.S("ac"), tsq), (b.S("ve"), b.S("ve").n)], inc=b.S("pe"))
            for h in range(H):
                tst2 = b.op("pe", lambda e, h=h: e.matmul(
                    pb[4][0:1, 128:256], s_g2sqc[:, h:h + 1],
                    s_o2sq[:, h * 128:(h + 1) * 128], start=(h == 0), stop=(h == H - 1)),
                    inc=b.S("pe"))
            b.op("ve", lambda e: e.tensor_scalar(
                self.s_rowa[0:1, 0:128], pb[4][0:1, 0:128], 1.0 / C, 1e-6,
                ALU.mult, ALU.add), waits=[(b.S("pe"), tst2)])
            b.op("ve", lambda e: e.reciprocal(self.s_rowa[0:1, 128:256],
                                              self.s_rowa[0:1, 0:128]))
            tr2 = b.op("sc", lambda e: e.activation(
                self.s_rowa[0:1, 256:384], self.s_rowa[0:1, 128:256], ACTF.Sqrt),
                waits=[(b.S("ve"), b.S("ve").n)])
            b.op("ve", lambda e: e.tensor_scalar(
                self.s_rowa[0:1, 384:512], pb[4][0:1, 128:256], 1.0 / C, None,
                ALU.mult), waits=[(b.S("ac"), tr2)])
            b.op("ve", lambda e: e.tensor_tensor(
                self.s_rowa[0:1, 512:640], self.s_rowa[0:1, 256:384],
                self.s_rowa[0:1, 256:384], ALU.mult))
            b.op("ve", lambda e: e.tensor_tensor(
                self.s_rowa[0:1, 640:768], self.s_rowa[0:1, 512:640],
                self.s_rowa[0:1, 384:512], ALU.mult))
            b.op("ve", lambda e: e.tensor_scalar(
                self.s_rowa[0:1, 640:768], self.s_rowa[0:1, 640:768], EPS_F32, None,
                ALU.add))
            b.op("ve", lambda e: e.reciprocal(self.s_rowa[0:1, 768:896],
                                              self.s_rowa[0:1, 640:768]))
            tr3 = b.op("sc", lambda e: e.activation(
                self.s_rowa[0:1, 896:1024], self.s_rowa[0:1, 768:896], ACTF.Sqrt))
            tr23 = b.op("ve", lambda e: e.tensor_tensor(
                self.s_rowa[0:1, 1024:1152], self.s_rowa[0:1, 256:384],
                self.s_rowa[0:1, 896:1024], ALU.mult), waits=[(b.S("ac"), tr3)])
            tgo = None
            for h in range(H):
                tgo = b.op("pe", lambda e, h=h: e.matmul(
                    pb[4 + h // 4][:, (h % 4) * 128:(h % 4 + 1) * 128],
                    s_g23row[0:1, h * 128:(h + 1) * 128],
                    self.s_rowa[0:1, 1024:1152], start=True, stop=True),
                    waits=[(b.S("ve"), tr23)], inc=b.S("pe"))
            for i in range(4):
                b.op("ve", lambda e, i=i: e.tensor_tensor(
                    s_zc[:, i * 512:(i + 1) * 512], s_o2[:, i * 512:(i + 1) * 512],
                    pb[4 + i][:, 0:512], ALU.mult),
                    waits=[(b.S("pe"), tgo), (b.S("dzq"), tzst_prev)])
            tzst_prev = b.op("sy", lambda e, c=c: e.dma_start(
                out=self.d_z.ap()[:, :].rearrange("p (h t) -> p h t", h=H)[
                    :, :, c * 128:(c + 1) * 128],
                in_=s_zc[:, :].rearrange("p (h t) -> p h t", t=128)),
                waits=[(b.S("ve"), b.S("ve").n)], inc=b.S("dzq"), inck=16)

    def stage_proj_ln1(self, pjs):
        nc, b, pb = self.nc, self.b, self.pb
        SBp = lambda n, shp, dt=F32: pjs.enter_context(nc.sbuf_tensor(n, shp, dt))
        s_wp = SBp("s_wp", [128, KC * C], BF16)
        s_zf = SBp("s_zf", [128, KC * 512], BF16)
        s_xt = [SBp(f"s_pxt{i}", [128, 512], F32) for i in range(2)]
        s_y = SBp("s_y", [128, KC * 512], F32)
        s_ysq = SBp("s_ysq", [128, 512], F32)
        s_z1 = [SBp(f"s_z1_{i}", [128, 512], F32) for i in range(2)]
        s_z1bt = [SBp(f"s_z1bt{i}", [128, 512], BF16) for i in range(2)]
        tw = None
        for kc in range(KC):
            tw = b.dma("sy", "dw", s_wp[:, kc * C:(kc + 1) * C],
                       self.d_wp.ap()[kc * 128:(kc + 1) * 128, :],
                       waits=[(b.S("cc"), self.t_ag_p)])
        tz1cast = {0: 0, 1: 0}
        tz1ca = {0: 0, 1: 0}
        for fch in range(4):
            tzf = None
            for kc in range(KC):
                tzf = b.dma("sy", "dq", s_zf[:, kc * 512:(kc + 1) * 512],
                            self.d_z.ap()[:, kc * T + fch * 512: kc * T + (fch + 1) * 512],
                            waits=[(b.S("pe"), b.S("pe").n)] if kc == 0 else [])
            for m in range(KC):
                pm = pb[m % 2]
                tmm = None
                for kc in range(KC):
                    tmm = b.op("pe", lambda e, m=m, kc=kc, pm=pm: e.matmul(
                        pm[:, 0:512],
                        s_wp[:, kc * C + m * 128: kc * C + (m + 1) * 128],
                        s_zf[:, kc * 512:(kc + 1) * 512],
                        start=(kc == 0), stop=(kc == KC - 1)),
                        waits=[(b.S("dw"), tw), (b.S("dq"), tzf),
                               (b.S("ve"), b.S("ve").n)] if kc == 0 else [],
                        inc=b.S("pe"))
                txt = b.dma("sy", "dx", s_xt[m % 2][:, :],
                            self.xT[m * 128:(m + 1) * 128, fch * 512:(fch + 1) * 512],
                            waits=[(b.S("ve"), b.S("ve").n)])
                b.op("ve", lambda e, m=m, pm=pm: e.tensor_tensor(
                    s_y[:, m * 512:(m + 1) * 512], pm[:, 0:512], s_xt[m % 2][:, :],
                    ALU.add), waits=[(b.S("pe"), tmm), (b.S("dx"), txt),
                                     (b.S("ac"), b.S("ac").n)])
            tsy = tsq = None
            for m in range(KC):
                tsy = b.op("pe", lambda e, m=m: e.matmul(
                    pb[2][0:1, 0:512], self.s_ones_col[:, :],
                    s_y[:, m * 512:(m + 1) * 512], start=(m == 0), stop=(m == KC - 1)),
                    waits=[(b.S("ve"), b.S("ve").n)], inc=b.S("pe"))
            for m in range(KC):
                ts = b.op("sc", lambda e, m=m: e.activation(
                    s_ysq[:, :], s_y[:, m * 512:(m + 1) * 512], ACTF.Square),
                    waits=[(b.S("pe"), tsy if m == 0 else b.S("pe").n)])
                tsq = b.op("pe", lambda e, m=m: e.matmul(
                    pb[3][0:1, 0:512], self.s_ones_col[:, :], s_ysq[:, :],
                    start=(m == 0), stop=(m == KC - 1)),
                    waits=[(b.S("ac"), ts)], inc=b.S("pe"))
            b.op("ve", lambda e: e.tensor_scalar(
                self.s_rowa[0:1, 0:512], pb[2][0:1, 0:512], 1.0 / C, None, ALU.mult),
                waits=[(b.S("pe"), tsq)])
            b.op("ve", lambda e: e.tensor_scalar(
                self.s_rowa[0:1, 512:1024], pb[3][0:1, 0:512], 1.0 / C, None, ALU.mult))
            b.op("ve", lambda e: e.tensor_tensor(
                self.s_rowb[0:1, 0:512], self.s_rowa[0:1, 0:512],
                self.s_rowa[0:1, 0:512], ALU.mult))
            b.op("ve", lambda e: e.tensor_tensor(
                self.s_rowb[0:1, 0:512], self.s_rowa[0:1, 512:1024],
                self.s_rowb[0:1, 0:512], ALU.subtract))
            b.op("ve", lambda e: e.tensor_scalar(
                self.s_rowb[0:1, 0:512], self.s_rowb[0:1, 0:512], 1e-5, None, ALU.add))
            b.op("ve", lambda e: e.reciprocal(self.s_rowb[0:1, 512:1024],
                                              self.s_rowb[0:1, 0:512]))
            trr = b.op("sc", lambda e: e.activation(
                self.s_rowb[0:1, 0:512], self.s_rowb[0:1, 512:1024], ACTF.Sqrt))
            tmt = b.op("ve", lambda e: e.tensor_tensor(
                self.s_rowa[0:1, 1024:1536], self.s_rowa[0:1, 0:512],
                self.s_rowb[0:1, 0:512], ALU.mult), waits=[(b.S("ac"), trr)])
            if fch == 0:
                b.op("sy", lambda e: e.dma_start(
                    out=self.d_dbg.ap()[0:1, 0:512], in_=self.s_rowa[0:1, 0:512]),
                    waits=[(b.S("ve"), tmt)], inc=b.S("dst"), inck=16)
                b.op("sy", lambda e: e.dma_start(
                    out=self.d_dbg.ap()[1:2, 0:512], in_=self.s_rowa[0:1, 512:1024]),
                    inc=b.S("dst"), inck=16)
                b.op("sy", lambda e: e.dma_start(
                    out=self.d_dbg.ap()[2:3, 0:512], in_=self.s_rowb[0:1, 0:512]),
                    inc=b.S("dst"), inck=16)
                b.op("sy", lambda e: e.dma_start(
                    out=self.d_dbg.ap()[3:4, 0:512], in_=s_ysq[0:1, 0:512]),
                    inc=b.S("dst"), inck=16)
            tB = b.op("pe", lambda e: e.matmul(
                pb[4][:, 0:512], self.s_ones_row[0:1, :], self.s_rowb[0:1, 0:512],
                start=True, stop=True), waits=[(b.S("ve"), tmt)], inc=b.S("pe"))
            tC = b.op("pe", lambda e: e.matmul(
                pb[5][:, 0:512], self.s_ones_row[0:1, :], self.s_rowa[0:1, 1024:1536],
                start=True, stop=True), inc=b.S("pe"))
            tz1st = {0: 0, 1: 0}
            for m in range(KC):
                pbuf = m % 2
                b.op("ve", lambda e, m=m, pbuf=pbuf: e.tensor_tensor(
                    s_z1[pbuf][:, :], s_y[:, m * 512:(m + 1) * 512], pb[4][:, 0:512],
                    ALU.mult), waits=[(b.S("pe"), tC), (b.S("ac"), tz1ca[pbuf]),
                                      (b.S("dzq"), tz1st[pbuf])])
                tz1 = b.op("ve", lambda e, pbuf=pbuf: e.tensor_tensor(
                    s_z1[pbuf][:, :], s_z1[pbuf][:, :], pb[5][:, 0:512], ALU.subtract))
                tz1st[pbuf] = b.op("sy", lambda e, m=m, fch=fch, pbuf=pbuf: e.dma_start(
                    out=self.d_z1.ap()[:, m * T + fch * 512: m * T + (fch + 1) * 512],
                    in_=s_z1[pbuf][:, :]), waits=[(b.S("ve"), tz1)],
                    inc=b.S("dzq"), inck=16)
                tz1ca[pbuf] = b.op("sc", lambda e, pbuf=pbuf: e.activation(
                    s_z1bt[pbuf][:, :], s_z1[pbuf][:, :], ACTF.Copy),
                    waits=[(b.S("ve"), tz1), (b.S("dg"), tz1cast[pbuf])])
                tz1cast[pbuf] = b.op("sy", lambda e, m=m, fch=fch, pbuf=pbuf: e.dma_start(
                    out=self.d_z1b.ap()[:, m * T + fch * 512: m * T + (fch + 1) * 512],
                    in_=s_z1bt[pbuf][:, :]), waits=[(b.S("ac"), tz1ca[pbuf])],
                    inc=b.S("dg"), inck=16)

    def stage_ffn(self, ffs):
        nc, b, pb = self.nc, self.b, self.pb
        SBf = lambda n, shp, dt=F32: ffs.enter_context(nc.sbuf_tensor(n, shp, dt))
        s_z1b = SBf("s_fz1b", [128, KC * 1024], BF16)
        s_wfp = [SBf(f"s_fwf{p}", [128, KC * 512], BF16) for p in range(2)]
        s_hsq = SBf("s_fhsq", [128, 8 * 1024], FP16)
        s_y2 = SBf("s_fy2", [128, KC * 1024], F32)
        s_wj = [SBf(f"s_fwj{i}", [128, 8 * 128], FP16) for i in range(2)]
        s_ysq = SBf("s_fysq", [128, 1024], F32)
        s_z2 = [SBf(f"s_fz2_{i}", [128, 1024], F32) for i in range(2)]
        self.s_rowc = SBf("s_rowc", [1, 2048], F32)

        war_wf = {0: 0, 1: 0}
        twf_pend = {}

        def load_wf(ghi, war_tok):
            par = ghi % 2
            sem = "dw" if par == 0 else "dw1"
            c0 = (ghi % 16) * 512
            twf = None
            for kc in range(KC):
                twf = b.dma("sy", sem, s_wfp[par][:, kc * 512:(kc + 1) * 512],
                            self.d_wf.ap()[kc * 128:(kc + 1) * 128, c0:c0 + 512],
                            waits=[(b.S("cc"), self.t_ag_f),
                                   (b.S("pe"), war_tok)] if kc == 0 else [])
            return twf

        for ft in range(2):
            tz = ty = None
            for kc in range(KC):
                tz = b.dma("sy", "dq", s_z1b[:, kc * 1024:(kc + 1) * 1024],
                           self.d_z1b.ap()[:, kc * T + ft * 1024: kc * T + (ft + 1) * 1024],
                           waits=[(b.S("gpc"), b.S("gpc").n), (b.S("dg"), b.S("dg").n),
                                  (b.S("pe"), b.S("pe").n)] if kc == 0 else [])
                ty = b.dma("sy", "dx", s_y2[:, kc * 1024:(kc + 1) * 1024],
                           self.d_z1.ap()[:, kc * T + ft * 1024: kc * T + (ft + 1) * 1024],
                           waits=[(b.S("ve"), b.S("ve").n)] if kc == 0 else [])
            for g in range(8):
                for hh in range(2):
                    ghi = ft * 16 + g * 2 + hh
                    if ghi == 0:
                        twf_pend[0] = load_wf(0, b.S("pe").n)
                    if ghi + 1 < 32:
                        twf_pend[ghi + 1] = load_wf(ghi + 1, war_wf[(ghi + 1) % 2])
                    twf = twf_pend.pop(ghi)
                    swf = s_wfp[ghi % 2]
                    dwn_f = "dw" if ghi % 2 == 0 else "dw1"
                    for hx in range(4):
                        hc = hh * 4 + hx
                        pp0, pp1 = pb[2 * (hc % 2)], pb[2 * (hc % 2) + 1]
                        tmm = None
                        for kc in range(KC):
                            b.op("pe", lambda e, kc=kc, hx=hx, pp0=pp0, swf=swf: e.matmul(
                                pp0[:, 0:512],
                                swf[:, kc * 512 + hx * 128: kc * 512 + (hx + 1) * 128],
                                s_z1b[:, kc * 1024: kc * 1024 + 512],
                                start=(kc == 0), stop=(kc == KC - 1)),
                                waits=[(b.S(dwn_f), twf), (b.S("dq"), tz),
                                       (b.S("ve"), b.S("ve").n)] if kc == 0 else [],
                                inc=b.S("pe"))
                            tmm = b.op("pe", lambda e, kc=kc, hx=hx, pp1=pp1, swf=swf: e.matmul(
                                pp1[:, 0:512],
                                swf[:, kc * 512 + hx * 128: kc * 512 + (hx + 1) * 128],
                                s_z1b[:, kc * 1024 + 512: kc * 1024 + 1024],
                                start=(kc == 0), stop=(kc == KC - 1)), inc=b.S("pe"))
                        for half, ppx in ((0, pp0), (1, pp1)):
                            b.op("ve", lambda e, ppx=ppx: e.tensor_scalar(
                                s_ysq[:, 0:512], ppx[:, 0:512], 0.0, None, ALU.max),
                                waits=[(b.S("pe"), tmm), (b.S("ac"), b.S("ac").n)])
                            b.op("sc", lambda e, hc=hc, half=half: e.activation(
                                s_hsq[:, hc * 1024 + half * 512: hc * 1024 + (half + 1) * 512],
                                s_ysq[:, 0:512], ACTF.Square), waits=[(b.S("ve"), b.S("ve").n)])
                    war_wf[ghi % 2] = b.S("pe").n
                for m in range(KC):
                    twj = b.dma("sy", "dw", s_wj[m % 2][:, :],
                                self.d_wj.ap()[g * 1024:(g + 1) * 1024,
                                               m * 128:(m + 1) * 128].rearrange(
                                    "(hc p) o -> p hc o", p=128),
                                waits=[(b.S("pe"), b.S("pe").n),
                                       (b.S("cc"), self.t_ag_j)])
                    pm0, pm1 = pb[4 + 2 * (m % 2)], pb[4 + 2 * (m % 2) + 1]
                    tpj = None
                    for hc in range(8):
                        b.op("pe", lambda e, hc=hc, m=m, pm0=pm0: e.matmul(
                            pm0[:, 0:512], s_wj[m % 2][:, hc * 128:(hc + 1) * 128],
                            s_hsq[:, hc * 1024: hc * 1024 + 512],
                            start=(hc == 0), stop=(hc == 7)),
                            waits=[(b.S("dw"), twj), (b.S("ac"), b.S("ac").n),
                                   (b.S("ve"), b.S("ve").n)]
                            if hc == 0 else [], inc=b.S("pe"))
                        tpj = b.op("pe", lambda e, hc=hc, m=m, pm1=pm1: e.matmul(
                            pm1[:, 0:512], s_wj[m % 2][:, hc * 128:(hc + 1) * 128],
                            s_hsq[:, hc * 1024 + 512: hc * 1024 + 1024],
                            start=(hc == 0), stop=(hc == 7)), inc=b.S("pe"))
                    b.op("ve", lambda e, m=m, pm0=pm0: e.tensor_tensor(
                        s_y2[:, m * 1024: m * 1024 + 512],
                        s_y2[:, m * 1024: m * 1024 + 512], pm0[:, 0:512], ALU.add),
                        waits=[(b.S("pe"), tpj), (b.S("dx"), ty)])
                    b.op("ve", lambda e, m=m, pm1=pm1: e.tensor_tensor(
                        s_y2[:, m * 1024 + 512: m * 1024 + 1024],
                        s_y2[:, m * 1024 + 512: m * 1024 + 1024], pm1[:, 0:512],
                        ALU.add))
            if ft == 1:
                b.op("sy", lambda e: e.dma_start(out=self.d_dbg2.ap(), in_=s_y2[:, :]),
                     waits=[(b.S("ve"), b.S("ve").n)], inc=b.S("dst"), inck=16)
                b.op("sy", lambda e: e.dma_start(out=self.d_dbg3.ap(), in_=s_hsq[:, :]),
                     waits=[(b.S("ac"), b.S("ac").n)], inc=b.S("dst"), inck=16)
            tsy = tsq = None
            for m in range(KC):
                tsy = b.op("pe", lambda e, m=m: e.matmul(
                    pb[0][0:1, 0:512], self.s_ones_col[:, :],
                    s_y2[:, m * 1024: m * 1024 + 512], start=(m == 0), stop=False if m < KC - 1 else True),
                    waits=[(b.S("ve"), b.S("ve").n)], inc=b.S("pe"))
                tsy = b.op("pe", lambda e, m=m: e.matmul(
                    pb[1][0:1, 0:512], self.s_ones_col[:, :],
                    s_y2[:, m * 1024 + 512: m * 1024 + 1024], start=(m == 0),
                    stop=False if m < KC - 1 else True), inc=b.S("pe"))
            for m in range(KC):
                ts = b.op("sc", lambda e, m=m: e.activation(
                    s_ysq[:, :], s_y2[:, m * 1024:(m + 1) * 1024], ACTF.Square),
                    waits=[(b.S("pe"), tsy if m == 0 else b.S("pe").n)])
                tsq = b.op("pe", lambda e, m=m: e.matmul(
                    pb[2][0:1, 0:512], self.s_ones_col[:, :], s_ysq[:, 0:512],
                    start=(m == 0), stop=(m == KC - 1)),
                    waits=[(b.S("ac"), ts)], inc=b.S("pe"))
                tsq = b.op("pe", lambda e, m=m: e.matmul(
                    pb[3][0:1, 0:512], self.s_ones_col[:, :], s_ysq[:, 512:1024],
                    start=(m == 0), stop=(m == KC - 1)), inc=b.S("pe"))
            for hf, pbm, pbq in ((0, pb[0], pb[2]), (1, pb[1], pb[3])):
                o = hf * 512
                b.op("ve", lambda e, pbm=pbm, o=o: e.tensor_scalar(
                    self.s_rowa[0:1, o:o + 512], pbm[0:1, 0:512], 1.0 / C, None,
                    ALU.mult), waits=[(b.S("pe"), tsq)])
                b.op("ve", lambda e, pbq=pbq, o=o: e.tensor_scalar(
                    self.s_rowb[0:1, o:o + 512], pbq[0:1, 0:512], 1.0 / C, None,
                    ALU.mult))
                b.op("ve", lambda e, o=o: e.tensor_tensor(
                    self.s_rowc[0:1, o:o + 512], self.s_rowa[0:1, o:o + 512],
                    self.s_rowa[0:1, o:o + 512], ALU.mult))
                b.op("ve", lambda e, o=o: e.tensor_tensor(
                    self.s_rowc[0:1, o:o + 512], self.s_rowb[0:1, o:o + 512],
                    self.s_rowc[0:1, o:o + 512], ALU.subtract))
                b.op("ve", lambda e, o=o: e.tensor_scalar(
                    self.s_rowc[0:1, o:o + 512], self.s_rowc[0:1, o:o + 512], 1e-5,
                    None, ALU.add))
                b.op("ve", lambda e, o=o: e.reciprocal(
                    self.s_rowb[0:1, o:o + 512], self.s_rowc[0:1, o:o + 512]))
                trr = b.op("sc", lambda e, o=o: e.activation(
                    self.s_rowc[0:1, o:o + 512], self.s_rowb[0:1, o:o + 512],
                    ACTF.Sqrt))
                b.op("ve", lambda e, o=o: e.tensor_tensor(
                    self.s_rowa[0:1, o:o + 512], self.s_rowa[0:1, o:o + 512],
                    self.s_rowc[0:1, o:o + 512], ALU.mult), waits=[(b.S("ac"), trr)])
            touts = []
            for hf in range(2):
                o = hf * 512
                tB = b.op("pe", lambda e, o=o, hf=hf: e.matmul(
                    pb[4][:, 0:512] if hf == 0 else pb[5][:, 0:512],
                    self.s_ones_row[0:1, :], self.s_rowc[0:1, o:o + 512],
                    start=True, stop=True), waits=[(b.S("ve"), b.S("ve").n)],
                    inc=b.S("pe"))
                tC = b.op("pe", lambda e, o=o, hf=hf: e.matmul(
                    pb[6][:, 0:512] if hf == 0 else pb[7][:, 0:512],
                    self.s_ones_row[0:1, :], self.s_rowa[0:1, o:o + 512],
                    start=True, stop=True), inc=b.S("pe"))
                touts.append(tC)
            tz2st = {0: 0, 1: 0}
            for m in range(KC):
                pbuf = m % 2
                for hf in range(2):
                    pB = pb[4] if hf == 0 else pb[5]
                    pC = pb[6] if hf == 0 else pb[7]
                    b.op("ve", lambda e, m=m, hf=hf, pbuf=pbuf, pB=pB: e.tensor_tensor(
                        s_z2[pbuf][:, hf * 512:(hf + 1) * 512],
                        s_y2[:, m * 1024 + hf * 512: m * 1024 + (hf + 1) * 512],
                        pB[:, 0:512], ALU.mult),
                        waits=[(b.S("pe"), touts[-1]), (b.S("dst"), tz2st[pbuf])])
                    b.op("ve", lambda e, hf=hf, pbuf=pbuf, pC=pC: e.tensor_tensor(
                        s_z2[pbuf][:, hf * 512:(hf + 1) * 512],
                        s_z2[pbuf][:, hf * 512:(hf + 1) * 512], pC[:, 0:512],
                        ALU.subtract))
                tzv = b.S("ve").n
                tz2st[pbuf] = b.op("sy", lambda e, m=m, ft=ft, pbuf=pbuf: e.dma_start(
                    out=self.out[m * 128:(m + 1) * 128, ft * 1024:(ft + 1) * 1024],
                    in_=s_z2[pbuf][:, :]), waits=[(b.S("ve"), tzv)],
                    inc=b.S("dst"), inck=16)


TAP_DEFS = {
    "t_q": ([128, H * T], BF16), "t_k": ([128, H * T], BF16),
    "t_v": ([128, H * T], BF16), "t_o": ([128, H * T], FP16),
    "t_z": ([128, H * T], BF16), "t_z1": ([128, KC * T], F32),
    "t_z1b": ([128, KC * T], BF16), "t_wq": ([C, QF], BF16),
    "t_wp": ([C, C], BF16), "t_wf": ([C, FH], BF16), "t_wj": ([FH, C], FP16),
    "t_brow": ([20, FH], F32), "t_rrow": ([20, FH], F32),
    "t_sto": ([512, H * D], F32),
    "t_dbg": ([8, 2048], F32),
    "t_dbg2": ([128, KC * 1024], F32),
    "t_dbg3": ([128, 8 * 1024], FP16),
}


def kernel(**inputs):
    in_maps = _host_shard(**inputs)
    nc = Builder(taps=TAPS).build()
    res = run_bass_kernel_spmd(nc, in_maps, core_ids=list(range(N_CORES)))
    full = np.empty((2, 8192, 2048), dtype=np.float32)
    for c in range(N_CORES):
        bb, seg = c // 4, c % 4
        full[bb, seg * T:(seg + 1) * T, :] = res.results[c]["out"].T
    if TAPS:
        kernel.tap_results = [{t: res.results[c][t] for t in TAPS}
                              for c in range(N_CORES)]
    return full


# version 31
# speedup vs baseline: 1.0124x; 1.0124x over previous
import numpy as np
import ml_dtypes
import concourse.bass as bass
import concourse.mybir as mybir
from concourse.bass_utils import run_bass_kernel_spmd
from contextlib import ExitStack

F32 = mybir.dt.float32
BF16 = mybir.dt.bfloat16
FP16 = mybir.dt.float16
ALU = mybir.AluOpType
ACTF = mybir.ActivationFunctionType
AX = mybir.AxisListType
BFD = ml_dtypes.bfloat16

MAGIC = float(np.float32(1.5 * 2 ** 23))
SCALE_ATTN = float(128 ** (-0.5))
EPS_F32 = float(np.finfo(np.float32).eps)
SHRINKS = [float(np.float32(s)) for s in np.linspace(0.4, 1.0, 16)]
ISHR = [float(np.float32(1.0 / np.float64(s))) for s in np.linspace(0.4, 1.0, 16)]

T = 2048
H = 16
D = 128
C = 2048
KC = 16
NCH = 16
QF = 6144
FH = 8192
N_CORES = 8

TAPS = ()


class Sem:
    __slots__ = ("h", "n")

    def __init__(self, h):
        self.h = h
        self.n = 0


class B:

    def __init__(self, nc, es, taps):
        self.nc = nc
        self.es = es
        self.taps = taps
        self.tap_aps = {}
        self.q = {e: [] for e in ("sy", "ve", "sc", "pe", "gp")}
        self.waited = {e: {} for e in ("sy", "ve", "sc", "pe", "gp")}
        self.sems = {}
        for name in ("dw", "dx", "dw1", "dx1", "dq", "dg", "cc", "pe", "ve", "ac",
                     "gpc", "dsh", "dst", "dzq"):
            self.sems[name] = Sem(es.enter_context(nc.semaphore(name)))
        self.selfsem = {"ve": self.sems["ve"], "sc": self.sems["ac"],
                        "gp": self.sems["gpc"]}

    def S(self, name):
        return self.sems[name]

    def op(self, eng, fn, waits=(), inc=None, inck=1):
        ws = []
        if eng in self.selfsem:
            s = self.selfsem[eng]
            if s.n > 0:
                ws.append((s, s.n))
            if inc is None:
                inc, inck = s, 1
        if eng == "sc" and self.sems["ve"].n > 0:
            ws.append((self.sems["ve"], self.sems["ve"].n))
        if eng == "gp":
            for nm in ("ve", "ac"):
                if self.sems[nm].n > 0:
                    ws.append((self.sems[nm], self.sems[nm].n))
        for (s, v) in waits:
            if s is None or v is None or v <= 0:
                continue
            ws.append((s, v))
        ws2 = []
        for (s, v) in ws:
            prev = self.waited[eng].get(id(s), 0)
            if v > prev:
                ws2.append((s, v))
                self.waited[eng][id(s)] = v
        tok = None
        if inc is not None:
            inc.n += inck
            tok = inc.n
        self.q[eng].append((fn, ws2, inc, inck))
        return tok

    def dma(self, eng, sem, dst, src, waits=()):
        return self.op(eng, lambda e, d=dst, s=src: e.dma_start(out=d, in_=s),
                       waits=waits, inc=self.S(sem), inck=16)

    def join(self):
        allsems = [(s, s.n) for s in self.sems.values() if s.n > 0]
        for eng in self.q:
            self.op(eng, lambda e: e.nop(), waits=allsems,
                    inc=None if eng in self.selfsem else None)

    def emit(self, block):
        def mk(key):
            def run(eng):
                for (fn, ws, inc, inck) in self.q[key]:
                    for (s, v) in ws:
                        eng.wait_ge(s.h, v)
                    ins = fn(eng)
                    if inc is not None:
                        if ins is None:
                            ins = eng.nop()
                        ins.then_inc(inc.h, inck)
            return run
        block.sync(mk("sy"))
        block.vector(mk("ve"))
        block.scalar(mk("sc"))
        block.tensor(mk("pe"))
        block.gpsimd(mk("gp"))


def _host_shard(L0_attn_x, L0_attn_qkv_w, L0_attn_qkv_norm_w, L0_attn_proj_w,
                L0_attn_proj_norm_w, L0_attn_norm_w, L0_attn_s, L0_ffn_w_fc,
                L0_ffn_w_proj):
    x = np.asarray(L0_attn_x, dtype=np.float32)
    wq = np.asarray(L0_attn_qkv_w, dtype=np.float32)
    wp = np.asarray(L0_attn_proj_w, dtype=np.float32)
    wf = np.asarray(L0_ffn_w_fc, dtype=np.float32)
    wj = np.asarray(L0_ffn_w_proj, dtype=np.float32)
    g1 = np.asarray(L0_attn_qkv_norm_w, dtype=np.float32).reshape(1, C)
    g2 = np.asarray(L0_attn_norm_w, dtype=np.float32).reshape(1, C)
    g3 = np.asarray(L0_attn_proj_norm_w, dtype=np.float32).reshape(1, C)
    srow = np.asarray(L0_attn_s, dtype=np.float32).reshape(1, H)
    maps = []
    for c in range(N_CORES):
        b, seg = c // 4, c % 4
        t0 = seg * T
        r0 = c * 256
        rj = c * 1024
        cv = np.zeros((1, 8), dtype=np.float32)
        for j in range(3):
            cv[0, j] = 1.0 if j < seg else 0.0
            cv[0, 3 + j] = float(seg - 1 - j) if j < seg else 0.0
        maps.append({
            "xT": np.ascontiguousarray(x[b, t0:t0 + T, :].T),
            "wq_sh": np.ascontiguousarray(wq[:, r0:r0 + 256].T),
            "wp_sh": np.ascontiguousarray(wp[:, r0:r0 + 256].T),
            "wf_sh": np.ascontiguousarray(wf[:, r0:r0 + 256].T),
            "wf_nat": np.ascontiguousarray(wf[:, r0:r0 + 256]),
            "wj_sh": np.ascontiguousarray(wj[:, rj:rj + 1024].T),
            "wj_nat": np.ascontiguousarray(wj[:, rj:rj + 1024]),
            "g1": g1.copy(), "g2": g2.copy(), "g3": g3.copy(),
            "srow": srow.copy(), "cvec": cv,
        })
    return maps


class Builder:
    def __init__(self, taps=()):
        self.taps = tuple(taps)
        nc = bass.Bass("TRN2", target_bir_lowering=False, debug=False)
        self.nc = nc
        self.xT = nc.dram_tensor("xT", [C, T], F32, kind="ExternalInput").ap()
        self.wq_sh = nc.dram_tensor("wq_sh", [256, QF], F32, kind="ExternalInput").ap()
        self.wp_sh = nc.dram_tensor("wp_sh", [256, C], F32, kind="ExternalInput").ap()
        self.wf_sh = nc.dram_tensor("wf_sh", [256, FH], F32, kind="ExternalInput").ap()
        self.wf_nat = nc.dram_tensor("wf_nat", [FH, 256], F32, kind="ExternalInput").ap()
        self.wj_sh = nc.dram_tensor("wj_sh", [1024, C], F32, kind="ExternalInput").ap()
        self.wj_nat = nc.dram_tensor("wj_nat", [C, 1024], F32, kind="ExternalInput").ap()
        self.g1 = nc.dram_tensor("g1", [1, C], F32, kind="ExternalInput").ap()
        self.g2 = nc.dram_tensor("g2", [1, C], F32, kind="ExternalInput").ap()
        self.g3 = nc.dram_tensor("g3", [1, C], F32, kind="ExternalInput").ap()
        self.srow_i = nc.dram_tensor("srow", [1, H], F32, kind="ExternalInput").ap()
        self.cvec_i = nc.dram_tensor("cvec", [1, 8], F32, kind="ExternalInput").ap()
        self.out = nc.dram_tensor("out", [C, T], F32, kind="ExternalOutput").ap()

        dr = lambda n, shp, dt=F32: nc.dram_tensor(n, shp, dt)
        self.d_sums_i = dr("d_sums_i", [1, 2]); self.d_sums_o = dr("d_sums_o", [1, 2])
        self.d_wq_s = dr("d_wq_s", [256, QF], BF16)
        self.d_wp_s = dr("d_wp_s", [256, C], BF16)
        self.d_wf_s = dr("d_wf_s", [256, FH], BF16)
        self.d_wj_s = dr("d_wj_s", [1024, C], FP16)
        self.d_wq = dr("d_wq", [C, QF], BF16)
        self.d_wp = dr("d_wp", [C, C], BF16)
        self.d_wf = dr("d_wf", [C, FH], BF16)
        self.d_wj = dr("d_wj", [FH, C], FP16)
        self.d_st_i = dr("d_st_i", [128, H * D])
        self.d_st_o = dr("d_st_o", [512, H * D])
        self.d_q = dr("d_q", [128, H * T], BF16)
        self.d_k = dr("d_k", [128, H * T], BF16)
        self.d_v = dr("d_v", [128, H * T], BF16)
        self.d_o = dr("d_o", [128, H * T], FP16)
        self.d_z = dr("d_z", [128, H * T], BF16)
        self.d_z1 = dr("d_z1", [128, KC * T])
        self.d_z1b = dr("d_z1b", [128, KC * T], BF16)
        self.d_brow = dr("d_brow", [20, FH])
        self.d_dbg = dr("d_dbg", [8, 2048])
        self.d_dbg2 = dr("d_dbg2", [128, KC * 1024])
        self.d_dbg3 = dr("d_dbg3", [128, 8 * 1024], FP16)
        self.d_rrow = dr("d_rrow", [20, FH])

        it = lambda a, n: nc.inline_tensor(np.ascontiguousarray(a), name=n)
        self.c_identb = it(np.eye(128, dtype=BFD), "c_identb")
        self.c_identf = it(np.eye(128, dtype=np.float32), "c_identf")
        ii, jj = np.meshgrid(np.arange(128), np.arange(128), indexing="ij")
        self.c_diffT = it((jj - ii).astype(np.float32), "c_diffT")
        self.c_causT = it((jj >= ii).astype(np.float32), "c_causT")
        self.c_iota16 = it(np.tile(np.arange(128, dtype=np.float32), (16, 1)), "c_iota16")
        self.c_rev = it(np.tile(128.0 - np.arange(128, dtype=np.float32), (16, 1)), "c_rev")
        self.c_ones_row = it(np.ones((1, 128), dtype=np.float32), "c_ones_row")
        self.c_ones_col = it(np.ones((128, 1), dtype=np.float32), "c_ones_col")
        lt = np.zeros((128, 32 * 128), dtype=np.float32)
        for i in range(16):
            for sl in range(2):
                c0 = (i * 2 + sl) * 128
                lt[0:64, c0 + 8 * i + 2 * sl] = 1
                lt[64:128, c0 + 8 * i + 2 * sl + 1] = 1
        self.c_ab_lhsT = it(np.asarray(lt, dtype=np.float16), "c_ab_lhsT")
        pp = np.zeros((128, 8), dtype=np.float32)
        for p in range(128):
            pp[p, p % 8] = 1
        self.c_par8 = it(pp, "c_par8")
        self.c_par8T = it(np.ascontiguousarray(pp.T), "c_par8T")
        b2 = np.zeros((2, 128), dtype=np.float32)
        b2[0, 0:64] = 1.0
        b2[1, 64:128] = 1.0
        self.c_ind2 = it(b2, "c_ind2")
        sp = np.zeros((128, 4), dtype=np.float32)
        ip_ = np.zeros((128, 1), dtype=np.float32)
        for p in range(128):
            i = p // 8
            sp[p, 0] = SHRINKS[i]
            sp[p, 1] = float(np.float32(SHRINKS[i]) * np.float32(SHRINKS[i]))
            sp[p, 2] = float(np.float32(2.0) * np.float32(SHRINKS[i]))
            sp[p, 3] = float(16 - i)
            ip_[p, 0] = ISHR[i]
        self.c_shrp = it(sp, "c_shrp")
        self.c_ishrp = it(ip_, "c_ishrp")
        rk = np.zeros((128, 512), dtype=np.float32)
        sh = np.zeros((128, 512), dtype=np.float32)
        ih = np.zeros((128, 512), dtype=np.float32)
        for col in range(512):
            i = (col % 128) // 8
            rk[:, col] = float(16 - i)
            sh[:, col] = SHRINKS[i]
            ih[:, col] = ISHR[i]
        self.c_rank512 = it(rk, "c_rank512")
        jk = np.zeros((128, 512), dtype=np.float32)
        for row in range(128):
            if row % 8 >= 4:
                jk[row, :] = 1e9
        self.c_junk512 = it(jk, "c_junk512")
        self.c_shr512 = it(sh, "c_shr512")
        self.c_ishr512 = it(ih, "c_ishr512")

    def build(self):
        nc = self.nc
        es = ExitStack()
        with es:
            b = B(nc, es, self.taps)
            self.b = b
            for tname in self.taps:
                shp, dt = TAP_DEFS[tname]
                b.tap_aps[tname] = nc.dram_tensor(tname, shp, dt,
                                                  kind="ExternalOutput").ap()
            SB = lambda n, shp, dt=F32: es.enter_context(nc.sbuf_tensor(n, shp, dt))
            self.SB = SB
            self.pb = [es.enter_context(nc.psum_tensor(f"pb{i}", [128, 512], F32))
                       for i in range(8)]

            self.s_identb = SB("s_identb", [128, 128], BF16)
            self.s_identf = SB("s_identf", [128, 128], F32)
            self.s_diffT = SB("s_diffT", [128, 128], F32)
            self.s_causT = SB("s_causT", [128, 128], F32)
            self.s_iota16 = SB("s_iota16", [16, 128], F32)
            self.s_rev16 = SB("s_rev16", [16, 128], F32)
            self.s_ones_row = SB("s_ones_row", [1, 128], F32)
            self.s_ones_col = SB("s_ones_col", [128, 1], F32)
            self.s_par8 = SB("s_par8", [128, 8], F32)
            self.s_par8T = SB("s_par8T", [8, 128], F32)
            self.s_ind2 = SB("s_ind2", [2, 128], F32)
            self.s_shrp = SB("s_shrp", [128, 4], F32)
            self.s_ishrp = SB("s_ishrp", [128, 1], F32)
            self.s_g1row = SB("s_g1row", [1, C], F32)
            self.s_srow = SB("s_srow", [1, H], F32)
            self.s_scol = SB("s_scol", [H, 1], F32)
            self.s_cvec = SB("s_cvec", [1, 8], F32)
            self.s_scolb = SB("s_scolb", [128, H], F32)
            self.s_snegb = SB("s_snegb", [128, H], F32)
            self.s_lrev = SB("s_lrev", [128, H], F32)
            self.s_rowa = SB("s_rowa", [1, 2048], F32)
            self.s_rowb = SB("s_rowb", [1, 2048], F32)
            self.s_mcol = SB("s_mcol", [128, 5], F32)

            self.stage_loads()
            with ExitStack() as tes:
                self.stage_ternary(tes)
                self.stage_prologue_s()
            with ExitStack() as qes:
                self.stage_int5(qes)
            b.join()
            with ExitStack() as aes:
                self.s_lt = aes.enter_context(nc.sbuf_tensor("s_lt", [128, H * 128], F32))
                self.s_lTf = aes.enter_context(nc.sbuf_tensor("s_lTf", [128, H * 128], F32))
                self.s_S = aes.enter_context(nc.sbuf_tensor("s_S", [128, H * D], F32))
                self.s_Sb = aes.enter_context(nc.sbuf_tensor("s_Sb", [128, H * D], FP16))
                self.s_P = aes.enter_context(nc.sbuf_tensor("s_P", [128, H * D], F32))
                self.s_sc512 = aes.enter_context(nc.sbuf_tensor("s_sc512", [128, 512], F32))
                self.stage_lambda_tiles()
                with ExitStack() as wes:
                    self.stage_qkv(wes)
                b.join()
                with ExitStack() as ses:
                    self.stage_scan(ses)
                b.join()
                self.stage_states()
                with ExitStack() as a2s:
                    self.stage_corr_rms(a2s)
                b.join()
            with ExitStack() as pjs:
                self.stage_proj_ln1(pjs)
            b.join()
            with ExitStack() as ffs:
                self.stage_ffn(ffs)
            tapsrc = {"t_q": self.d_q, "t_k": self.d_k, "t_v": self.d_v,
                      "t_o": self.d_o, "t_z": self.d_z, "t_z1": self.d_z1,
                      "t_z1b": self.d_z1b, "t_wq": self.d_wq, "t_wp": self.d_wp,
                      "t_wf": self.d_wf, "t_wj": self.d_wj, "t_brow": self.d_brow,
                      "t_rrow": self.d_rrow, "t_sto": self.d_st_o,
                      "t_dbg": self.d_dbg, "t_dbg2": self.d_dbg2,
                      "t_dbg3": self.d_dbg3}
            b.join()
            for tname in self.taps:
                b.dma("sy", "dst", b.tap_aps[tname], tapsrc[tname].ap())
            b.op("sy", lambda e: e.nop(), waits=[(b.S("dst"), b.S("dst").n)])

            block = es.enter_context(nc.Block())
            b.emit(block)
        return nc

    def stage_loads(self):
        b = self.b
        pairs = [
            (self.s_identb, self.c_identb.ap()), (self.s_identf, self.c_identf.ap()),
            (self.s_diffT, self.c_diffT.ap()), (self.s_causT, self.c_causT.ap()),
            (self.s_iota16, self.c_iota16.ap()), (self.s_rev16, self.c_rev.ap()),
            (self.s_ones_row, self.c_ones_row.ap()), (self.s_ones_col, self.c_ones_col.ap()),
            (self.s_par8, self.c_par8.ap()), (self.s_par8T, self.c_par8T.ap()),
            (self.s_ind2, self.c_ind2.ap()), (self.s_shrp, self.c_shrp.ap()),
            (self.s_ishrp, self.c_ishrp.ap()),
            (self.s_g1row, self.g1), (self.s_srow, self.srow_i), (self.s_cvec, self.cvec_i),
        ]
        t = None
        for dst, src in pairs:
            t = b.dma("sy", "dx", dst[:], src)
        self.t_consts = t
        self.t_scol = b.dma("sy", "dx", self.s_scol[:, :], self.s_srow[0:1, :],
                            waits=[(b.S("dx"), t)])
        b.op("ve", lambda e: e.memset(self.s_mcol[:, 4:5], MAGIC))

    def stage_ternary(self, qes):
        nc, b = self.nc, self.b
        SBq = lambda n, shp, dt=F32: qes.enter_context(nc.sbuf_tensor(n, shp, dt))
        s_tw = [SBq(f"s_tw{i}", [128, QF], F32) for i in range(2)]
        s_pw = [SBq(f"s_pw{i}", [128, C], F32) for i in range(2)]
        s_tq = SBq("s_tq", [128, QF], BF16)
        s_sums = SBq("s_sums", [128, 4], F32)
        t_twl = [b.dma("sy", "dw", s_tw[r][:, :], self.wq_sh[r * 128:(r + 1) * 128, :])
                 for r in range(2)]
        t_pwl = [b.dma("sy", "dw", s_pw[r][:, :], self.wp_sh[r * 128:(r + 1) * 128, :])
                 for r in range(2)]
        for r in range(2):
            b.op("ve", lambda e, r=r: e.tensor_reduce(
                s_sums[:, r:r + 1], s_tw[r][:, :], AX.X, ALU.add,
                apply_absolute_value=True), waits=[(b.S("dw"), t_twl[r])])
        for r in range(2):
            b.op("ve", lambda e, r=r: e.tensor_reduce(
                s_sums[:, 2 + r:3 + r], s_pw[r][:, :], AX.X, ALU.add,
                apply_absolute_value=True), waits=[(b.S("dw"), t_pwl[r])])
        tok_ve = b.S("ve").n
        tsum = b.op("pe", lambda e: e.matmul(
            self.pb[0][0:1, 0:4], self.s_ones_col[:, :], s_sums[:, 0:4],
            start=True, stop=True), waits=[(b.S("ve"), tok_ve), (b.S("dx"), self.t_consts)],
            inc=b.S("pe"))
        b.op("ve", lambda e: e.tensor_copy(self.s_rowa[0:1, 12:16], self.pb[0][0:1, 0:4]),
             waits=[(b.S("pe"), tsum)])
        tcmb = b.op("ve", lambda e: e.tensor_tensor(
            self.s_rowa[0:1, 0:2], self.s_rowa[0:1, 12:16:2], self.s_rowa[0:1, 13:16:2],
            ALU.add))
        tst = b.op("gp", lambda e: e.dma_start(out=self.d_sums_i.ap(),
                                               in_=self.s_rowa[0:1, 0:2]),
                   waits=[(b.S("ve"), tcmb)], inc=b.S("dg"), inck=16)
        tar = b.op("gp", lambda e: e.collective_compute(
            "AllReduce", ALU.add, replica_groups=[list(range(8))],
            ins=[self.d_sums_i.ap().opt()], outs=[self.d_sums_o.ap().opt()]),
            waits=[(b.S("dg"), tst)], inc=b.S("cc"))
        tld = b.dma("sy", "dx", self.s_rowa[0:1, 4:6], self.d_sums_o.ap(),
                    waits=[(b.S("cc"), tar)])
        b.op("ve", lambda e: e.tensor_scalar(
            self.s_rowa[0:1, 6:7], self.s_rowa[0:1, 4:5], 1.0 / (QF * C), 1e-5,
            ALU.mult, ALU.max), waits=[(b.S("dx"), tld)])
        b.op("ve", lambda e: e.tensor_scalar(
            self.s_rowa[0:1, 7:8], self.s_rowa[0:1, 5:6], 1.0 / (C * C), 1e-5,
            ALU.mult, ALU.max))
        b.op("ve", lambda e: e.reciprocal(self.s_rowa[0:1, 8:9], self.s_rowa[0:1, 6:7]))
        tow = b.op("ve", lambda e: e.reciprocal(self.s_rowa[0:1, 9:10], self.s_rowa[0:1, 7:8]))
        tmb = b.op("pe", lambda e: e.matmul(
            self.pb[1][:, 0:4], self.s_ones_row[0:1, :], self.s_rowa[0:1, 6:10],
            start=True, stop=True), waits=[(b.S("ve"), tow)], inc=b.S("pe"))
        b.op("ve", lambda e: e.tensor_copy(self.s_mcol[:, 0:4], self.pb[1][:, 0:4]),
             waits=[(b.S("pe"), tmb)])

        def tern(tile, Fw, si, mi, dst, r, tok_load):
            ta = b.op("sc", lambda e: e.activation(
                tile[:, 0:Fw], tile[:, 0:Fw], ACTF.Identity,
                bias=self.s_mcol[:, 4:5], scale=self.s_mcol[:, si:si + 1]),
                waits=[(b.S("ve"), b.S("ve").n), (b.S("dw"), tok_load)])
            b.op("ve", lambda e: e.tensor_scalar(
                tile[:, 0:Fw], tile[:, 0:Fw], MAGIC, 1.0, ALU.subtract, ALU.min),
                waits=[(b.S("ac"), ta)])
            tq = b.op("ve", lambda e: e.tensor_scalar(
                s_tq[:, 0:Fw], tile[:, 0:Fw], -1.0, self.s_mcol[:, mi:mi + 1],
                ALU.max, ALU.mult), waits=[(b.S("dsh"), b.S("dsh").n)])
            return b.op("sy", lambda e: e.dma_start(
                out=dst.ap()[r * 128:(r + 1) * 128, 0:Fw], in_=s_tq[:, 0:Fw]),
                waits=[(b.S("ve"), tq)], inc=b.S("dsh"), inck=16)

        for r in range(2):
            tq1 = tern(s_tw[r], QF, 2, 0, self.d_wq_s, r, t_twl[r])
        self.t_ag_q = b.op("gp", lambda e: e.collective_compute(
            "AllGather", ALU.bypass, replica_groups=[list(range(8))],
            ins=[self.d_wq_s.ap().opt()], outs=[self.d_wq.ap().opt()]),
            waits=[(b.S("dsh"), tq1)], inc=b.S("cc"))
        for r in range(2):
            tp1 = tern(s_pw[r], C, 3, 1, self.d_wp_s, r, t_pwl[r])
        self.t_ag_p = b.op("gp", lambda e: e.collective_compute(
            "AllGather", ALU.bypass, replica_groups=[list(range(8))],
            ins=[self.d_wp_s.ap().opt()], outs=[self.d_wp.ap().opt()]),
            waits=[(b.S("dsh"), tp1)], inc=b.S("cc"))

    def stage_prologue_s(self):
        b, pb = self.b, self.pb
        t1 = b.op("pe", lambda e: e.matmul(
            pb[2][:, 0:16], self.s_ones_row[0:1, :], self.s_srow[0:1, :],
            start=True, stop=True), waits=[(b.S("dx"), self.t_consts)], inc=b.S("pe"))
        b.op("ve", lambda e: e.tensor_copy(self.s_scolb[:, :], pb[2][:, 0:16]),
             waits=[(b.S("pe"), t1)])
        tneg = b.op("ve", lambda e: e.tensor_scalar(
            self.s_snegb[:, :], self.s_scolb[:, :], -1.0, None, ALU.mult))
        t2 = b.op("pe", lambda e: e.matmul(
            pb[2][:, 16:32], self.s_rev16[0:1, :], self.s_srow[0:1, :],
            start=True, stop=True), waits=[(b.S("ve"), tneg)], inc=b.S("pe"))
        tl1 = b.op("ve", lambda e: e.tensor_scalar(
            self.s_lrev[:, :], pb[2][:, 16:32], -1.0, None, ALU.mult),
            waits=[(b.S("pe"), t2)])
        b.op("sc", lambda e: e.activation(self.s_lrev[:, :], self.s_lrev[:, :], ACTF.Exp),
             waits=[(b.S("ve"), tl1)])

    def stage_int5(self, qes):
        nc, b, pb = self.nc, self.b, self.pb
        SBq = lambda n, shp, dt=F32: qes.enter_context(nc.sbuf_tensor(n, shp, dt))
        s_nat = SBq("s_nat", [128, 1024], F32)
        s_natb = SBq("s_natb", [128, 1024], F32)
        s_natj = [s_nat, s_natb]
        s_amf = SBq("s_amf", [128, 256], F32)
        s_amp = SBq("s_amp", [128, 256], F32)
        s_trx = SBq("s_trx", [128, 128], F32)
        s_ab = SBq("s_ab", [128, 32 * 128], FP16)
        t_ab = b.dma("sy", "dx", s_ab[:, :], self.c_ab_lhsT.ap())

        tam = None

        def amax_stream(nch, src, dst, nOc, buf_of, first):
            trds = {}
            tam_l = None
            for ch in range(nch):
                buf = buf_of(ch)
                sem = "dw" if ch % 2 == 0 else "dw1"
                if ch - 2 in trds:
                    w = [(b.S("ve"), trds[ch - 2])]
                else:
                    w = [(b.S("ve"), b.S("ve").n), (b.S("dsh"), b.S("dsh").n),
                         (b.S("ac"), b.S("ac").n)]
                tl = b.dma("sy", sem, buf, src[ch * 128:(ch + 1) * 128, :], waits=w)
                trds[ch] = tam_l = b.op("ve", lambda e, ch=ch, buf=buf: e.tensor_reduce(
                    dst[:, ch:256:nOc], buf.rearrange("p (k b) -> p k b", b=64),
                    AX.X, ALU.max, apply_absolute_value=True),
                    waits=[(b.S(sem), tl)])
            return tam_l

        amax_stream(64, self.wf_nat, s_amf, 64,
                    lambda ch: s_nat[:, (ch % 2) * 256:(ch % 2) * 256 + 256], True)
        tam = amax_stream(16, self.wj_nat, s_amp, 16,
                          lambda ch: s_natj[ch % 2][:, 0:1024], False)
        s_rbf = SBq("s_rbf", [128, 256], F32)
        s_rbp = SBq("s_rbp", [128, 256], F32)
        R15 = float(np.float32(1.0 / 15.0))
        trb = None
        for s_am, s_rb in ((s_amf, s_rbf), (s_amp, s_rbp)):
            b.op("ve", lambda e, s_am=s_am: e.tensor_scalar(
                s_am[:, :], s_am[:, :], 1e-8, R15, ALU.max, ALU.mult),
                waits=[(b.S("ve"), tam)])
            trb = b.op("ve", lambda e, s_am=s_am, s_rb=s_rb: e.reciprocal(
                s_rb[:, :], s_am[:, :]))

        def rows_out(slab, nchunks, nOc, row0, dst):
            tlast = None
            for k in range(nchunks):
                tt = b.op("pe", lambda e, k=k, slab=slab: e.transpose(
                    pb[0][:, 0:128], slab[:, k * 128:(k + 1) * 128], self.s_identf[:, :]),
                    waits=[(b.S("ve"), b.S("ve").n), (b.S("dst"), b.S("dst").n)],
                    inc=b.S("pe"))
                te = b.op("ve", lambda e: e.tensor_copy(s_trx[:, :], pb[0][:, 0:128]),
                          waits=[(b.S("pe"), tt)])
                nblk_local = 128 // nOc
                for loc in range(nblk_local):
                    blk = k * nblk_local + loc
                    tlast = b.op("sy", lambda e, loc=loc, blk=blk, dst=dst, nOc=nOc:
                        e.dma_start(
                            out=dst.ap()[row0 + blk:row0 + blk + 1, 0:nOc * 128],
                            in_=s_trx[loc * nOc:(loc + 1) * nOc, 0:128]),
                        waits=[(b.S("ve"), te)], inc=b.S("dst"), inck=16)
            return tlast
        rows_out(s_amf, 2, 64, 0, self.d_brow)
        rows_out(s_rbf, 2, 64, 0, self.d_rrow)
        rows_out(s_amp, 2, 16, 4, self.d_brow)
        t_rows = rows_out(s_rbp, 2, 16, 4, self.d_rrow)

        s_u = [SBq(f"s_u{i}", [128, 512], F32) for i in range(2)]
        s_v = [SBq(f"s_v{i}", [128, 512], F32) for i in range(2)]
        s_wtp = [[SBq(f"s_wt{p}{i}", [128, 512], F32) for i in range(2)]
                 for p in range(2)]
        s_rp = [SBq(f"s_rq{p}", [128, 512], F32) for p in range(2)]
        s_rvp = [SBq(f"s_rvq{p}", [128, 512], F32) for p in range(2)]
        s_d2p = [SBq(f"s_d2q{p}", [128, 512], F32) for p in range(2)]
        s_hip = [SBq(f"s_hiq{p}", [128, 512], FP16) for p in range(2)]
        s_lop = [SBq(f"s_loq{p}", [128, 512], FP16) for p in range(2)]
        s_r, s_rv, s_r2 = s_rp[0], s_rvp[0], s_d2p[0]
        s_brsp = [[SBq(f"s_brs{p}{i}", [2, 512], F32) for i in range(2)]
                  for p in range(2)]
        s_rrsp = [[SBq(f"s_rrs{p}{i}", [2, 512], F32) for i in range(2)]
                  for p in range(2)]
        s_sls = [SBq(f"s_sls{i}", [2, 512], F32) for i in range(2)]
        s_ils = [SBq(f"s_ils{i}", [2, 512], F32) for i in range(2)]
        s_sls2 = [SBq(f"s_sls2{i}", [2, 512], F32) for i in range(2)]
        s_ils2 = [SBq(f"s_ils2{i}", [2, 512], F32) for i in range(2)]
        s_EN = SBq("s_EN", [128, 32], F32)
        s_E = SBq("s_E", [128, 512], F32)
        s_ET = SBq("s_ET", [128, 512], F32)
        s_mn = SBq("s_mn", [128, 32], F32)
        s_W = SBq("s_W", [128, 512], F32)
        s_t1 = SBq("s_t1", [128, 512], F32)
        s_sm32 = SBq("s_sm32", [128, 32], F32)
        s_im32 = SBq("s_im32", [128, 32], F32)
        s_tr32 = SBq("s_tr32", [32, 128], F32)
        s_rank = SBq("s_rank", [128, 512], F32)
        s_junk = SBq("s_junk", [128, 512], F32)
        s_shr5 = SBq("s_shr5", [128, 512], F32)
        s_ishr5 = SBq("s_ishr5", [128, 512], F32)
        s_dq = SBq("s_dq", [128, 512], BF16)
        s_dqh = SBq("s_dqh", [128, 512], FP16)
        for dst, cst in ((s_rank, self.c_rank512), (s_shr5, self.c_shr512),
                         (s_ishr5, self.c_ishr512), (s_junk, self.c_junk512)):
            t_ab = b.dma("sy", "dx", dst[:, :], cst.ap())

        def pack(rcs, Fw, src, dst, row0, half=False):
            npk = len(rcs)
            tlast = 0

            def issue_loads(fc, war_tok):
                par = fc % 2
                dwn, dxn = ("dw", "dx") if par == 0 else ("dw1", "dx1")
                sl_f = slice(fc * 512, (fc + 1) * 512)
                tw = [b.dma("sy", dwn, s_wtp[par][sl][:, :],
                            src[rcs[sl] * 128:(rcs[sl] + 1) * 128, sl_f],
                            waits=[(b.S("ve"), war_tok)]) for sl in range(npk)]
                trr = None
                for sl in range(npk):
                    b.dma("sy", dxn, s_brsp[par][sl][:, :],
                          self.d_brow.ap()[row0 + 2 * sl:row0 + 2 * sl + 2, sl_f],
                          waits=[(b.S("dst"), t_rows), (b.S("ve"), war_tok),
                                 (b.S("pe"), b.S("pe").n)])
                    trr = b.dma("sy", dxn, s_rrsp[par][sl][:, :],
                                self.d_rrow.ap()[row0 + 2 * sl:row0 + 2 * sl + 2, sl_f])
                return tw, trr

            def chunk(fc, s_wt, s_brs, s_rrs, tw, trr, prevE):
                par = fc % 2
                dwn, dxn = ("dw", "dx") if par == 0 else ("dw1", "dx1")
                sl_f = slice(fc * 512, (fc + 1) * 512)
                tu = []
                for sl in range(npk):
                    tb = b.op("pe", lambda e, sl=sl: e.matmul(
                        pb[2][:, 0:512], self.s_ind2[0:2, :], s_rrs[sl][:, :],
                        start=True, stop=True),
                        waits=[(b.S(dxn), trr), (b.S("ve"), b.S("ve").n)], inc=b.S("pe"))
                    b.op("ve", lambda e, sl=sl: e.tensor_tensor(
                        s_u[sl][:, :], s_wt[sl][:, :], pb[2][:, 0:512], ALU.mult),
                        waits=[(b.S("pe"), tb), (b.S(dwn), tw[sl])])
                    tu.append(b.op("ve", lambda e, sl=sl: e.scalar_tensor_tensor(
                        s_v[sl][:, :], s_u[sl][:, :], -1.0, s_u[sl][:, :],
                        ALU.mult, ALU.max)))
                tA = None
                tca = {0: 0, 1: 0}
                tlo = {0: 0, 1: 0}
                for i in range(16):
                    for sl in range(npk):
                        k = i * npk + sl
                        par = k % 2
                        sr, srv = s_rp[par], s_rvp[par]
                        sd2, shi, slo = s_d2p[par], s_hip[par], s_lop[par]
                        b.op("ve", lambda e, i=i, sl=sl, sr=sr: e.tensor_scalar(
                            sr[:, :], s_v[sl][:, :], ISHR[i], 15.49, ALU.mult, ALU.min),
                            waits=[(b.S("ve"), tu[-1])])
                        b.op("ve", lambda e, sr=sr: e.tensor_scalar(
                            sr[:, :], sr[:, :], MAGIC, MAGIC, ALU.add, ALU.subtract))
                        trv = b.op("ve", lambda e, i=i, sl=sl, sr=sr, srv=srv:
                            e.scalar_tensor_tensor(
                                srv[:, :], sr[:, :], SHRINKS[i], s_v[sl][:, :],
                                ALU.mult, ALU.subtract),
                            waits=[(b.S("ac"), tca[par])])
                        tq1 = b.op("sc", lambda e, srv=srv, sd2=sd2: e.activation(
                            sd2[:, :], srv[:, :], ACTF.Square),
                            waits=[(b.S("gpc"), tlo[par])])
                        tq2 = b.op("sc", lambda e, sd2=sd2, shi=shi: e.activation(
                            shi[:, :], sd2[:, :], ACTF.Copy),
                            waits=[(b.S("pe"), b.S("pe").n)])
                        tca[par] = tq2
                        tl = b.op("gp", lambda e, sd2=sd2, shi=shi, slo=slo:
                            e.tensor_tensor(slo[:, :], sd2[:, :], shi[:, :],
                                            ALU.subtract),
                            waits=[(b.S("pe"), b.S("pe").n)])
                        tlo[par] = tl
                        ls = s_ab[:, (i * 2 + sl) * 128:(i * 2 + sl + 1) * 128]
                        first = (i == 0 and sl == 0)
                        last = (i == 15 and sl == npk - 1)
                        b.op("pe", lambda e, ls=ls, shi=shi, first=first: e.matmul(
                            pb[0][:, 0:512], ls, shi[:, :], start=first, stop=False),
                            waits=[(b.S("ac"), tq2), (b.S("dx"), t_ab),
                                   (b.S("ve"), prevE if first else 0)], inc=b.S("pe"))
                        tA = b.op("pe", lambda e, ls=ls, slo=slo, last=last: e.matmul(
                            pb[0][:, 0:512], ls, slo[:, :], start=False, stop=last),
                            waits=[(b.S("gpc"), tl)], inc=b.S("pe"))
                prevE = b.op("ve", lambda e: e.tensor_tensor(
                    s_E[:, :], pb[0][:, 0:512], s_junk[:, :], ALU.add),
                    waits=[(b.S("pe"), tA)])
                for k in range(4):
                    tt = b.op("pe", lambda e, k=k: e.transpose(
                        pb[5][:, k * 128:(k + 1) * 128], s_E[:, k * 128:(k + 1) * 128],
                        self.s_identf[:, :]),
                        waits=[(b.S("ve"), prevE)], inc=b.S("pe"))
                tev5 = b.op("ve", lambda e: e.tensor_copy(s_ET[:, :], pb[5][:, 0:512]),
                            waits=[(b.S("pe"), tt)])
                for k in range(4):
                    b.op("ve", lambda e, k=k: e.tensor_reduce(
                        s_mn[:, k * 8:(k + 1) * 8],
                        s_ET[:, k * 128:(k + 1) * 128].rearrange(
                            "p (i s) -> p s i", i=16), AX.X, ALU.min))
                for k in range(4):
                    b.op("ve", lambda e, k=k: e.tensor_tensor(
                        s_W[:, k * 128:(k + 1) * 128].rearrange("p (i s) -> p s i", i=16),
                        s_ET[:, k * 128:(k + 1) * 128].rearrange("p (i s) -> p s i", i=16),
                        s_mn[:, k * 8:(k + 1) * 8].unsqueeze(2).broadcast_to([128, 8, 16]),
                        ALU.is_equal))
                b.op("ve", lambda e: e.tensor_tensor(
                    s_W[:, :], s_W[:, :], s_rank[:, :], ALU.mult))
                for k in range(4):
                    b.op("ve", lambda e, k=k: e.tensor_reduce(
                        s_mn[:, k * 8:(k + 1) * 8],
                        s_W[:, k * 128:(k + 1) * 128].rearrange(
                            "p (i s) -> p s i", i=16), AX.X, ALU.max))
                for k in range(4):
                    b.op("ve", lambda e, k=k: e.tensor_tensor(
                        s_W[:, k * 128:(k + 1) * 128].rearrange("p (i s) -> p s i", i=16),
                        s_W[:, k * 128:(k + 1) * 128].rearrange("p (i s) -> p s i", i=16),
                        s_mn[:, k * 8:(k + 1) * 8].unsqueeze(2).broadcast_to([128, 8, 16]),
                        ALU.is_equal))
                b.op("ve", lambda e: e.tensor_tensor(
                    s_t1[:, :], s_W[:, :], s_shr5[:, :], ALU.mult))
                for k in range(4):
                    b.op("ve", lambda e, k=k: e.tensor_reduce(
                        s_sm32[:, k:32:4], s_t1[:, k * 128:(k + 1) * 128].rearrange(
                            "p (i s) -> p s i", i=16), AX.X, ALU.add))
                if not half:
                    b.op("ve", lambda e: e.tensor_tensor(
                        s_t1[:, :], s_W[:, :], s_ishr5[:, :], ALU.mult))
                    for k in range(4):
                        b.op("ve", lambda e, k=k: e.tensor_reduce(
                            s_im32[:, k:32:4], s_t1[:, k * 128:(k + 1) * 128].rearrange(
                                "p (i s) -> p s i", i=16), AX.X, ALU.add))
                tsel = b.S("ve").n
                tts = b.op("pe", lambda e: e.transpose(
                    pb[6][0:32, 0:128], s_sm32[:, 0:32], self.s_identf[:, :]),
                    waits=[(b.S("ve"), tsel)], inc=b.S("pe"))
                tevs = b.op("ve", lambda e: e.tensor_copy(s_tr32[0:32, :],
                                                          pb[6][0:32, 0:128]),
                            waits=[(b.S("pe"), tts)])
                tsd = None
                for sl in range(npk):
                    for j in range(2):
                        tsd = b.dma("sy", dxn, s_sls[sl][j:j + 1, :],
                                    s_tr32[(2 * sl + j) * 4:(2 * sl + j) * 4 + 4, :],
                                    waits=[(b.S("ve"), tevs)])
                if not half:
                    tti = b.op("pe", lambda e: e.transpose(
                        pb[6][0:32, 0:128], s_im32[:, 0:32], self.s_identf[:, :]),
                        waits=[(b.S("ve"), tevs), (b.S(dxn), tsd)], inc=b.S("pe"))
                    tevi = b.op("ve", lambda e: e.tensor_copy(s_tr32[0:32, :],
                                                              pb[6][0:32, 0:128]),
                                waits=[(b.S("pe"), tti)])
                    for sl in range(npk):
                        for j in range(2):
                            tsd = b.dma("sy", dxn, s_ils[sl][j:j + 1, :],
                                        s_tr32[(2 * sl + j) * 4:(2 * sl + j) * 4 + 4, :],
                                        waits=[(b.S("ve"), tevi)])
                if half:
                    TTo = lambda o, a, c, op: (lambda e: e.tensor_tensor(
                        o[:, :], a[:, :], c[:, :], op))
                    b.op("ve", lambda e: e.scalar_tensor_tensor(
                        s_E[:, :], s_W[:, :], 1e9, s_ET[:, :], ALU.mult, ALU.add),
                        waits=[(b.S(dxn), tsd)])
                    for k in range(4):
                        b.op("ve", lambda e, k=k: e.tensor_reduce(
                            s_mn[:, k * 8:(k + 1) * 8],
                            s_E[:, k * 128:(k + 1) * 128].rearrange(
                                "p (i s) -> p s i", i=16), AX.X, ALU.min))
                    for k in range(4):
                        b.op("ve", lambda e, k=k: e.tensor_tensor(
                            s_W[:, k * 128:(k + 1) * 128].rearrange("p (i s) -> p s i", i=16),
                            s_E[:, k * 128:(k + 1) * 128].rearrange("p (i s) -> p s i", i=16),
                            s_mn[:, k * 8:(k + 1) * 8].unsqueeze(2).broadcast_to([128, 8, 16]),
                            ALU.is_equal))
                    b.op("ve", lambda e: e.tensor_tensor(
                        s_W[:, :], s_W[:, :], s_rank[:, :], ALU.mult))
                    for k in range(4):
                        b.op("ve", lambda e, k=k: e.tensor_reduce(
                            s_mn[:, k * 8:(k + 1) * 8],
                            s_W[:, k * 128:(k + 1) * 128].rearrange(
                                "p (i s) -> p s i", i=16), AX.X, ALU.max))
                    for k in range(4):
                        b.op("ve", lambda e, k=k: e.tensor_tensor(
                            s_W[:, k * 128:(k + 1) * 128].rearrange("p (i s) -> p s i", i=16),
                            s_W[:, k * 128:(k + 1) * 128].rearrange("p (i s) -> p s i", i=16),
                            s_mn[:, k * 8:(k + 1) * 8].unsqueeze(2).broadcast_to([128, 8, 16]),
                            ALU.is_equal))
                    b.op("ve", lambda e: e.tensor_tensor(
                        s_t1[:, :], s_W[:, :], s_shr5[:, :], ALU.mult))
                    for k in range(4):
                        b.op("ve", lambda e, k=k: e.tensor_reduce(
                            s_sm32[:, k:32:4], s_t1[:, k * 128:(k + 1) * 128].rearrange(
                                "p (i s) -> p s i", i=16), AX.X, ALU.add))
                    t2sel = b.S("ve").n
                    t2ts = b.op("pe", lambda e: e.transpose(
                        pb[6][0:32, 0:128], s_sm32[:, 0:32], self.s_identf[:, :]),
                        waits=[(b.S("ve"), t2sel), (b.S(dxn), tsd)], inc=b.S("pe"))
                    t2ev = b.op("ve", lambda e: e.tensor_copy(
                        s_tr32[0:32, :], pb[6][0:32, 0:128]), waits=[(b.S("pe"), t2ts)])
                    for sl in range(npk):
                        for j in range(2):
                            tsd = b.dma("sy", dxn, s_sls2[sl][j:j + 1, :],
                                        s_tr32[(2 * sl + j) * 4:(2 * sl + j) * 4 + 4, :],
                                        waits=[(b.S("ve"), t2ev)])
                    for sl in range(npk):
                        b.op("ve", TTo(s_rrs[sl], s_sls[sl], s_sls2[sl], ALU.min),
                             waits=[(b.S(dxn), tsd), (b.S("pe"), b.S("pe").n)])
                        b.op("ve", TTo(s_sls2[sl], s_sls[sl], s_sls2[sl], ALU.max))
                        b.op("ve", lambda e, sl=sl: e.tensor_copy(
                            s_sls[sl][:, :], s_rrs[sl][:, :]))
                    for c, sel_r in ((0, s_sls), (1, s_sls2)):
                        for sl in range(npk):
                            b.op("ve", TTo(s_rrs[sl], s_brs[sl], sel_r[sl], ALU.mult))
                            tbc1 = b.op("pe", lambda e, sl=sl: e.matmul(
                                pb[3][:, 0:512], self.s_ind2[0:2, :], s_rrs[sl][:, :],
                                start=True, stop=True),
                                waits=[(b.S("ve"), b.S("ve").n)], inc=b.S("pe"))
                            b.op("ve", lambda e: e.tensor_copy(
                                s_E[:, :], pb[3][:, 0:512]), waits=[(b.S("pe"), tbc1)])
                            b.op("ve", lambda e: e.reciprocal(s_ET[:, :], s_E[:, :]))
                            b.op("ve", lambda e, sl=sl: e.tensor_tensor(
                                s_r[:, :], s_wt[sl][:, :], s_ET[:, :], ALU.mult))
                            b.op("ve", lambda e: e.tensor_scalar(
                                s_r[:, :], s_r[:, :], MAGIC, MAGIC, ALU.add, ALU.subtract))
                            b.op("ve", lambda e: e.tensor_scalar(
                                s_r[:, :], s_r[:, :], 15.0, -15.0, ALU.min, ALU.max))
                            b.op("ve", lambda e: e.tensor_tensor(
                                s_r2[:, :], s_r[:, :], s_E[:, :], ALU.mult))
                            b.op("ve", lambda e, sl=sl: e.tensor_tensor(
                                s_r2[:, :], s_r2[:, :], s_wt[sl][:, :], ALU.subtract))
                            td2 = b.op("ve", lambda e: e.tensor_tensor(
                                s_r2[:, :], s_r2[:, :], s_r2[:, :], ALU.mult))
                            ttrs = []
                            for q in range(4):
                                ttrs.append(b.op("pe", lambda e, q=q: e.transpose(
                                    pb[5][:, q * 128:(q + 1) * 128],
                                    s_r2[:, q * 128:(q + 1) * 128], self.s_identf[:, :]),
                                    waits=[(b.S("ve"), td2)], inc=b.S("pe")))
                            for q in range(4):
                                c0 = c * 16 + 2 * sl * 4 + q
                                b.op("ve", lambda e, q=q, c0=c0: e.tensor_reduce(
                                    s_EN[:, c0:c0 + 5:4],
                                    pb[5][:, q * 128:(q + 1) * 128].rearrange(
                                        "p (j f) -> p j f", j=2), AX.X, ALU.add),
                                    waits=[(b.S("pe"), ttrs[3])])
                    b.op("ve", lambda e: e.tensor_tensor(
                        s_sm32[:, 0:16], s_EN[:, 0:16], s_EN[:, 16:32], ALU.min))
                    tww = b.op("ve", lambda e: e.tensor_tensor(
                        s_sm32[:, 0:16], s_sm32[:, 0:16], s_EN[:, 0:16], ALU.not_equal))
                    tw2 = b.op("pe", lambda e: e.transpose(
                        pb[6][0:16, 0:128], s_sm32[:, 0:16], self.s_identf[:, :]),
                        waits=[(b.S("ve"), tww), (b.S(dxn), b.S(dxn).n)],
                        inc=b.S("pe"))
                    tew = b.op("ve", lambda e: e.tensor_copy(
                        s_tr32[0:16, :], pb[6][0:16, 0:128]), waits=[(b.S("pe"), tw2)])
                    for sl in range(npk):
                        for j in range(2):
                            tsd = b.dma("sy", dxn, s_ils[sl][j:j + 1, :],
                                        s_tr32[(2 * sl + j) * 4:(2 * sl + j) * 4 + 4, :],
                                        waits=[(b.S("ve"), tew)])
                    for sl in range(npk):
                        b.op("ve", TTo(s_rrs[sl], s_sls2[sl], s_sls[sl], ALU.subtract),
                             waits=[(b.S(dxn), tsd)])
                        b.op("ve", TTo(s_rrs[sl], s_rrs[sl], s_ils[sl], ALU.mult))
                        b.op("ve", TTo(s_sls[sl], s_sls[sl], s_rrs[sl], ALU.add))
                for sl in range(npk):
                    tm2 = b.op("ve", lambda e, sl=sl: e.tensor_tensor(
                        s_rrs[sl][:, :], s_brs[sl][:, :], s_sls[sl][:, :], ALU.mult),
                        waits=[(b.S(dxn), tsd)])
                    tsb = b.op("pe", lambda e, sl=sl: e.matmul(
                        pb[3][:, 0:512], self.s_ind2[0:2, :], s_rrs[sl][:, :],
                        start=True, stop=True),
                        waits=[(b.S("ve"), b.S("ve").n)], inc=b.S("pe"))
                    dq_t = s_dqh if half else s_dq
                    if not half:
                        tib = b.op("pe", lambda e, sl=sl: e.matmul(
                            pb[2][:, 0:512], self.s_ind2[0:2, :], s_ils[sl][:, :],
                            start=True, stop=True),
                            waits=[(b.S("ve"), b.S("ve").n)], inc=b.S("pe"))
                        b.op("ve", lambda e, sl=sl: e.tensor_tensor(
                            s_t1[:, :], s_u[sl][:, :], pb[2][:, 0:512], ALU.mult),
                            waits=[(b.S("pe"), tib)])
                        b.op("ve", lambda e: e.tensor_scalar(
                            s_t1[:, :], s_t1[:, :], MAGIC, MAGIC, ALU.add, ALU.subtract))
                        b.op("ve", lambda e: e.tensor_scalar(
                            s_t1[:, :], s_t1[:, :], 15.0, -15.0, ALU.min, ALU.max))
                        tdq = b.op("ve", lambda e, dq_t=dq_t: e.tensor_tensor(
                            dq_t[:, :], s_t1[:, :], pb[3][:, 0:512], ALU.mult),
                            waits=[(b.S("pe"), tsb), (b.S("dsh"), b.S("dsh").n)])
                    else:
                        TT = lambda o, a, c, op: (lambda e: e.tensor_tensor(
                            o[:, :], a[:, :], c[:, :], op))
                        TS = lambda o, a, s1, s2, op1, op2=None: (
                            (lambda e: e.tensor_scalar(o[:, :], a[:, :], s1, s2,
                                                       op1, op2)) if op2 is not None
                            else (lambda e: e.tensor_scalar(o[:, :], a[:, :], s1,
                                                            s2, op1)))
                        b.op("ve", lambda e: e.tensor_copy(s_E[:, :], pb[3][:, 0:512]),
                             waits=[(b.S("pe"), tsb)])
                        b.op("ve", lambda e: e.reciprocal(s_ET[:, :], s_E[:, :]))
                        b.op("ve", TT(s_r2, s_ET, s_E, ALU.mult))
                        b.op("ve", TS(s_t1, s_E, 4097.0, None, ALU.mult))
                        b.op("ve", TT(s_u[0], s_t1, s_E, ALU.subtract))
                        b.op("ve", TT(s_rv, s_t1, s_u[0], ALU.subtract))
                        b.op("ve", TT(s_W, s_E, s_rv, ALU.subtract))
                        b.op("ve", TS(s_t1, s_ET, 4097.0, None, ALU.mult))
                        b.op("ve", TT(s_u[0], s_t1, s_ET, ALU.subtract))
                        b.op("ve", TT(s_u[1], s_t1, s_u[0], ALU.subtract))
                        b.op("ve", TT(s_u[0], s_ET, s_u[1], ALU.subtract))
                        b.op("ve", TT(s_v[0], s_u[1], s_rv, ALU.mult))
                        b.op("ve", TT(s_v[0], s_v[0], s_r2, ALU.subtract))
                        b.op("ve", TT(s_v[1], s_u[1], s_W, ALU.mult))
                        b.op("ve", TT(s_v[0], s_v[0], s_v[1], ALU.add))
                        b.op("ve", TT(s_v[1], s_u[0], s_rv, ALU.mult))
                        b.op("ve", TT(s_v[0], s_v[0], s_v[1], ALU.add))
                        b.op("ve", TT(s_v[1], s_u[0], s_W, ALU.mult))
                        b.op("ve", TT(s_v[0], s_v[0], s_v[1], ALU.add))
                        b.op("ve", TS(s_v[1], s_r2, -1.0, 1.0, ALU.mult, ALU.add))
                        b.op("ve", TT(s_v[1], s_v[1], s_v[0], ALU.subtract))
                        b.op("ve", TT(s_v[1], s_v[1], s_ET, ALU.mult))
                        b.op("ve", TT(s_ET, s_ET, s_v[1], ALU.add))
                        b.op("ve", TT(s_r, s_wt[sl], s_ET, ALU.mult))
                        b.op("ve", TS(s_r, s_r, MAGIC, MAGIC, ALU.add, ALU.subtract))
                        b.op("ve", TS(s_r, s_r, 15.0, -15.0, ALU.min, ALU.max))
                        tdq = b.op("ve", lambda e, dq_t=dq_t: e.tensor_tensor(
                            dq_t[:, :], s_r[:, :], s_E[:, :], ALU.mult),
                            waits=[(b.S("dsh"), b.S("dsh").n)])
                    rc = rcs[sl]
                    tlast = b.op("sy", lambda e, rc=rc, sl_f=sl_f, dq_t=dq_t: e.dma_start(
                        out=dst.ap()[rc * 128:(rc + 1) * 128, sl_f], in_=dq_t[:, :]),
                        waits=[(b.S("ve"), tdq)], inc=b.S("dsh"), inck=16)
                return tlast, prevE

            nfc = Fw // 512
            war = {0: b.S("ve").n, 1: b.S("ve").n}
            loads = {0: issue_loads(0, b.S("ve").n)}
            prevE = 0
            for fc in range(nfc):
                par = fc % 2
                if fc + 1 < nfc:
                    loads[fc + 1] = issue_loads(fc + 1, war[(fc + 1) % 2])
                tw, trr = loads.pop(fc)
                tlast, prevE = chunk(fc, s_wtp[par], s_brsp[par], s_rrsp[par],
                                     tw, trr, prevE)
                war[par] = b.S("ve").n
            return tlast

        t_fc = pack([0, 1], FH, self.wf_sh, self.d_wf_s, 0)
        self.t_ag_f = b.op("gp", lambda e: e.collective_compute(
            "AllGather", ALU.bypass, replica_groups=[list(range(8))],
            ins=[self.d_wf_s.ap().opt()], outs=[self.d_wf.ap().opt()]),
            waits=[(b.S("dsh"), t_fc)], inc=b.S("cc"))
        pack([0, 1], C, self.wj_sh, self.d_wj_s, 4, half=True)
        pack([2, 3], C, self.wj_sh, self.d_wj_s, 8, half=True)
        pack([4, 5], C, self.wj_sh, self.d_wj_s, 12, half=True)
        t_pj = pack([6, 7], C, self.wj_sh, self.d_wj_s, 16, half=True)
        self.t_ag_j = b.op("gp", lambda e: e.collective_compute(
            "AllGather", ALU.bypass, replica_groups=[list(range(8))],
            ins=[self.d_wj_s.ap().opt()], outs=[self.d_wj.ap().opt()]),
            waits=[(b.S("dsh"), t_pj)], inc=b.S("cc"))

    def stage_lambda_tiles(self):
        nc, b, pb = self.nc, self.b, self.pb
        s_l16 = self.s_rev16
        tb1 = b.op("ve", lambda e: e.tensor_scalar(
            s_l16[:, :], self.s_iota16[:, :], self.s_scol[0:16, 0:1], -1.0,
            ALU.mult, ALU.mult), waits=[(b.S("dx"), self.t_scol)])
        tb2 = b.op("sc", lambda e: e.activation(s_l16[:, :], s_l16[:, :], ACTF.Exp),
                   waits=[(b.S("ve"), tb1)])
        tb3 = b.dma("sy", "dx", self.s_rowb[0:1, 0:2048], s_l16[0:16, :],
                    waits=[(b.S("ac"), tb2)])
        for i in range(4):
            tg = b.op("pe", lambda e, i=i: e.matmul(
                pb[5][:, 0:512], self.s_ones_row[0:1, :],
                self.s_rowb[0:1, i * 512:(i + 1) * 512], start=True, stop=True),
                waits=[(b.S("dx"), tb3), (b.S("ve"), b.S("ve").n)], inc=b.S("pe"))
            b.op("ve", lambda e, i=i: e.tensor_copy(
                self.s_lt[:, i * 512:(i + 1) * 512], pb[5][:, 0:512]),
                waits=[(b.S("pe"), tg)])
        tc1 = b.op("ve", lambda e: e.tensor_scalar(
            self.s_rowa[0:1, 0:16], self.s_srow[0:1, :], -128.0, None, ALU.mult))
        tc2 = b.op("sc", lambda e: e.activation(
            self.s_rowa[0:1, 0:16], self.s_rowa[0:1, 0:16], ACTF.Exp),
            waits=[(b.S("ve"), tc1)])
        tc3 = b.op("ve", lambda e: e.tensor_copy(
            self.s_rowb[0:1, 0:2048].rearrange("o (h t) -> o h t", h=16),
            self.s_rowa[0:1, 0:16].unsqueeze(2).broadcast_to([1, 16, 128])),
            waits=[(b.S("ac"), tc2)])
        for i in range(4):
            tg = b.op("pe", lambda e, i=i: e.matmul(
                pb[5][:, 0:512], self.s_ones_row[0:1, :],
                self.s_rowb[0:1, i * 512:(i + 1) * 512], start=True, stop=True),
                waits=[(b.S("ve"), b.S("ve").n)], inc=b.S("pe"))
            b.op("ve", lambda e, i=i: e.tensor_copy(
                self.s_lTf[:, i * 512:(i + 1) * 512], pb[5][:, 0:512]),
                waits=[(b.S("pe"), tg)])
        b.op("ve", lambda e: e.memset(self.s_S[:, :], 0.0))
        b.op("ve", lambda e: e.memset(self.s_Sb[:, :], 0.0))

    def stage_qkv(self, wes):
        nc, b, pb = self.nc, self.b, self.pb
        SBw = lambda n, shp, dt=F32: wes.enter_context(nc.sbuf_tensor(n, shp, dt))
        s_h = SBw("s_h", [128, KC * T], BF16)
        s_wres = SBw("s_wres", [128, KC * T], BF16)
        s_x = [SBw(f"s_x{i}", [128, 512], F32) for i in range(2)]
        s_xsq = SBw("s_xsq", [128, 512], F32)
        s_rr0 = self.s_rowb
        s_ev = [SBw(f"s_ev{i}", [128, 512], BF16) for i in range(2)]

        tstat = []
        for q4 in range(4):
            tq = None
            for kc in range(KC):
                tx = b.dma("sy", "dx", s_x[kc % 2][:, :],
                           self.xT[kc * 128:(kc + 1) * 128, q4 * 512:(q4 + 1) * 512],
                           waits=[(b.S("ac"), b.S("ac").n), (b.S("pe"), b.S("pe").n)]
                           if kc >= 2 else [])
                ts = b.op("sc", lambda e, kc=kc: e.activation(
                    s_xsq[:, :], s_x[kc % 2][:, :], ACTF.Square),
                    waits=[(b.S("dx"), tx), (b.S("pe"), b.S("pe").n)])
                tq = b.op("pe", lambda e, kc=kc, q4=q4: e.matmul(
                    pb[6][0:1, 0:512], self.s_ones_col[:, :], s_xsq[:, :],
                    start=(kc == 0), stop=(kc == KC - 1)),
                    waits=[(b.S("ac"), ts)] + ([(b.S("ve"), b.S("ve").n)]
                                               if kc == 0 else []), inc=b.S("pe"))
            tr1 = b.op("ve", lambda e, q4=q4: e.tensor_scalar(
                self.s_rowa[0:1, 0:512], pb[6][0:1, 0:512], 1.0 / C, EPS_F32,
                ALU.mult, ALU.add), waits=[(b.S("pe"), tq)])
            b.op("ve", lambda e: e.reciprocal(self.s_rowa[0:1, 512:1024],
                                              self.s_rowa[0:1, 0:512]))
            tstat.append(b.op("sc", lambda e, q4=q4: e.activation(
                s_rr0[0:1, q4 * 512:(q4 + 1) * 512], self.s_rowa[0:1, 512:1024],
                ACTF.Sqrt), waits=[(b.S("ve"), b.S("ve").n)]))

        for kc in range(KC):
            for q4 in range(4):
                tx = b.dma("sy", "dx", s_x[q4 % 2][:, :],
                           self.xT[kc * 128:(kc + 1) * 128, q4 * 512:(q4 + 1) * 512],
                           waits=[(b.S("ve"), b.S("ve").n), (b.S("ac"), b.S("ac").n)])
                tout = b.op("pe", lambda e, kc=kc, q4=q4: e.matmul(
                    pb[7][:, 0:512], self.s_g1row[0:1, kc * 128:(kc + 1) * 128],
                    s_rr0[0:1, q4 * 512:(q4 + 1) * 512], start=True, stop=True),
                    waits=[(b.S("ac"), tstat[-1]), (b.S("ve"), b.S("ve").n)],
                    inc=b.S("pe"))
                b.op("ve", lambda e, kc=kc, q4=q4: e.tensor_tensor(
                    s_h[:, kc * T + q4 * 512: kc * T + (q4 + 1) * 512],
                    s_x[q4 % 2][:, :], pb[7][:, 0:512], ALU.mult),
                    waits=[(b.S("pe"), tout), (b.S("dx"), tx)])

        tev_st = {0: 0, 1: 0}

        def proj_phase(col0, dst):
            tw = None
            for kc in range(KC):
                tw = b.dma("sy", "dw", s_wres[:, kc * T:(kc + 1) * T],
                           self.d_wq.ap()[kc * 128:(kc + 1) * 128, col0:col0 + C],
                           waits=[(b.S("cc"), self.t_ag_q), (b.S("pe"), b.S("pe").n)])
            for m in range(16):
                pset = [pb[(m % 2) * 4 + i] for i in range(4)]
                tmm = None
                for kc in range(KC):
                    for fch in range(4):
                        tmm = b.op("pe", lambda e, m=m, kc=kc, fch=fch, pset=pset:
                            e.matmul(
                                pset[fch][:, 0:512],
                                s_wres[:, kc * T + m * 128: kc * T + (m + 1) * 128],
                                s_h[:, kc * T + fch * 512: kc * T + (fch + 1) * 512],
                                start=(kc == 0), stop=(kc == KC - 1)),
                            waits=[(b.S("dw"), tw), (b.S("ve"), b.S("ve").n)]
                            if kc == 0 and fch == 0 else [], inc=b.S("pe"))
                for fch in range(4):
                    tev = b.op("ve", lambda e, fch=fch, pset=pset: e.tensor_copy(
                        s_ev[fch % 2][:, :], pset[fch][:, 0:512]),
                        waits=[(b.S("pe"), tmm), (b.S("dzq"), tev_st[fch % 2])])
                    tev_st[fch % 2] = b.op("sy", lambda e, m=m, fch=fch: e.dma_start(
                        out=dst.ap()[:, m * T + fch * 512: m * T + (fch + 1) * 512],
                        in_=s_ev[fch % 2][:, :]),
                        waits=[(b.S("ve"), tev)], inc=b.S("dzq"), inck=16)
        proj_phase(0, self.d_q)
        proj_phase(C, self.d_k)
        proj_phase(2 * C, self.d_v)

    def stage_scan(self, ses):
        nc, b, pb = self.nc, self.b, self.pb
        SBs = lambda n, shp, dt=BF16: ses.enter_context(nc.sbuf_tensor(n, shp, dt))
        s_maskT = ses.enter_context(nc.sbuf_tensor("s_maskT", [128, H * 128], F32))
        for h in range(H):
            ta = b.op("ve", lambda e, h=h: e.tensor_scalar(
                s_maskT[:, h * 128:(h + 1) * 128], self.s_diffT[:, :],
                self.s_snegb[:, h:h + 1], 88.0, ALU.mult, ALU.min))
            tb2 = b.op("sc", lambda e, h=h: e.activation(
                s_maskT[:, h * 128:(h + 1) * 128], s_maskT[:, h * 128:(h + 1) * 128],
                ACTF.Exp), waits=[(b.S("ve"), ta)])
            b.op("ve", lambda e, h=h: e.tensor_tensor(
                s_maskT[:, h * 128:(h + 1) * 128], s_maskT[:, h * 128:(h + 1) * 128],
                self.s_causT[:, :], ALU.mult), waits=[(b.S("ac"), tb2)])

        s_qc = [SBs(f"s_qc{i}", [128, 2048]) for i in range(2)]
        s_kc = [SBs(f"s_kc{i}", [128, 2048]) for i in range(2)]
        s_vc = [SBs(f"s_vc{i}", [128, 2048]) for i in range(2)]
        s_qt = ses.enter_context(nc.sbuf_tensor("s_qt", [128, 2048], FP16))
        s_kt = ses.enter_context(nc.sbuf_tensor("s_kt", [128, 512], FP16))
        s_vt = ses.enter_context(nc.sbuf_tensor("s_vt", [128, 2048], FP16))
        s_sm = ses.enter_context(nc.sbuf_tensor("s_sm", [128, 512], FP16))
        s_oc = ses.enter_context(nc.sbuf_tensor("s_oc", [128, 2048], FP16))
        tprev_o = 0
        to_prev = 0

        def scan_loads(c, war_ve, war_pe):
            bufc = c % 2
            sem = "dq" if bufc == 0 else "dg"
            ap3 = lambda d: d.ap()[:, :].rearrange("p (h t) -> p h t", h=H)[
                :, :, c * 128:(c + 1) * 128]
            tq = b.dma("sy", sem, s_qc[bufc][:, :].rearrange("p (h t) -> p h t", t=128),
                       ap3(self.d_q), waits=[(b.S("ve"), war_ve), (b.S("pe"), war_pe)])
            tk = b.dma("sy", sem, s_kc[bufc][:, :].rearrange("p (h t) -> p h t", t=128),
                       ap3(self.d_k))
            tv = b.dma("sy", sem, s_vc[bufc][:, :].rearrange("p (h t) -> p h t", t=128),
                       ap3(self.d_v))
            return sem, tq, tk, tv

        sc_pend = {0: scan_loads(0, b.S("ve").n, b.S("pe").n)}
        sc_war = {0: (0, 0), 1: (0, 0)}
        for c in range(NCH):
            bufc = c % 2
            if c + 1 < NCH:
                sc_pend[c + 1] = scan_loads(c + 1, *sc_war[(c + 1) % 2])
            dqn, tq, tk, tv = sc_pend.pop(c)
            tqt = b.op("ve", lambda e, bufc=bufc: e.tensor_tensor(
                s_qt[:, :], s_qc[bufc][:, :], self.s_lt[:, :], ALU.mult),
                waits=[(b.S(dqn), tq), (b.S("pe"), b.S("pe").n)])
            for g4 in range(4):
                pv = pb[g4 % 2][:, 0:512].bitcast(BF16)
                for hh in range(4):
                    h = g4 * 4 + hh
                    tt = b.op("pe", lambda e, h=h, hh=hh, bufc=bufc, pv=pv: e.transpose(
                        pv[:, hh * 128:(hh + 1) * 128],
                        s_vc[bufc][:, h * 128:(h + 1) * 128], self.s_identb[:, :]),
                        waits=[(b.S(dqn), tv), (b.S("ve"), b.S("ve").n)], inc=b.S("pe"))
                tve = b.op("ve", lambda e, g4=g4, pv=pv: e.tensor_copy(
                    s_vt[:, g4 * 512:(g4 + 1) * 512], pv[:, 0:512]),
                    waits=[(b.S("pe"), tt)])
            for g4 in range(4):
                pkv = pb[2][:, 0:512].bitcast(BF16)
                for hh in range(4):
                    h = g4 * 4 + hh
                    ttk = b.op("pe", lambda e, h=h, hh=hh, bufc=bufc, pkv=pkv: e.transpose(
                        pkv[:, hh * 128:(hh + 1) * 128],
                        s_kc[bufc][:, h * 128:(h + 1) * 128], self.s_identb[:, :]),
                        waits=[(b.S(dqn), tk), (b.S("ve"), b.S("ve").n)], inc=b.S("pe"))
                for hh in range(4):
                    h = g4 * 4 + hh
                    b.op("ve", lambda e, h=h, hh=hh, pkv=pkv: e.tensor_scalar(
                        s_kt[:, hh * 128:(hh + 1) * 128], pkv[:, hh * 128:(hh + 1) * 128],
                        self.s_lrev[:, h:h + 1], None, ALU.mult),
                        waits=[(b.S("pe"), ttk)])
                tkt = b.S("ve").n
                for hh in range(4):
                    h = g4 * 4 + hh
                    tsc = b.op("pe", lambda e, h=h, hh=hh, bufc=bufc: e.matmul(
                        pb[3][:, hh * 128:(hh + 1) * 128],
                        s_kc[bufc][:, h * 128:(h + 1) * 128],
                        s_qc[bufc][:, h * 128:(h + 1) * 128],
                        start=True, stop=True),
                        waits=[(b.S("ve"), tkt)], inc=b.S("pe"))
                tms = b.op("ve", lambda e, g4=g4: e.tensor_tensor(
                    s_sm[:, :], pb[3][:, 0:512], s_maskT[:, g4 * 512:(g4 + 1) * 512],
                    ALU.mult), waits=[(b.S("pe"), tsc)])
                po = pb[4 + (g4 % 2)]
                for hh in range(4):
                    h = g4 * 4 + hh
                    tin = b.op("pe", lambda e, h=h, hh=hh, po=po: e.matmul(
                        po[:, hh * 128:(hh + 1) * 128],
                        s_vt[:, (h // 4) * 512 + (h % 4) * 128:
                             (h // 4) * 512 + (h % 4) * 128 + 128],
                        s_sm[:, hh * 128:(hh + 1) * 128], start=True, stop=False),
                        waits=[(b.S("ve"), tms)] + ([(b.S("ac"), b.S("ac").n)]
                                                    if hh == 0 else []),
                        inc=b.S("pe"))
                    tcx = b.op("pe", lambda e, h=h, hh=hh, po=po: e.matmul(
                        po[:, hh * 128:(hh + 1) * 128],
                        self.s_Sb[:, h * 128:(h + 1) * 128],
                        s_qt[:, h * 128:(h + 1) * 128], start=False, stop=True),
                        waits=[(b.S("ve"), tqt), (b.S("ve"), tprev_o)], inc=b.S("pe"))
                tev = b.op("sc", lambda e, g4=g4, po=po: e.activation(
                    s_oc[:, g4 * 512:(g4 + 1) * 512], po[:, 0:512], ACTF.Identity,
                    scale=SCALE_ATTN),
                    waits=[(b.S("pe"), tcx), (b.S("dzq"), to_prev)])
                pk = pb[6 + (g4 % 2)]
                for hh in range(4):
                    h = g4 * 4 + hh
                    tkv = b.op("pe", lambda e, h=h, hh=hh, pk=pk: e.matmul(
                        pk[:, hh * 128:(hh + 1) * 128],
                        s_kt[:, hh * 128:(hh + 1) * 128],
                        s_vt[:, (h // 4) * 512 + (h % 4) * 128:
                             (h // 4) * 512 + (h % 4) * 128 + 128],
                        start=True, stop=True),
                        waits=[(b.S("ve"), b.S("ve").n)] if hh == 0 else [],
                        inc=b.S("pe"))
                sl = slice(g4 * 512, (g4 + 1) * 512)
                b.op("ve", lambda e, sl=sl, pk=pk: e.tensor_tensor(
                    self.s_S[:, sl], self.s_S[:, sl], self.s_lTf[:, sl], ALU.mult),
                    waits=[(b.S("pe"), tkv)])
                b.op("ve", lambda e, sl=sl, pk=pk: e.tensor_tensor(
                    self.s_S[:, sl], self.s_S[:, sl], pk[:, 0:512], ALU.add))
                tprev_o = b.op("ve", lambda e, sl=sl: e.tensor_copy(
                    self.s_Sb[:, sl], self.s_S[:, sl]))
            to_prev = b.op("sy", lambda e, c=c: e.dma_start(
                out=self.d_o.ap()[:, :].rearrange("p (h t) -> p h t", h=H)[
                    :, :, c * 128:(c + 1) * 128],
                in_=s_oc[:, :].rearrange("p (h t) -> p h t", t=128)),
                waits=[(b.S("ac"), b.S("ac").n)], inc=b.S("dzq"), inck=16)
            sc_war[c % 2] = (b.S("ve").n, b.S("pe").n)

    def stage_states(self):
        nc, b, pb = self.nc, self.b, self.pb
        tso = b.op("gp", lambda e: e.dma_start(out=self.d_st_i.ap(), in_=self.s_S[:, :]),
                   waits=[(b.S("ve"), b.S("ve").n)], inc=b.S("dg"), inck=16)
        tag = b.op("gp", lambda e: e.collective_compute(
            "AllGather", ALU.bypass,
            replica_groups=[[0, 1, 2, 3], [4, 5, 6, 7]],
            ins=[self.d_st_i.ap().opt()], outs=[self.d_st_o.ap().opt()]),
            waits=[(b.S("dg"), tso)], inc=b.S("cc"))
        b.op("ve", lambda e: e.memset(self.s_S[:, :], 0.0),
             waits=[(b.S("cc"), tag), (b.S("dg"), tso)])
        for j in range(3):
            tp = b.dma("sy", "dq", self.s_P[:, :],
                       self.d_st_o.ap()[j * 128:(j + 1) * 128, :],
                       waits=[(b.S("cc"), tag), (b.S("ve"), b.S("ve").n)])
            t1 = b.op("ve", lambda e, j=j: e.tensor_scalar(
                self.s_rowa[0:1, 0:16], self.s_srow[0:1, :],
                self.s_cvec[0:1, 3 + j:4 + j], -2048.0, ALU.mult, ALU.mult))
            t2 = b.op("sc", lambda e: e.activation(
                self.s_rowa[0:1, 0:16], self.s_rowa[0:1, 0:16], ACTF.Exp),
                waits=[(b.S("ve"), t1)])
            b.op("ve", lambda e, j=j: e.tensor_scalar(
                self.s_rowa[0:1, 0:16], self.s_rowa[0:1, 0:16],
                self.s_cvec[0:1, j:j + 1], None, ALU.mult), waits=[(b.S("ac"), t2)])
            t4 = b.op("ve", lambda e: e.tensor_copy(
                self.s_rowb[0:1, 0:2048].rearrange("o (h t) -> o h t", h=16),
                self.s_rowa[0:1, 0:16].unsqueeze(2).broadcast_to([1, 16, 128])))
            for i in range(4):
                sl = slice(i * 512, (i + 1) * 512)
                tg = b.op("pe", lambda e, i=i: e.matmul(
                    pb[5][:, 0:512], self.s_ones_row[0:1, :],
                    self.s_rowb[0:1, i * 512:(i + 1) * 512], start=True, stop=True),
                    waits=[(b.S("ve"), b.S("ve").n), (b.S("dq"), tp)], inc=b.S("pe"))
                b.op("ve", lambda e, sl=sl: e.tensor_tensor(
                    self.s_sc512[:, :], self.s_P[:, sl], pb[5][:, 0:512], ALU.mult),
                    waits=[(b.S("pe"), tg)])
                b.op("ve", lambda e, sl=sl: e.tensor_tensor(
                    self.s_S[:, sl], self.s_S[:, sl], self.s_sc512[:, :], ALU.add))
        b.op("ve", lambda e: e.tensor_copy(self.s_Sb[:, :], self.s_S[:, :]))

    def stage_corr_rms(self, a2s):
        nc, b, pb = self.nc, self.b, self.pb
        SBa = lambda n, shp, dt=F32: a2s.enter_context(nc.sbuf_tensor(n, shp, dt))
        s_qc = [SBa(f"s_aqc{i}", [128, 2048], BF16) for i in range(2)]
        s_oc = [SBa(f"s_aoc{i}", [128, 2048], FP16) for i in range(2)]
        s_qt = SBa("s_aqt", [128, 2048], FP16)
        s_o2 = SBa("s_o2", [128, 2048], F32)
        s_o2sq = SBa("s_o2sq", [128, 2048], F32)
        s_zc = SBa("s_zc", [128, 2048], BF16)
        s_g2row = SBa("s_g2row", [1, C], F32)
        s_g3row = SBa("s_g3row", [1, C], F32)
        s_g23row = SBa("s_g23row", [1, C], F32)
        s_g2sqc = SBa("s_g2sqc", [128, H], F32)
        tg2 = b.dma("sy", "dx", s_g2row[0:1, :], self.g2)
        tg3 = b.dma("sy", "dx", s_g3row[0:1, :], self.g3)
        b.op("ve", lambda e: e.tensor_tensor(
            s_g23row[0:1, :], s_g2row[0:1, :], s_g3row[0:1, :], ALU.mult),
            waits=[(b.S("dx"), tg3)])
        tgc = None
        for h in range(H):
            tgc = b.dma("sy", "dx", s_g2sqc[0:128, h:h + 1],
                        s_g2row[0:1, h * 128:(h + 1) * 128],
                        waits=[(b.S("dx"), tg2)])
        b.op("ve", lambda e: e.tensor_tensor(
            s_g2sqc[:, :], s_g2sqc[:, :], s_g2sqc[:, :], ALU.mult),
            waits=[(b.S("dx"), tgc)])

        tzst_prev = 0
        def corr_loads(c, war_ve):
            bufc = c % 2
            sem = "dq" if bufc == 0 else "dg"
            ap3 = lambda d: d.ap()[:, :].rearrange("p (h t) -> p h t", h=H)[
                :, :, c * 128:(c + 1) * 128]
            tq = b.dma("sy", sem, s_qc[bufc][:, :].rearrange("p (h t) -> p h t", t=128),
                       ap3(self.d_q), waits=[(b.S("ve"), war_ve)])
            to = b.dma("sy", sem, s_oc[bufc][:, :].rearrange("p (h t) -> p h t", t=128),
                       ap3(self.d_o))
            return sem, tq, to

        cr_pend = {0: corr_loads(0, b.S("ve").n)}
        cr_war = {0: 0, 1: 0}
        for c in range(NCH):
            bufc = c % 2
            if c + 1 < NCH:
                cr_pend[c + 1] = corr_loads(c + 1, cr_war[(c + 1) % 2])
            dqn, tq, to = cr_pend.pop(c)
            tqt = b.op("ve", lambda e, bufc=bufc: e.tensor_tensor(
                s_qt[:, :], s_qc[bufc][:, :], self.s_lt[:, :], ALU.mult),
                waits=[(b.S(dqn), tq), (b.S("pe"), b.S("pe").n)])
            tcr = None
            for h in range(H):
                tcr = b.op("pe", lambda e, h=h: e.matmul(
                    pb[h // 4][:, (h % 4) * 128:(h % 4 + 1) * 128],
                    self.s_Sb[:, h * 128:(h + 1) * 128],
                    s_qt[:, h * 128:(h + 1) * 128], start=True, stop=True),
                    waits=[(b.S("ve"), tqt)], inc=b.S("pe"))
            for i in range(4):
                b.op("ve", lambda e, i=i, bufc=bufc: e.scalar_tensor_tensor(
                    s_o2[:, i * 512:(i + 1) * 512], pb[i][:, 0:512], SCALE_ATTN,
                    s_oc[bufc][:, i * 512:(i + 1) * 512], ALU.mult, ALU.add),
                    waits=[(b.S("pe"), tcr), (b.S(dqn), to), (b.S("ac"), b.S("ac").n)])
            for i in range(4):
                sl = slice(i * 512, (i + 1) * 512)
                b.op("ve", lambda e, sl=sl: e.tensor_tensor(
                    self.s_S[:, sl], self.s_S[:, sl], self.s_lTf[:, sl], ALU.mult))
            b.op("ve", lambda e: e.tensor_copy(self.s_Sb[:, :], self.s_S[:, :]))
            tsq = b.op("sc", lambda e: e.activation(s_o2sq[:, :], s_o2[:, :], ACTF.Square),
                       waits=[(b.S("ve"), b.S("ve").n)])
            tst = tst2 = None
            for h in range(H):
                tst = b.op("pe", lambda e, h=h: e.matmul(
                    pb[4][0:1, 0:128], self.s_ones_col[:, :],
                    s_o2sq[:, h * 128:(h + 1) * 128], start=(h == 0), stop=(h == H - 1)),
                    waits=[(b.S("ac"), tsq), (b.S("ve"), b.S("ve").n)], inc=b.S("pe"))
            for h in range(H):
                tst2 = b.op("pe", lambda e, h=h: e.matmul(
                    pb[4][0:1, 128:256], s_g2sqc[:, h:h + 1],
                    s_o2sq[:, h * 128:(h + 1) * 128], start=(h == 0), stop=(h == H - 1)),
                    inc=b.S("pe"))
            b.op("ve", lambda e: e.tensor_scalar(
                self.s_rowa[0:1, 0:128], pb[4][0:1, 0:128], 1.0 / C, 1e-6,
                ALU.mult, ALU.add), waits=[(b.S("pe"), tst2)])
            b.op("ve", lambda e: e.reciprocal(self.s_rowa[0:1, 128:256],
                                              self.s_rowa[0:1, 0:128]))
            tr2 = b.op("sc", lambda e: e.activation(
                self.s_rowa[0:1, 256:384], self.s_rowa[0:1, 128:256], ACTF.Sqrt),
                waits=[(b.S("ve"), b.S("ve").n)])
            b.op("ve", lambda e: e.tensor_scalar(
                self.s_rowa[0:1, 384:512], pb[4][0:1, 128:256], 1.0 / C, None,
                ALU.mult), waits=[(b.S("ac"), tr2)])
            b.op("ve", lambda e: e.tensor_tensor(
                self.s_rowa[0:1, 512:640], self.s_rowa[0:1, 256:384],
                self.s_rowa[0:1, 256:384], ALU.mult))
            b.op("ve", lambda e: e.tensor_tensor(
                self.s_rowa[0:1, 640:768], self.s_rowa[0:1, 512:640],
                self.s_rowa[0:1, 384:512], ALU.mult))
            b.op("ve", lambda e: e.tensor_scalar(
                self.s_rowa[0:1, 640:768], self.s_rowa[0:1, 640:768], EPS_F32, None,
                ALU.add))
            b.op("ve", lambda e: e.reciprocal(self.s_rowa[0:1, 768:896],
                                              self.s_rowa[0:1, 640:768]))
            tr3 = b.op("sc", lambda e: e.activation(
                self.s_rowa[0:1, 896:1024], self.s_rowa[0:1, 768:896], ACTF.Sqrt))
            tr23 = b.op("ve", lambda e: e.tensor_tensor(
                self.s_rowa[0:1, 1024:1152], self.s_rowa[0:1, 256:384],
                self.s_rowa[0:1, 896:1024], ALU.mult), waits=[(b.S("ac"), tr3)])
            tgo = None
            for h in range(H):
                tgo = b.op("pe", lambda e, h=h: e.matmul(
                    pb[4 + h // 4][:, (h % 4) * 128:(h % 4 + 1) * 128],
                    s_g23row[0:1, h * 128:(h + 1) * 128],
                    self.s_rowa[0:1, 1024:1152], start=True, stop=True),
                    waits=[(b.S("ve"), tr23)], inc=b.S("pe"))
            for i in range(4):
                b.op("ve", lambda e, i=i: e.tensor_tensor(
                    s_zc[:, i * 512:(i + 1) * 512], s_o2[:, i * 512:(i + 1) * 512],
                    pb[4 + i][:, 0:512], ALU.mult),
                    waits=[(b.S("pe"), tgo), (b.S("dzq"), tzst_prev)])
            tzst_prev = b.op("sy", lambda e, c=c: e.dma_start(
                out=self.d_z.ap()[:, :].rearrange("p (h t) -> p h t", h=H)[
                    :, :, c * 128:(c + 1) * 128],
                in_=s_zc[:, :].rearrange("p (h t) -> p h t", t=128)),
                waits=[(b.S("ve"), b.S("ve").n)], inc=b.S("dzq"), inck=16)
            cr_war[c % 2] = b.S("ve").n

    def stage_proj_ln1(self, pjs):
        nc, b, pb = self.nc, self.b, self.pb
        SBp = lambda n, shp, dt=F32: pjs.enter_context(nc.sbuf_tensor(n, shp, dt))
        s_wp = SBp("s_wp", [128, KC * C], BF16)
        s_zf = SBp("s_zf", [128, KC * 512], BF16)
        s_xt = [SBp(f"s_pxt{i}", [128, 512], F32) for i in range(2)]
        s_y = SBp("s_y", [128, KC * 512], F32)
        s_ysq = SBp("s_ysq", [128, 512], F32)
        s_z1 = [SBp(f"s_z1_{i}", [128, 512], F32) for i in range(2)]
        s_z1bt = [SBp(f"s_z1bt{i}", [128, 512], BF16) for i in range(2)]
        tw = None
        for kc in range(KC):
            tw = b.dma("sy", "dw", s_wp[:, kc * C:(kc + 1) * C],
                       self.d_wp.ap()[kc * 128:(kc + 1) * 128, :],
                       waits=[(b.S("cc"), self.t_ag_p)])
        tz1cast = {0: 0, 1: 0}
        tz1ca = {0: 0, 1: 0}
        for fch in range(4):
            tzf = None
            for kc in range(KC):
                tzf = b.dma("sy", "dq", s_zf[:, kc * 512:(kc + 1) * 512],
                            self.d_z.ap()[:, kc * T + fch * 512: kc * T + (fch + 1) * 512],
                            waits=[(b.S("pe"), b.S("pe").n)] if kc == 0 else [])
            for m in range(KC):
                pm = pb[m % 2]
                tmm = None
                for kc in range(KC):
                    tmm = b.op("pe", lambda e, m=m, kc=kc, pm=pm: e.matmul(
                        pm[:, 0:512],
                        s_wp[:, kc * C + m * 128: kc * C + (m + 1) * 128],
                        s_zf[:, kc * 512:(kc + 1) * 512],
                        start=(kc == 0), stop=(kc == KC - 1)),
                        waits=[(b.S("dw"), tw), (b.S("dq"), tzf),
                               (b.S("ve"), b.S("ve").n)] if kc == 0 else [],
                        inc=b.S("pe"))
                txt = b.dma("sy", "dx", s_xt[m % 2][:, :],
                            self.xT[m * 128:(m + 1) * 128, fch * 512:(fch + 1) * 512],
                            waits=[(b.S("ve"), b.S("ve").n)])
                b.op("ve", lambda e, m=m, pm=pm: e.tensor_tensor(
                    s_y[:, m * 512:(m + 1) * 512], pm[:, 0:512], s_xt[m % 2][:, :],
                    ALU.add), waits=[(b.S("pe"), tmm), (b.S("dx"), txt),
                                     (b.S("ac"), b.S("ac").n)])
            tsy = tsq = None
            for m in range(KC):
                tsy = b.op("pe", lambda e, m=m: e.matmul(
                    pb[2][0:1, 0:512], self.s_ones_col[:, :],
                    s_y[:, m * 512:(m + 1) * 512], start=(m == 0), stop=(m == KC - 1)),
                    waits=[(b.S("ve"), b.S("ve").n)], inc=b.S("pe"))
            for m in range(KC):
                ts = b.op("sc", lambda e, m=m: e.activation(
                    s_ysq[:, :], s_y[:, m * 512:(m + 1) * 512], ACTF.Square),
                    waits=[(b.S("pe"), tsy if m == 0 else b.S("pe").n)])
                tsq = b.op("pe", lambda e, m=m: e.matmul(
                    pb[3][0:1, 0:512], self.s_ones_col[:, :], s_ysq[:, :],
                    start=(m == 0), stop=(m == KC - 1)),
                    waits=[(b.S("ac"), ts)], inc=b.S("pe"))
            b.op("ve", lambda e: e.tensor_scalar(
                self.s_rowa[0:1, 0:512], pb[2][0:1, 0:512], 1.0 / C, None, ALU.mult),
                waits=[(b.S("pe"), tsq)])
            b.op("ve", lambda e: e.tensor_scalar(
                self.s_rowa[0:1, 512:1024], pb[3][0:1, 0:512], 1.0 / C, None, ALU.mult))
            b.op("ve", lambda e: e.tensor_tensor(
                self.s_rowb[0:1, 0:512], self.s_rowa[0:1, 0:512],
                self.s_rowa[0:1, 0:512], ALU.mult))
            b.op("ve", lambda e: e.tensor_tensor(
                self.s_rowb[0:1, 0:512], self.s_rowa[0:1, 512:1024],
                self.s_rowb[0:1, 0:512], ALU.subtract))
            b.op("ve", lambda e: e.tensor_scalar(
                self.s_rowb[0:1, 0:512], self.s_rowb[0:1, 0:512], 1e-5, None, ALU.add))
            b.op("ve", lambda e: e.reciprocal(self.s_rowb[0:1, 512:1024],
                                              self.s_rowb[0:1, 0:512]))
            trr = b.op("sc", lambda e: e.activation(
                self.s_rowb[0:1, 0:512], self.s_rowb[0:1, 512:1024], ACTF.Sqrt))
            tmt = b.op("ve", lambda e: e.tensor_tensor(
                self.s_rowa[0:1, 1024:1536], self.s_rowa[0:1, 0:512],
                self.s_rowb[0:1, 0:512], ALU.mult), waits=[(b.S("ac"), trr)])
            if fch == 0:
                b.op("sy", lambda e: e.dma_start(
                    out=self.d_dbg.ap()[0:1, 0:512], in_=self.s_rowa[0:1, 0:512]),
                    waits=[(b.S("ve"), tmt)], inc=b.S("dst"), inck=16)
                b.op("sy", lambda e: e.dma_start(
                    out=self.d_dbg.ap()[1:2, 0:512], in_=self.s_rowa[0:1, 512:1024]),
                    inc=b.S("dst"), inck=16)
                b.op("sy", lambda e: e.dma_start(
                    out=self.d_dbg.ap()[2:3, 0:512], in_=self.s_rowb[0:1, 0:512]),
                    inc=b.S("dst"), inck=16)
                b.op("sy", lambda e: e.dma_start(
                    out=self.d_dbg.ap()[3:4, 0:512], in_=s_ysq[0:1, 0:512]),
                    inc=b.S("dst"), inck=16)
            tB = b.op("pe", lambda e: e.matmul(
                pb[4][:, 0:512], self.s_ones_row[0:1, :], self.s_rowb[0:1, 0:512],
                start=True, stop=True), waits=[(b.S("ve"), tmt)], inc=b.S("pe"))
            tC = b.op("pe", lambda e: e.matmul(
                pb[5][:, 0:512], self.s_ones_row[0:1, :], self.s_rowa[0:1, 1024:1536],
                start=True, stop=True), inc=b.S("pe"))
            tz1st = {0: 0, 1: 0}
            for m in range(KC):
                pbuf = m % 2
                b.op("ve", lambda e, m=m, pbuf=pbuf: e.tensor_tensor(
                    s_z1[pbuf][:, :], s_y[:, m * 512:(m + 1) * 512], pb[4][:, 0:512],
                    ALU.mult), waits=[(b.S("pe"), tC), (b.S("ac"), tz1ca[pbuf]),
                                      (b.S("dzq"), tz1st[pbuf])])
                tz1 = b.op("ve", lambda e, pbuf=pbuf: e.tensor_tensor(
                    s_z1[pbuf][:, :], s_z1[pbuf][:, :], pb[5][:, 0:512], ALU.subtract))
                tz1st[pbuf] = b.op("sy", lambda e, m=m, fch=fch, pbuf=pbuf: e.dma_start(
                    out=self.d_z1.ap()[:, m * T + fch * 512: m * T + (fch + 1) * 512],
                    in_=s_z1[pbuf][:, :]), waits=[(b.S("ve"), tz1)],
                    inc=b.S("dzq"), inck=16)
                tz1ca[pbuf] = b.op("sc", lambda e, pbuf=pbuf: e.activation(
                    s_z1bt[pbuf][:, :], s_z1[pbuf][:, :], ACTF.Copy),
                    waits=[(b.S("ve"), tz1), (b.S("dg"), tz1cast[pbuf])])
                tz1cast[pbuf] = b.op("sy", lambda e, m=m, fch=fch, pbuf=pbuf: e.dma_start(
                    out=self.d_z1b.ap()[:, m * T + fch * 512: m * T + (fch + 1) * 512],
                    in_=s_z1bt[pbuf][:, :]), waits=[(b.S("ac"), tz1ca[pbuf])],
                    inc=b.S("dg"), inck=16)

    def stage_ffn(self, ffs):
        nc, b, pb = self.nc, self.b, self.pb
        SBf = lambda n, shp, dt=F32: ffs.enter_context(nc.sbuf_tensor(n, shp, dt))
        s_z1b = SBf("s_fz1b", [128, KC * 1024], BF16)
        s_wfp = [SBf(f"s_fwf{p}", [128, KC * 512], BF16) for p in range(2)]
        s_hsq = SBf("s_fhsq", [128, 8 * 1024], FP16)
        s_y2 = SBf("s_fy2", [128, KC * 1024], F32)
        s_wj = [SBf(f"s_fwj{i}", [128, 8 * 128], FP16) for i in range(2)]
        s_ysq = SBf("s_fysq", [128, 1024], F32)
        s_z2 = [SBf(f"s_fz2_{i}", [128, 1024], F32) for i in range(2)]
        self.s_rowc = SBf("s_rowc", [1, 2048], F32)

        war_wf = {0: 0, 1: 0}
        twf_pend = {}

        def load_wf(ghi, war_tok):
            par = ghi % 2
            sem = "dw" if par == 0 else "dw1"
            c0 = (ghi % 16) * 512
            twf = None
            for kc in range(KC):
                twf = b.dma("sy", sem, s_wfp[par][:, kc * 512:(kc + 1) * 512],
                            self.d_wf.ap()[kc * 128:(kc + 1) * 128, c0:c0 + 512],
                            waits=[(b.S("cc"), self.t_ag_f),
                                   (b.S("pe"), war_tok)] if kc == 0 else [])
            return twf

        for ft in range(2):
            tz = ty = None
            for kc in range(KC):
                tz = b.dma("sy", "dq", s_z1b[:, kc * 1024:(kc + 1) * 1024],
                           self.d_z1b.ap()[:, kc * T + ft * 1024: kc * T + (ft + 1) * 1024],
                           waits=[(b.S("gpc"), b.S("gpc").n), (b.S("dg"), b.S("dg").n),
                                  (b.S("pe"), b.S("pe").n)] if kc == 0 else [])
                ty = b.dma("sy", "dx", s_y2[:, kc * 1024:(kc + 1) * 1024],
                           self.d_z1.ap()[:, kc * T + ft * 1024: kc * T + (ft + 1) * 1024],
                           waits=[(b.S("ve"), b.S("ve").n)] if kc == 0 else [])
            for g in range(8):
                for hh in range(2):
                    ghi = ft * 16 + g * 2 + hh
                    if ghi == 0:
                        twf_pend[0] = load_wf(0, b.S("pe").n)
                    if ghi + 1 < 32:
                        twf_pend[ghi + 1] = load_wf(ghi + 1, war_wf[(ghi + 1) % 2])
                    twf = twf_pend.pop(ghi)
                    swf = s_wfp[ghi % 2]
                    dwn_f = "dw" if ghi % 2 == 0 else "dw1"
                    for hx in range(4):
                        hc = hh * 4 + hx
                        pp0, pp1 = pb[2 * (hc % 2)], pb[2 * (hc % 2) + 1]
                        tmm = None
                        for kc in range(KC):
                            b.op("pe", lambda e, kc=kc, hx=hx, pp0=pp0, swf=swf: e.matmul(
                                pp0[:, 0:512],
                                swf[:, kc * 512 + hx * 128: kc * 512 + (hx + 1) * 128],
                                s_z1b[:, kc * 1024: kc * 1024 + 512],
                                start=(kc == 0), stop=(kc == KC - 1)),
                                waits=[(b.S(dwn_f), twf), (b.S("dq"), tz),
                                       (b.S("ve"), b.S("ve").n)] if kc == 0 else [],
                                inc=b.S("pe"))
                            tmm = b.op("pe", lambda e, kc=kc, hx=hx, pp1=pp1, swf=swf: e.matmul(
                                pp1[:, 0:512],
                                swf[:, kc * 512 + hx * 128: kc * 512 + (hx + 1) * 128],
                                s_z1b[:, kc * 1024 + 512: kc * 1024 + 1024],
                                start=(kc == 0), stop=(kc == KC - 1)), inc=b.S("pe"))
                        for half, ppx in ((0, pp0), (1, pp1)):
                            b.op("ve", lambda e, ppx=ppx: e.tensor_scalar(
                                s_ysq[:, 0:512], ppx[:, 0:512], 0.0, None, ALU.max),
                                waits=[(b.S("pe"), tmm), (b.S("ac"), b.S("ac").n)])
                            b.op("sc", lambda e, hc=hc, half=half: e.activation(
                                s_hsq[:, hc * 1024 + half * 512: hc * 1024 + (half + 1) * 512],
                                s_ysq[:, 0:512], ACTF.Square), waits=[(b.S("ve"), b.S("ve").n)])
                    war_wf[ghi % 2] = b.S("pe").n
                for m in range(KC):
                    twj = b.dma("sy", "dw", s_wj[m % 2][:, :],
                                self.d_wj.ap()[g * 1024:(g + 1) * 1024,
                                               m * 128:(m + 1) * 128].rearrange(
                                    "(hc p) o -> p hc o", p=128),
                                waits=[(b.S("pe"), b.S("pe").n),
                                       (b.S("cc"), self.t_ag_j)])
                    pm0, pm1 = pb[4 + 2 * (m % 2)], pb[4 + 2 * (m % 2) + 1]
                    tpj = None
                    for hc in range(8):
                        b.op("pe", lambda e, hc=hc, m=m, pm0=pm0: e.matmul(
                            pm0[:, 0:512], s_wj[m % 2][:, hc * 128:(hc + 1) * 128],
                            s_hsq[:, hc * 1024: hc * 1024 + 512],
                            start=(hc == 0), stop=(hc == 7)),
                            waits=[(b.S("dw"), twj), (b.S("ac"), b.S("ac").n),
                                   (b.S("ve"), b.S("ve").n)]
                            if hc == 0 else [], inc=b.S("pe"))
                        tpj = b.op("pe", lambda e, hc=hc, m=m, pm1=pm1: e.matmul(
                            pm1[:, 0:512], s_wj[m % 2][:, hc * 128:(hc + 1) * 128],
                            s_hsq[:, hc * 1024 + 512: hc * 1024 + 1024],
                            start=(hc == 0), stop=(hc == 7)), inc=b.S("pe"))
                    b.op("ve", lambda e, m=m, pm0=pm0: e.tensor_tensor(
                        s_y2[:, m * 1024: m * 1024 + 512],
                        s_y2[:, m * 1024: m * 1024 + 512], pm0[:, 0:512], ALU.add),
                        waits=[(b.S("pe"), tpj), (b.S("dx"), ty)])
                    b.op("ve", lambda e, m=m, pm1=pm1: e.tensor_tensor(
                        s_y2[:, m * 1024 + 512: m * 1024 + 1024],
                        s_y2[:, m * 1024 + 512: m * 1024 + 1024], pm1[:, 0:512],
                        ALU.add))
            if ft == 1:
                b.op("sy", lambda e: e.dma_start(out=self.d_dbg2.ap(), in_=s_y2[:, :]),
                     waits=[(b.S("ve"), b.S("ve").n)], inc=b.S("dst"), inck=16)
                b.op("sy", lambda e: e.dma_start(out=self.d_dbg3.ap(), in_=s_hsq[:, :]),
                     waits=[(b.S("ac"), b.S("ac").n)], inc=b.S("dst"), inck=16)
            tsy = tsq = None
            for m in range(KC):
                tsy = b.op("pe", lambda e, m=m: e.matmul(
                    pb[0][0:1, 0:512], self.s_ones_col[:, :],
                    s_y2[:, m * 1024: m * 1024 + 512], start=(m == 0), stop=False if m < KC - 1 else True),
                    waits=[(b.S("ve"), b.S("ve").n)], inc=b.S("pe"))
                tsy = b.op("pe", lambda e, m=m: e.matmul(
                    pb[1][0:1, 0:512], self.s_ones_col[:, :],
                    s_y2[:, m * 1024 + 512: m * 1024 + 1024], start=(m == 0),
                    stop=False if m < KC - 1 else True), inc=b.S("pe"))
            for m in range(KC):
                ts = b.op("sc", lambda e, m=m: e.activation(
                    s_ysq[:, :], s_y2[:, m * 1024:(m + 1) * 1024], ACTF.Square),
                    waits=[(b.S("pe"), tsy if m == 0 else b.S("pe").n)])
                tsq = b.op("pe", lambda e, m=m: e.matmul(
                    pb[2][0:1, 0:512], self.s_ones_col[:, :], s_ysq[:, 0:512],
                    start=(m == 0), stop=(m == KC - 1)),
                    waits=[(b.S("ac"), ts)], inc=b.S("pe"))
                tsq = b.op("pe", lambda e, m=m: e.matmul(
                    pb[3][0:1, 0:512], self.s_ones_col[:, :], s_ysq[:, 512:1024],
                    start=(m == 0), stop=(m == KC - 1)), inc=b.S("pe"))
            for hf, pbm, pbq in ((0, pb[0], pb[2]), (1, pb[1], pb[3])):
                o = hf * 512
                b.op("ve", lambda e, pbm=pbm, o=o: e.tensor_scalar(
                    self.s_rowa[0:1, o:o + 512], pbm[0:1, 0:512], 1.0 / C, None,
                    ALU.mult), waits=[(b.S("pe"), tsq)])
                b.op("ve", lambda e, pbq=pbq, o=o: e.tensor_scalar(
                    self.s_rowb[0:1, o:o + 512], pbq[0:1, 0:512], 1.0 / C, None,
                    ALU.mult))
                b.op("ve", lambda e, o=o: e.tensor_tensor(
                    self.s_rowc[0:1, o:o + 512], self.s_rowa[0:1, o:o + 512],
                    self.s_rowa[0:1, o:o + 512], ALU.mult))
                b.op("ve", lambda e, o=o: e.tensor_tensor(
                    self.s_rowc[0:1, o:o + 512], self.s_rowb[0:1, o:o + 512],
                    self.s_rowc[0:1, o:o + 512], ALU.subtract))
                b.op("ve", lambda e, o=o: e.tensor_scalar(
                    self.s_rowc[0:1, o:o + 512], self.s_rowc[0:1, o:o + 512], 1e-5,
                    None, ALU.add))
                b.op("ve", lambda e, o=o: e.reciprocal(
                    self.s_rowb[0:1, o:o + 512], self.s_rowc[0:1, o:o + 512]))
                trr = b.op("sc", lambda e, o=o: e.activation(
                    self.s_rowc[0:1, o:o + 512], self.s_rowb[0:1, o:o + 512],
                    ACTF.Sqrt))
                b.op("ve", lambda e, o=o: e.tensor_tensor(
                    self.s_rowa[0:1, o:o + 512], self.s_rowa[0:1, o:o + 512],
                    self.s_rowc[0:1, o:o + 512], ALU.mult), waits=[(b.S("ac"), trr)])
            touts = []
            for hf in range(2):
                o = hf * 512
                tB = b.op("pe", lambda e, o=o, hf=hf: e.matmul(
                    pb[4][:, 0:512] if hf == 0 else pb[5][:, 0:512],
                    self.s_ones_row[0:1, :], self.s_rowc[0:1, o:o + 512],
                    start=True, stop=True), waits=[(b.S("ve"), b.S("ve").n)],
                    inc=b.S("pe"))
                tC = b.op("pe", lambda e, o=o, hf=hf: e.matmul(
                    pb[6][:, 0:512] if hf == 0 else pb[7][:, 0:512],
                    self.s_ones_row[0:1, :], self.s_rowa[0:1, o:o + 512],
                    start=True, stop=True), inc=b.S("pe"))
                touts.append(tC)
            tz2st = {0: 0, 1: 0}
            for m in range(KC):
                pbuf = m % 2
                for hf in range(2):
                    pB = pb[4] if hf == 0 else pb[5]
                    pC = pb[6] if hf == 0 else pb[7]
                    b.op("ve", lambda e, m=m, hf=hf, pbuf=pbuf, pB=pB: e.tensor_tensor(
                        s_z2[pbuf][:, hf * 512:(hf + 1) * 512],
                        s_y2[:, m * 1024 + hf * 512: m * 1024 + (hf + 1) * 512],
                        pB[:, 0:512], ALU.mult),
                        waits=[(b.S("pe"), touts[-1]), (b.S("dst"), tz2st[pbuf])])
                    b.op("ve", lambda e, hf=hf, pbuf=pbuf, pC=pC: e.tensor_tensor(
                        s_z2[pbuf][:, hf * 512:(hf + 1) * 512],
                        s_z2[pbuf][:, hf * 512:(hf + 1) * 512], pC[:, 0:512],
                        ALU.subtract))
                tzv = b.S("ve").n
                tz2st[pbuf] = b.op("sy", lambda e, m=m, ft=ft, pbuf=pbuf: e.dma_start(
                    out=self.out[m * 128:(m + 1) * 128, ft * 1024:(ft + 1) * 1024],
                    in_=s_z2[pbuf][:, :]), waits=[(b.S("ve"), tzv)],
                    inc=b.S("dst"), inck=16)


TAP_DEFS = {
    "t_q": ([128, H * T], BF16), "t_k": ([128, H * T], BF16),
    "t_v": ([128, H * T], BF16), "t_o": ([128, H * T], FP16),
    "t_z": ([128, H * T], BF16), "t_z1": ([128, KC * T], F32),
    "t_z1b": ([128, KC * T], BF16), "t_wq": ([C, QF], BF16),
    "t_wp": ([C, C], BF16), "t_wf": ([C, FH], BF16), "t_wj": ([FH, C], FP16),
    "t_brow": ([20, FH], F32), "t_rrow": ([20, FH], F32),
    "t_sto": ([512, H * D], F32),
    "t_dbg": ([8, 2048], F32),
    "t_dbg2": ([128, KC * 1024], F32),
    "t_dbg3": ([128, 8 * 1024], FP16),
}


def kernel(**inputs):
    in_maps = _host_shard(**inputs)
    nc = Builder(taps=TAPS).build()
    res = run_bass_kernel_spmd(nc, in_maps, core_ids=list(range(N_CORES)))
    full = np.empty((2, 8192, 2048), dtype=np.float32)
    for c in range(N_CORES):
        bb, seg = c // 4, c % 4
        full[bb, seg * T:(seg + 1) * T, :] = res.results[c]["out"].T
    if TAPS:
        kernel.tap_results = [{t: res.results[c][t] for t in TAPS}
                              for c in range(N_CORES)]
    return full


# version 32
# speedup vs baseline: 1.0136x; 1.0012x over previous
import numpy as np
import ml_dtypes
import concourse.bass as bass
import concourse.mybir as mybir
from concourse.bass_utils import run_bass_kernel_spmd
from contextlib import ExitStack

F32 = mybir.dt.float32
BF16 = mybir.dt.bfloat16
FP16 = mybir.dt.float16
ALU = mybir.AluOpType
ACTF = mybir.ActivationFunctionType
AX = mybir.AxisListType
BFD = ml_dtypes.bfloat16

MAGIC = float(np.float32(1.5 * 2 ** 23))
SCALE_ATTN = float(128 ** (-0.5))
EPS_F32 = float(np.finfo(np.float32).eps)
SHRINKS = [float(np.float32(s)) for s in np.linspace(0.4, 1.0, 16)]
ISHR = [float(np.float32(1.0 / np.float64(s))) for s in np.linspace(0.4, 1.0, 16)]

T = 2048
H = 16
D = 128
C = 2048
KC = 16
NCH = 16
QF = 6144
FH = 8192
N_CORES = 8

TAPS = ()


class Sem:
    __slots__ = ("h", "n")

    def __init__(self, h):
        self.h = h
        self.n = 0


class B:

    def __init__(self, nc, es, taps):
        self.nc = nc
        self.es = es
        self.taps = taps
        self.tap_aps = {}
        self.q = {e: [] for e in ("sy", "ve", "sc", "pe", "gp")}
        self.waited = {e: {} for e in ("sy", "ve", "sc", "pe", "gp")}
        self.sems = {}
        for name in ("dw", "dx", "dw1", "dx1", "dq", "dg", "cc", "pe", "ve", "ac",
                     "gpc", "dsh", "dst", "dzq"):
            self.sems[name] = Sem(es.enter_context(nc.semaphore(name)))
        self.selfsem = {"ve": self.sems["ve"], "sc": self.sems["ac"],
                        "gp": self.sems["gpc"]}

    def S(self, name):
        return self.sems[name]

    def op(self, eng, fn, waits=(), inc=None, inck=1):
        ws = []
        if eng in self.selfsem:
            s = self.selfsem[eng]
            if s.n > 0:
                ws.append((s, s.n))
            if inc is None:
                inc, inck = s, 1
        if eng == "sc" and self.sems["ve"].n > 0:
            ws.append((self.sems["ve"], self.sems["ve"].n))
        if eng == "gp":
            for nm in ("ve", "ac"):
                if self.sems[nm].n > 0:
                    ws.append((self.sems[nm], self.sems[nm].n))
        for (s, v) in waits:
            if s is None or v is None or v <= 0:
                continue
            ws.append((s, v))
        ws2 = []
        for (s, v) in ws:
            prev = self.waited[eng].get(id(s), 0)
            if v > prev:
                ws2.append((s, v))
                self.waited[eng][id(s)] = v
        tok = None
        if inc is not None:
            inc.n += inck
            tok = inc.n
        self.q[eng].append((fn, ws2, inc, inck))
        return tok

    def dma(self, eng, sem, dst, src, waits=()):
        return self.op(eng, lambda e, d=dst, s=src: e.dma_start(out=d, in_=s),
                       waits=waits, inc=self.S(sem), inck=16)

    def join(self):
        allsems = [(s, s.n) for s in self.sems.values() if s.n > 0]
        for eng in self.q:
            self.op(eng, lambda e: e.nop(), waits=allsems,
                    inc=None if eng in self.selfsem else None)

    def emit(self, block):
        def mk(key):
            def run(eng):
                for (fn, ws, inc, inck) in self.q[key]:
                    for (s, v) in ws:
                        eng.wait_ge(s.h, v)
                    ins = fn(eng)
                    if inc is not None:
                        if ins is None:
                            ins = eng.nop()
                        ins.then_inc(inc.h, inck)
            return run
        block.sync(mk("sy"))
        block.vector(mk("ve"))
        block.scalar(mk("sc"))
        block.tensor(mk("pe"))
        block.gpsimd(mk("gp"))


def _host_shard(L0_attn_x, L0_attn_qkv_w, L0_attn_qkv_norm_w, L0_attn_proj_w,
                L0_attn_proj_norm_w, L0_attn_norm_w, L0_attn_s, L0_ffn_w_fc,
                L0_ffn_w_proj):
    x = np.asarray(L0_attn_x, dtype=np.float32)
    wq = np.asarray(L0_attn_qkv_w, dtype=np.float32)
    wp = np.asarray(L0_attn_proj_w, dtype=np.float32)
    wf = np.asarray(L0_ffn_w_fc, dtype=np.float32)
    wj = np.asarray(L0_ffn_w_proj, dtype=np.float32)
    g1 = np.asarray(L0_attn_qkv_norm_w, dtype=np.float32).reshape(1, C)
    g2 = np.asarray(L0_attn_norm_w, dtype=np.float32).reshape(1, C)
    g3 = np.asarray(L0_attn_proj_norm_w, dtype=np.float32).reshape(1, C)
    srow = np.asarray(L0_attn_s, dtype=np.float32).reshape(1, H)
    maps = []
    for c in range(N_CORES):
        b, seg = c // 4, c % 4
        t0 = seg * T
        r0 = c * 256
        rj = c * 1024
        cv = np.zeros((1, 8), dtype=np.float32)
        for j in range(3):
            cv[0, j] = 1.0 if j < seg else 0.0
            cv[0, 3 + j] = float(seg - 1 - j) if j < seg else 0.0
        maps.append({
            "xT": np.ascontiguousarray(x[b, t0:t0 + T, :].T),
            "wq_sh": np.ascontiguousarray(wq[:, r0:r0 + 256].T),
            "wp_sh": np.ascontiguousarray(wp[:, r0:r0 + 256].T),
            "wf_sh": np.ascontiguousarray(wf[:, r0:r0 + 256].T),
            "wf_nat": np.ascontiguousarray(wf[:, r0:r0 + 256]),
            "wj_sh": np.ascontiguousarray(wj[:, rj:rj + 1024].T),
            "wj_nat": np.ascontiguousarray(wj[:, rj:rj + 1024]),
            "g1": g1.copy(), "g2": g2.copy(), "g3": g3.copy(),
            "srow": srow.copy(), "cvec": cv,
        })
    return maps


class Builder:
    def __init__(self, taps=()):
        self.taps = tuple(taps)
        nc = bass.Bass("TRN2", target_bir_lowering=False, debug=False)
        self.nc = nc
        self.xT = nc.dram_tensor("xT", [C, T], F32, kind="ExternalInput").ap()
        self.wq_sh = nc.dram_tensor("wq_sh", [256, QF], F32, kind="ExternalInput").ap()
        self.wp_sh = nc.dram_tensor("wp_sh", [256, C], F32, kind="ExternalInput").ap()
        self.wf_sh = nc.dram_tensor("wf_sh", [256, FH], F32, kind="ExternalInput").ap()
        self.wf_nat = nc.dram_tensor("wf_nat", [FH, 256], F32, kind="ExternalInput").ap()
        self.wj_sh = nc.dram_tensor("wj_sh", [1024, C], F32, kind="ExternalInput").ap()
        self.wj_nat = nc.dram_tensor("wj_nat", [C, 1024], F32, kind="ExternalInput").ap()
        self.g1 = nc.dram_tensor("g1", [1, C], F32, kind="ExternalInput").ap()
        self.g2 = nc.dram_tensor("g2", [1, C], F32, kind="ExternalInput").ap()
        self.g3 = nc.dram_tensor("g3", [1, C], F32, kind="ExternalInput").ap()
        self.srow_i = nc.dram_tensor("srow", [1, H], F32, kind="ExternalInput").ap()
        self.cvec_i = nc.dram_tensor("cvec", [1, 8], F32, kind="ExternalInput").ap()
        self.out = nc.dram_tensor("out", [C, T], F32, kind="ExternalOutput").ap()

        dr = lambda n, shp, dt=F32: nc.dram_tensor(n, shp, dt)
        self.d_sums_i = dr("d_sums_i", [1, 2]); self.d_sums_o = dr("d_sums_o", [1, 2])
        self.d_wq_s = dr("d_wq_s", [256, QF], BF16)
        self.d_wp_s = dr("d_wp_s", [256, C], BF16)
        self.d_wf_s = dr("d_wf_s", [256, FH], BF16)
        self.d_wj_s = dr("d_wj_s", [1024, C], FP16)
        self.d_wq = dr("d_wq", [C, QF], BF16)
        self.d_wp = dr("d_wp", [C, C], BF16)
        self.d_wf = dr("d_wf", [C, FH], BF16)
        self.d_wj = dr("d_wj", [FH, C], FP16)
        self.d_st_i = dr("d_st_i", [128, H * D])
        self.d_st_o = dr("d_st_o", [512, H * D])
        self.d_q = dr("d_q", [128, H * T], BF16)
        self.d_k = dr("d_k", [128, H * T], BF16)
        self.d_v = dr("d_v", [128, H * T], BF16)
        self.d_o = dr("d_o", [128, H * T], FP16)
        self.d_z = dr("d_z", [128, H * T], BF16)
        self.d_z1 = dr("d_z1", [128, KC * T])
        self.d_z1b = dr("d_z1b", [128, KC * T], BF16)
        self.d_brow = dr("d_brow", [20, FH])
        self.d_dbg = dr("d_dbg", [8, 2048])
        self.d_dbg2 = dr("d_dbg2", [128, KC * 1024])
        self.d_dbg3 = dr("d_dbg3", [128, 8 * 1024], FP16)
        self.d_rrow = dr("d_rrow", [20, FH])

        it = lambda a, n: nc.inline_tensor(np.ascontiguousarray(a), name=n)
        self.c_identb = it(np.eye(128, dtype=BFD), "c_identb")
        self.c_identf = it(np.eye(128, dtype=np.float32), "c_identf")
        ii, jj = np.meshgrid(np.arange(128), np.arange(128), indexing="ij")
        self.c_diffT = it((jj - ii).astype(np.float32), "c_diffT")
        self.c_causT = it((jj >= ii).astype(np.float32), "c_causT")
        self.c_iota16 = it(np.tile(np.arange(128, dtype=np.float32), (16, 1)), "c_iota16")
        self.c_rev = it(np.tile(128.0 - np.arange(128, dtype=np.float32), (16, 1)), "c_rev")
        self.c_ones_row = it(np.ones((1, 128), dtype=np.float32), "c_ones_row")
        self.c_ones_col = it(np.ones((128, 1), dtype=np.float32), "c_ones_col")
        lt = np.zeros((128, 32 * 128), dtype=np.float32)
        for i in range(16):
            for sl in range(2):
                c0 = (i * 2 + sl) * 128
                lt[0:64, c0 + 8 * i + 2 * sl] = 1
                lt[64:128, c0 + 8 * i + 2 * sl + 1] = 1
        self.c_ab_lhsT = it(np.asarray(lt, dtype=np.float16), "c_ab_lhsT")
        pp = np.zeros((128, 8), dtype=np.float32)
        for p in range(128):
            pp[p, p % 8] = 1
        self.c_par8 = it(pp, "c_par8")
        self.c_par8T = it(np.ascontiguousarray(pp.T), "c_par8T")
        b2 = np.zeros((2, 128), dtype=np.float32)
        b2[0, 0:64] = 1.0
        b2[1, 64:128] = 1.0
        self.c_ind2 = it(b2, "c_ind2")
        sp = np.zeros((128, 4), dtype=np.float32)
        ip_ = np.zeros((128, 1), dtype=np.float32)
        for p in range(128):
            i = p // 8
            sp[p, 0] = SHRINKS[i]
            sp[p, 1] = float(np.float32(SHRINKS[i]) * np.float32(SHRINKS[i]))
            sp[p, 2] = float(np.float32(2.0) * np.float32(SHRINKS[i]))
            sp[p, 3] = float(16 - i)
            ip_[p, 0] = ISHR[i]
        self.c_shrp = it(sp, "c_shrp")
        self.c_ishrp = it(ip_, "c_ishrp")
        rk = np.zeros((128, 512), dtype=np.float32)
        sh = np.zeros((128, 512), dtype=np.float32)
        ih = np.zeros((128, 512), dtype=np.float32)
        for col in range(512):
            i = (col % 128) // 8
            rk[:, col] = float(16 - i)
            sh[:, col] = SHRINKS[i]
            ih[:, col] = ISHR[i]
        self.c_rank512 = it(rk, "c_rank512")
        jk = np.zeros((128, 512), dtype=np.float32)
        for row in range(128):
            if row % 8 >= 4:
                jk[row, :] = 1e9
        self.c_junk512 = it(jk, "c_junk512")
        self.c_shr512 = it(sh, "c_shr512")
        self.c_ishr512 = it(ih, "c_ishr512")

    def build(self):
        nc = self.nc
        es = ExitStack()
        with es:
            b = B(nc, es, self.taps)
            self.b = b
            for tname in self.taps:
                shp, dt = TAP_DEFS[tname]
                b.tap_aps[tname] = nc.dram_tensor(tname, shp, dt,
                                                  kind="ExternalOutput").ap()
            SB = lambda n, shp, dt=F32: es.enter_context(nc.sbuf_tensor(n, shp, dt))
            self.SB = SB
            self.pb = [es.enter_context(nc.psum_tensor(f"pb{i}", [128, 512], F32))
                       for i in range(8)]

            self.s_identb = SB("s_identb", [128, 128], BF16)
            self.s_identf = SB("s_identf", [128, 128], F32)
            self.s_diffT = SB("s_diffT", [128, 128], F32)
            self.s_causT = SB("s_causT", [128, 128], F32)
            self.s_iota16 = SB("s_iota16", [16, 128], F32)
            self.s_rev16 = SB("s_rev16", [16, 128], F32)
            self.s_ones_row = SB("s_ones_row", [1, 128], F32)
            self.s_ones_col = SB("s_ones_col", [128, 1], F32)
            self.s_par8 = SB("s_par8", [128, 8], F32)
            self.s_par8T = SB("s_par8T", [8, 128], F32)
            self.s_ind2 = SB("s_ind2", [2, 128], F32)
            self.s_shrp = SB("s_shrp", [128, 4], F32)
            self.s_ishrp = SB("s_ishrp", [128, 1], F32)
            self.s_g1row = SB("s_g1row", [1, C], F32)
            self.s_srow = SB("s_srow", [1, H], F32)
            self.s_scol = SB("s_scol", [H, 1], F32)
            self.s_cvec = SB("s_cvec", [1, 8], F32)
            self.s_scolb = SB("s_scolb", [128, H], F32)
            self.s_snegb = SB("s_snegb", [128, H], F32)
            self.s_lrev = SB("s_lrev", [128, H], F32)
            self.s_rowa = SB("s_rowa", [1, 2048], F32)
            self.s_rowb = SB("s_rowb", [1, 2048], F32)
            self.s_mcol = SB("s_mcol", [128, 5], F32)

            self.stage_loads()
            with ExitStack() as tes:
                self.stage_ternary(tes)
                self.stage_prologue_s()
            with ExitStack() as qes:
                self.stage_int5(qes)
            b.join()
            with ExitStack() as aes:
                self.s_lt = aes.enter_context(nc.sbuf_tensor("s_lt", [128, H * 128], F32))
                self.s_lTf = aes.enter_context(nc.sbuf_tensor("s_lTf", [128, H * 128], F32))
                self.s_S = aes.enter_context(nc.sbuf_tensor("s_S", [128, H * D], F32))
                self.s_Sb = aes.enter_context(nc.sbuf_tensor("s_Sb", [128, H * D], FP16))
                self.s_P = aes.enter_context(nc.sbuf_tensor("s_P", [128, H * D], F32))
                self.s_sc512 = aes.enter_context(nc.sbuf_tensor("s_sc512", [128, 512], F32))
                self.stage_lambda_tiles()
                with ExitStack() as wes:
                    self.stage_qkv(wes)
                b.join()
                with ExitStack() as ses:
                    self.stage_scan(ses)
                b.join()
                self.stage_states()
                with ExitStack() as a2s:
                    self.stage_corr_rms(a2s)
                b.join()
            with ExitStack() as pjs:
                self.stage_proj_ln1(pjs)
            b.join()
            with ExitStack() as ffs:
                self.stage_ffn(ffs)
            tapsrc = {"t_q": self.d_q, "t_k": self.d_k, "t_v": self.d_v,
                      "t_o": self.d_o, "t_z": self.d_z, "t_z1": self.d_z1,
                      "t_z1b": self.d_z1b, "t_wq": self.d_wq, "t_wp": self.d_wp,
                      "t_wf": self.d_wf, "t_wj": self.d_wj, "t_brow": self.d_brow,
                      "t_rrow": self.d_rrow, "t_sto": self.d_st_o,
                      "t_dbg": self.d_dbg, "t_dbg2": self.d_dbg2,
                      "t_dbg3": self.d_dbg3}
            b.join()
            for tname in self.taps:
                b.dma("sy", "dst", b.tap_aps[tname], tapsrc[tname].ap())
            b.op("sy", lambda e: e.nop(), waits=[(b.S("dst"), b.S("dst").n)])

            block = es.enter_context(nc.Block())
            b.emit(block)
        return nc

    def stage_loads(self):
        b = self.b
        pairs = [
            (self.s_identb, self.c_identb.ap()), (self.s_identf, self.c_identf.ap()),
            (self.s_diffT, self.c_diffT.ap()), (self.s_causT, self.c_causT.ap()),
            (self.s_iota16, self.c_iota16.ap()), (self.s_rev16, self.c_rev.ap()),
            (self.s_ones_row, self.c_ones_row.ap()), (self.s_ones_col, self.c_ones_col.ap()),
            (self.s_par8, self.c_par8.ap()), (self.s_par8T, self.c_par8T.ap()),
            (self.s_ind2, self.c_ind2.ap()), (self.s_shrp, self.c_shrp.ap()),
            (self.s_ishrp, self.c_ishrp.ap()),
            (self.s_g1row, self.g1), (self.s_srow, self.srow_i), (self.s_cvec, self.cvec_i),
        ]
        t = None
        for dst, src in pairs:
            t = b.dma("sy", "dx", dst[:], src)
        self.t_consts = t
        self.t_scol = b.dma("sy", "dx", self.s_scol[:, :], self.s_srow[0:1, :],
                            waits=[(b.S("dx"), t)])
        b.op("ve", lambda e: e.memset(self.s_mcol[:, 4:5], MAGIC))

    def stage_ternary(self, qes):
        nc, b = self.nc, self.b
        SBq = lambda n, shp, dt=F32: qes.enter_context(nc.sbuf_tensor(n, shp, dt))
        s_tw = [SBq(f"s_tw{i}", [128, QF], F32) for i in range(2)]
        s_pw = [SBq(f"s_pw{i}", [128, C], F32) for i in range(2)]
        s_tq = SBq("s_tq", [128, QF], BF16)
        s_sums = SBq("s_sums", [128, 4], F32)
        t_twl = [b.dma("sy", "dw", s_tw[r][:, :], self.wq_sh[r * 128:(r + 1) * 128, :])
                 for r in range(2)]
        t_pwl = [b.dma("sy", "dw", s_pw[r][:, :], self.wp_sh[r * 128:(r + 1) * 128, :])
                 for r in range(2)]
        for r in range(2):
            b.op("ve", lambda e, r=r: e.tensor_reduce(
                s_sums[:, r:r + 1], s_tw[r][:, :], AX.X, ALU.add,
                apply_absolute_value=True), waits=[(b.S("dw"), t_twl[r])])
        for r in range(2):
            b.op("ve", lambda e, r=r: e.tensor_reduce(
                s_sums[:, 2 + r:3 + r], s_pw[r][:, :], AX.X, ALU.add,
                apply_absolute_value=True), waits=[(b.S("dw"), t_pwl[r])])
        tok_ve = b.S("ve").n
        tsum = b.op("pe", lambda e: e.matmul(
            self.pb[0][0:1, 0:4], self.s_ones_col[:, :], s_sums[:, 0:4],
            start=True, stop=True), waits=[(b.S("ve"), tok_ve), (b.S("dx"), self.t_consts)],
            inc=b.S("pe"))
        b.op("ve", lambda e: e.tensor_copy(self.s_rowa[0:1, 12:16], self.pb[0][0:1, 0:4]),
             waits=[(b.S("pe"), tsum)])
        tcmb = b.op("ve", lambda e: e.tensor_tensor(
            self.s_rowa[0:1, 0:2], self.s_rowa[0:1, 12:16:2], self.s_rowa[0:1, 13:16:2],
            ALU.add))
        tst = b.op("gp", lambda e: e.dma_start(out=self.d_sums_i.ap(),
                                               in_=self.s_rowa[0:1, 0:2]),
                   waits=[(b.S("ve"), tcmb)], inc=b.S("dg"), inck=16)
        tar = b.op("gp", lambda e: e.collective_compute(
            "AllReduce", ALU.add, replica_groups=[list(range(8))],
            ins=[self.d_sums_i.ap().opt()], outs=[self.d_sums_o.ap().opt()]),
            waits=[(b.S("dg"), tst)], inc=b.S("cc"))
        tld = b.dma("sy", "dx", self.s_rowa[0:1, 4:6], self.d_sums_o.ap(),
                    waits=[(b.S("cc"), tar)])
        b.op("ve", lambda e: e.tensor_scalar(
            self.s_rowa[0:1, 6:7], self.s_rowa[0:1, 4:5], 1.0 / (QF * C), 1e-5,
            ALU.mult, ALU.max), waits=[(b.S("dx"), tld)])
        b.op("ve", lambda e: e.tensor_scalar(
            self.s_rowa[0:1, 7:8], self.s_rowa[0:1, 5:6], 1.0 / (C * C), 1e-5,
            ALU.mult, ALU.max))
        b.op("ve", lambda e: e.reciprocal(self.s_rowa[0:1, 8:9], self.s_rowa[0:1, 6:7]))
        tow = b.op("ve", lambda e: e.reciprocal(self.s_rowa[0:1, 9:10], self.s_rowa[0:1, 7:8]))
        tmb = b.op("pe", lambda e: e.matmul(
            self.pb[1][:, 0:4], self.s_ones_row[0:1, :], self.s_rowa[0:1, 6:10],
            start=True, stop=True), waits=[(b.S("ve"), tow)], inc=b.S("pe"))
        b.op("ve", lambda e: e.tensor_copy(self.s_mcol[:, 0:4], self.pb[1][:, 0:4]),
             waits=[(b.S("pe"), tmb)])

        def tern(tile, Fw, si, mi, dst, r, tok_load):
            ta = b.op("sc", lambda e: e.activation(
                tile[:, 0:Fw], tile[:, 0:Fw], ACTF.Identity,
                bias=self.s_mcol[:, 4:5], scale=self.s_mcol[:, si:si + 1]),
                waits=[(b.S("ve"), b.S("ve").n), (b.S("dw"), tok_load)])
            b.op("ve", lambda e: e.tensor_scalar(
                tile[:, 0:Fw], tile[:, 0:Fw], MAGIC, 1.0, ALU.subtract, ALU.min),
                waits=[(b.S("ac"), ta)])
            tq = b.op("ve", lambda e: e.tensor_scalar(
                s_tq[:, 0:Fw], tile[:, 0:Fw], -1.0, self.s_mcol[:, mi:mi + 1],
                ALU.max, ALU.mult), waits=[(b.S("dsh"), b.S("dsh").n)])
            return b.op("sy", lambda e: e.dma_start(
                out=dst.ap()[r * 128:(r + 1) * 128, 0:Fw], in_=s_tq[:, 0:Fw]),
                waits=[(b.S("ve"), tq)], inc=b.S("dsh"), inck=16)

        for r in range(2):
            tq1 = tern(s_tw[r], QF, 2, 0, self.d_wq_s, r, t_twl[r])
        self.t_ag_q = b.op("gp", lambda e: e.collective_compute(
            "AllGather", ALU.bypass, replica_groups=[list(range(8))],
            ins=[self.d_wq_s.ap().opt()], outs=[self.d_wq.ap().opt()]),
            waits=[(b.S("dsh"), tq1)], inc=b.S("cc"))
        for r in range(2):
            tp1 = tern(s_pw[r], C, 3, 1, self.d_wp_s, r, t_pwl[r])
        self.t_ag_p = b.op("gp", lambda e: e.collective_compute(
            "AllGather", ALU.bypass, replica_groups=[list(range(8))],
            ins=[self.d_wp_s.ap().opt()], outs=[self.d_wp.ap().opt()]),
            waits=[(b.S("dsh"), tp1)], inc=b.S("cc"))

    def stage_prologue_s(self):
        b, pb = self.b, self.pb
        t1 = b.op("pe", lambda e: e.matmul(
            pb[2][:, 0:16], self.s_ones_row[0:1, :], self.s_srow[0:1, :],
            start=True, stop=True), waits=[(b.S("dx"), self.t_consts)], inc=b.S("pe"))
        b.op("ve", lambda e: e.tensor_copy(self.s_scolb[:, :], pb[2][:, 0:16]),
             waits=[(b.S("pe"), t1)])
        tneg = b.op("ve", lambda e: e.tensor_scalar(
            self.s_snegb[:, :], self.s_scolb[:, :], -1.0, None, ALU.mult))
        t2 = b.op("pe", lambda e: e.matmul(
            pb[2][:, 16:32], self.s_rev16[0:1, :], self.s_srow[0:1, :],
            start=True, stop=True), waits=[(b.S("ve"), tneg)], inc=b.S("pe"))
        tl1 = b.op("ve", lambda e: e.tensor_scalar(
            self.s_lrev[:, :], pb[2][:, 16:32], -1.0, None, ALU.mult),
            waits=[(b.S("pe"), t2)])
        b.op("sc", lambda e: e.activation(self.s_lrev[:, :], self.s_lrev[:, :], ACTF.Exp),
             waits=[(b.S("ve"), tl1)])

    def stage_int5(self, qes):
        nc, b, pb = self.nc, self.b, self.pb
        SBq = lambda n, shp, dt=F32: qes.enter_context(nc.sbuf_tensor(n, shp, dt))
        s_nat = SBq("s_nat", [128, 1024], F32)
        s_natb = SBq("s_natb", [128, 1024], F32)
        s_natj = [s_nat, s_natb]
        s_amf = SBq("s_amf", [128, 256], F32)
        s_amp = SBq("s_amp", [128, 256], F32)
        s_trx = SBq("s_trx", [128, 128], F32)
        s_ab = SBq("s_ab", [128, 32 * 128], FP16)
        t_ab = b.dma("sy", "dx", s_ab[:, :], self.c_ab_lhsT.ap())

        tam = None

        def amax_stream(nch, src, dst, nOc, buf_of, first):
            trds = {}
            tam_l = None
            for ch in range(nch):
                buf = buf_of(ch)
                sem = "dw" if ch % 2 == 0 else "dw1"
                if ch - 2 in trds:
                    w = [(b.S("ve"), trds[ch - 2])]
                else:
                    w = [(b.S("ve"), b.S("ve").n), (b.S("dsh"), b.S("dsh").n),
                         (b.S("ac"), b.S("ac").n)]
                tl = b.dma("sy", sem, buf, src[ch * 128:(ch + 1) * 128, :], waits=w)
                trds[ch] = tam_l = b.op("ve", lambda e, ch=ch, buf=buf: e.tensor_reduce(
                    dst[:, ch:256:nOc], buf.rearrange("p (k b) -> p k b", b=64),
                    AX.X, ALU.max, apply_absolute_value=True),
                    waits=[(b.S(sem), tl)])
            return tam_l

        amax_stream(64, self.wf_nat, s_amf, 64,
                    lambda ch: s_nat[:, (ch % 2) * 256:(ch % 2) * 256 + 256], True)
        tam = amax_stream(16, self.wj_nat, s_amp, 16,
                          lambda ch: s_natj[ch % 2][:, 0:1024], False)
        s_rbf = SBq("s_rbf", [128, 256], F32)
        s_rbp = SBq("s_rbp", [128, 256], F32)
        R15 = float(np.float32(1.0 / 15.0))
        trb = None
        for s_am, s_rb in ((s_amf, s_rbf), (s_amp, s_rbp)):
            b.op("ve", lambda e, s_am=s_am: e.tensor_scalar(
                s_am[:, :], s_am[:, :], 1e-8, R15, ALU.max, ALU.mult),
                waits=[(b.S("ve"), tam)])
            trb = b.op("ve", lambda e, s_am=s_am, s_rb=s_rb: e.reciprocal(
                s_rb[:, :], s_am[:, :]))

        def rows_out(slab, nchunks, nOc, row0, dst):
            tlast = None
            for k in range(nchunks):
                tt = b.op("pe", lambda e, k=k, slab=slab: e.transpose(
                    pb[0][:, 0:128], slab[:, k * 128:(k + 1) * 128], self.s_identf[:, :]),
                    waits=[(b.S("ve"), b.S("ve").n), (b.S("dst"), b.S("dst").n)],
                    inc=b.S("pe"))
                te = b.op("ve", lambda e: e.tensor_copy(s_trx[:, :], pb[0][:, 0:128]),
                          waits=[(b.S("pe"), tt)])
                nblk_local = 128 // nOc
                for loc in range(nblk_local):
                    blk = k * nblk_local + loc
                    tlast = b.op("sy", lambda e, loc=loc, blk=blk, dst=dst, nOc=nOc:
                        e.dma_start(
                            out=dst.ap()[row0 + blk:row0 + blk + 1, 0:nOc * 128],
                            in_=s_trx[loc * nOc:(loc + 1) * nOc, 0:128]),
                        waits=[(b.S("ve"), te)], inc=b.S("dst"), inck=16)
            return tlast
        rows_out(s_amf, 2, 64, 0, self.d_brow)
        rows_out(s_rbf, 2, 64, 0, self.d_rrow)
        rows_out(s_amp, 2, 16, 4, self.d_brow)
        t_rows = rows_out(s_rbp, 2, 16, 4, self.d_rrow)

        s_u = [SBq(f"s_u{i}", [128, 512], F32) for i in range(2)]
        s_v = [SBq(f"s_v{i}", [128, 512], F32) for i in range(2)]
        s_wtp = [[SBq(f"s_wt{p}{i}", [128, 512], F32) for i in range(2)]
                 for p in range(2)]
        s_rp = [SBq(f"s_rq{p}", [128, 512], F32) for p in range(2)]
        s_rvp = [SBq(f"s_rvq{p}", [128, 512], F32) for p in range(2)]
        s_d2p = [SBq(f"s_d2q{p}", [128, 512], F32) for p in range(2)]
        s_hip = [SBq(f"s_hiq{p}", [128, 512], FP16) for p in range(2)]
        s_lop = [SBq(f"s_loq{p}", [128, 512], FP16) for p in range(2)]
        s_r, s_rv, s_r2 = s_rp[0], s_rvp[0], s_d2p[0]
        s_brsp = [[SBq(f"s_brs{p}{i}", [2, 512], F32) for i in range(2)]
                  for p in range(2)]
        s_rrsp = [[SBq(f"s_rrs{p}{i}", [2, 512], F32) for i in range(2)]
                  for p in range(2)]
        s_sls = [SBq(f"s_sls{i}", [2, 512], F32) for i in range(2)]
        s_ils = [SBq(f"s_ils{i}", [2, 512], F32) for i in range(2)]
        s_sls2 = [SBq(f"s_sls2{i}", [2, 512], F32) for i in range(2)]
        s_ils2 = [SBq(f"s_ils2{i}", [2, 512], F32) for i in range(2)]
        s_EN = SBq("s_EN", [128, 32], F32)
        s_E = SBq("s_E", [128, 512], F32)
        s_ET = SBq("s_ET", [128, 512], F32)
        s_mn = SBq("s_mn", [128, 32], F32)
        s_W = SBq("s_W", [128, 512], F32)
        s_t1 = SBq("s_t1", [128, 512], F32)
        s_sm32 = SBq("s_sm32", [128, 32], F32)
        s_im32 = SBq("s_im32", [128, 32], F32)
        s_tr32 = SBq("s_tr32", [32, 128], F32)
        s_rank = SBq("s_rank", [128, 512], F32)
        s_junk = SBq("s_junk", [128, 512], F32)
        s_shr5 = SBq("s_shr5", [128, 512], F32)
        s_ishr5 = SBq("s_ishr5", [128, 512], F32)
        s_dq = SBq("s_dq", [128, 512], BF16)
        s_dqh = SBq("s_dqh", [128, 512], FP16)
        for dst, cst in ((s_rank, self.c_rank512), (s_shr5, self.c_shr512),
                         (s_ishr5, self.c_ishr512), (s_junk, self.c_junk512)):
            t_ab = b.dma("sy", "dx", dst[:, :], cst.ap())

        def pack(rcs, Fw, src, dst, row0, half=False):
            npk = len(rcs)
            tlast = 0

            def issue_loads(fc, war_tok):
                par = fc % 2
                dwn, dxn = ("dw", "dx") if par == 0 else ("dw1", "dx1")
                sl_f = slice(fc * 512, (fc + 1) * 512)
                tw = [b.dma("sy", dwn, s_wtp[par][sl][:, :],
                            src[rcs[sl] * 128:(rcs[sl] + 1) * 128, sl_f],
                            waits=[(b.S("ve"), war_tok)]) for sl in range(npk)]
                trr = None
                for sl in range(npk):
                    b.dma("sy", dxn, s_brsp[par][sl][:, :],
                          self.d_brow.ap()[row0 + 2 * sl:row0 + 2 * sl + 2, sl_f],
                          waits=[(b.S("dst"), t_rows), (b.S("ve"), war_tok),
                                 (b.S("pe"), b.S("pe").n)])
                    trr = b.dma("sy", dxn, s_rrsp[par][sl][:, :],
                                self.d_rrow.ap()[row0 + 2 * sl:row0 + 2 * sl + 2, sl_f])
                return tw, trr

            def chunk(fc, s_wt, s_brs, s_rrs, tw, trr, prevE):
                par = fc % 2
                dwn, dxn = ("dw", "dx") if par == 0 else ("dw1", "dx1")
                sl_f = slice(fc * 512, (fc + 1) * 512)
                tu = []
                for sl in range(npk):
                    tb = b.op("pe", lambda e, sl=sl: e.matmul(
                        pb[2][:, 0:512], self.s_ind2[0:2, :], s_rrs[sl][:, :],
                        start=True, stop=True),
                        waits=[(b.S(dxn), trr), (b.S("ve"), b.S("ve").n)], inc=b.S("pe"))
                    b.op("ve", lambda e, sl=sl: e.tensor_tensor(
                        s_u[sl][:, :], s_wt[sl][:, :], pb[2][:, 0:512], ALU.mult),
                        waits=[(b.S("pe"), tb), (b.S(dwn), tw[sl])])
                    tu.append(b.op("ve", lambda e, sl=sl: e.scalar_tensor_tensor(
                        s_v[sl][:, :], s_u[sl][:, :], -1.0, s_u[sl][:, :],
                        ALU.mult, ALU.max)))
                tA = None
                tca = {0: 0, 1: 0}
                tlo = {0: 0, 1: 0}
                for i in range(16):
                    for sl in range(npk):
                        k = i * npk + sl
                        par = k % 2
                        sr, srv = s_rp[par], s_rvp[par]
                        sd2, shi, slo = s_d2p[par], s_hip[par], s_lop[par]
                        b.op("ve", lambda e, i=i, sl=sl, sr=sr: e.tensor_scalar(
                            sr[:, :], s_v[sl][:, :], ISHR[i], 15.49, ALU.mult, ALU.min),
                            waits=[(b.S("ve"), tu[-1])])
                        b.op("ve", lambda e, sr=sr: e.tensor_scalar(
                            sr[:, :], sr[:, :], MAGIC, MAGIC, ALU.add, ALU.subtract))
                        trv = b.op("ve", lambda e, i=i, sl=sl, sr=sr, srv=srv:
                            e.scalar_tensor_tensor(
                                srv[:, :], sr[:, :], SHRINKS[i], s_v[sl][:, :],
                                ALU.mult, ALU.subtract),
                            waits=[(b.S("ac"), tca[par])])
                        tq1 = b.op("sc", lambda e, srv=srv, sd2=sd2: e.activation(
                            sd2[:, :], srv[:, :], ACTF.Square),
                            waits=[(b.S("gpc"), tlo[par])])
                        tq2 = b.op("sc", lambda e, sd2=sd2, shi=shi: e.activation(
                            shi[:, :], sd2[:, :], ACTF.Copy),
                            waits=[(b.S("pe"), b.S("pe").n)])
                        tca[par] = tq2
                        tl = b.op("gp", lambda e, sd2=sd2, shi=shi, slo=slo:
                            e.tensor_tensor(slo[:, :], sd2[:, :], shi[:, :],
                                            ALU.subtract),
                            waits=[(b.S("pe"), b.S("pe").n)])
                        tlo[par] = tl
                        ls = s_ab[:, (i * 2 + sl) * 128:(i * 2 + sl + 1) * 128]
                        first = (i == 0 and sl == 0)
                        last = (i == 15 and sl == npk - 1)
                        b.op("pe", lambda e, ls=ls, shi=shi, first=first: e.matmul(
                            pb[0][:, 0:512], ls, shi[:, :], start=first, stop=False),
                            waits=[(b.S("ac"), tq2), (b.S("dx"), t_ab),
                                   (b.S("ve"), prevE if first else 0)], inc=b.S("pe"))
                        tA = b.op("pe", lambda e, ls=ls, slo=slo, last=last: e.matmul(
                            pb[0][:, 0:512], ls, slo[:, :], start=False, stop=last),
                            waits=[(b.S("gpc"), tl)], inc=b.S("pe"))
                prevE = b.op("ve", lambda e: e.tensor_tensor(
                    s_E[:, :], pb[0][:, 0:512], s_junk[:, :], ALU.add),
                    waits=[(b.S("pe"), tA)])
                for k in range(4):
                    tt = b.op("pe", lambda e, k=k: e.transpose(
                        pb[5][:, k * 128:(k + 1) * 128], s_E[:, k * 128:(k + 1) * 128],
                        self.s_identf[:, :]),
                        waits=[(b.S("ve"), prevE)], inc=b.S("pe"))
                tev5 = b.op("ve", lambda e: e.tensor_copy(s_ET[:, :], pb[5][:, 0:512]),
                            waits=[(b.S("pe"), tt)])
                for k in range(4):
                    b.op("ve", lambda e, k=k: e.tensor_reduce(
                        s_mn[:, k * 8:(k + 1) * 8],
                        s_ET[:, k * 128:(k + 1) * 128].rearrange(
                            "p (i s) -> p s i", i=16), AX.X, ALU.min))
                for k in range(4):
                    b.op("ve", lambda e, k=k: e.tensor_tensor(
                        s_W[:, k * 128:(k + 1) * 128].rearrange("p (i s) -> p s i", i=16),
                        s_ET[:, k * 128:(k + 1) * 128].rearrange("p (i s) -> p s i", i=16),
                        s_mn[:, k * 8:(k + 1) * 8].unsqueeze(2).broadcast_to([128, 8, 16]),
                        ALU.is_equal))
                b.op("ve", lambda e: e.tensor_tensor(
                    s_W[:, :], s_W[:, :], s_rank[:, :], ALU.mult))
                for k in range(4):
                    b.op("ve", lambda e, k=k: e.tensor_reduce(
                        s_mn[:, k * 8:(k + 1) * 8],
                        s_W[:, k * 128:(k + 1) * 128].rearrange(
                            "p (i s) -> p s i", i=16), AX.X, ALU.max))
                for k in range(4):
                    b.op("ve", lambda e, k=k: e.tensor_tensor(
                        s_W[:, k * 128:(k + 1) * 128].rearrange("p (i s) -> p s i", i=16),
                        s_W[:, k * 128:(k + 1) * 128].rearrange("p (i s) -> p s i", i=16),
                        s_mn[:, k * 8:(k + 1) * 8].unsqueeze(2).broadcast_to([128, 8, 16]),
                        ALU.is_equal))
                b.op("ve", lambda e: e.tensor_tensor(
                    s_t1[:, :], s_W[:, :], s_shr5[:, :], ALU.mult))
                for k in range(4):
                    b.op("ve", lambda e, k=k: e.tensor_reduce(
                        s_sm32[:, k:32:4], s_t1[:, k * 128:(k + 1) * 128].rearrange(
                            "p (i s) -> p s i", i=16), AX.X, ALU.add))
                if not half:
                    b.op("ve", lambda e: e.tensor_tensor(
                        s_t1[:, :], s_W[:, :], s_ishr5[:, :], ALU.mult))
                    for k in range(4):
                        b.op("ve", lambda e, k=k: e.tensor_reduce(
                            s_im32[:, k:32:4], s_t1[:, k * 128:(k + 1) * 128].rearrange(
                                "p (i s) -> p s i", i=16), AX.X, ALU.add))
                tsel = b.S("ve").n
                tts = b.op("pe", lambda e: e.transpose(
                    pb[6][0:32, 0:128], s_sm32[:, 0:32], self.s_identf[:, :]),
                    waits=[(b.S("ve"), tsel)], inc=b.S("pe"))
                tevs = b.op("ve", lambda e: e.tensor_copy(s_tr32[0:32, :],
                                                          pb[6][0:32, 0:128]),
                            waits=[(b.S("pe"), tts)])
                tsd = None
                for sl in range(npk):
                    for j in range(2):
                        tsd = b.dma("sy", dxn, s_sls[sl][j:j + 1, :],
                                    s_tr32[(2 * sl + j) * 4:(2 * sl + j) * 4 + 4, :],
                                    waits=[(b.S("ve"), tevs)])
                if not half:
                    tti = b.op("pe", lambda e: e.transpose(
                        pb[6][0:32, 0:128], s_im32[:, 0:32], self.s_identf[:, :]),
                        waits=[(b.S("ve"), tevs), (b.S(dxn), tsd)], inc=b.S("pe"))
                    tevi = b.op("ve", lambda e: e.tensor_copy(s_tr32[0:32, :],
                                                              pb[6][0:32, 0:128]),
                                waits=[(b.S("pe"), tti)])
                    for sl in range(npk):
                        for j in range(2):
                            tsd = b.dma("sy", dxn, s_ils[sl][j:j + 1, :],
                                        s_tr32[(2 * sl + j) * 4:(2 * sl + j) * 4 + 4, :],
                                        waits=[(b.S("ve"), tevi)])
                if half:
                    TTo = lambda o, a, c, op: (lambda e: e.tensor_tensor(
                        o[:, :], a[:, :], c[:, :], op))
                    b.op("ve", lambda e: e.scalar_tensor_tensor(
                        s_E[:, :], s_W[:, :], 1e9, s_ET[:, :], ALU.mult, ALU.add),
                        waits=[(b.S(dxn), tsd)])
                    for k in range(4):
                        b.op("ve", lambda e, k=k: e.tensor_reduce(
                            s_mn[:, k * 8:(k + 1) * 8],
                            s_E[:, k * 128:(k + 1) * 128].rearrange(
                                "p (i s) -> p s i", i=16), AX.X, ALU.min))
                    for k in range(4):
                        b.op("ve", lambda e, k=k: e.tensor_tensor(
                            s_W[:, k * 128:(k + 1) * 128].rearrange("p (i s) -> p s i", i=16),
                            s_E[:, k * 128:(k + 1) * 128].rearrange("p (i s) -> p s i", i=16),
                            s_mn[:, k * 8:(k + 1) * 8].unsqueeze(2).broadcast_to([128, 8, 16]),
                            ALU.is_equal))
                    b.op("ve", lambda e: e.tensor_tensor(
                        s_W[:, :], s_W[:, :], s_rank[:, :], ALU.mult))
                    for k in range(4):
                        b.op("ve", lambda e, k=k: e.tensor_reduce(
                            s_mn[:, k * 8:(k + 1) * 8],
                            s_W[:, k * 128:(k + 1) * 128].rearrange(
                                "p (i s) -> p s i", i=16), AX.X, ALU.max))
                    for k in range(4):
                        b.op("ve", lambda e, k=k: e.tensor_tensor(
                            s_W[:, k * 128:(k + 1) * 128].rearrange("p (i s) -> p s i", i=16),
                            s_W[:, k * 128:(k + 1) * 128].rearrange("p (i s) -> p s i", i=16),
                            s_mn[:, k * 8:(k + 1) * 8].unsqueeze(2).broadcast_to([128, 8, 16]),
                            ALU.is_equal))
                    b.op("ve", lambda e: e.tensor_tensor(
                        s_t1[:, :], s_W[:, :], s_shr5[:, :], ALU.mult))
                    for k in range(4):
                        b.op("ve", lambda e, k=k: e.tensor_reduce(
                            s_sm32[:, k:32:4], s_t1[:, k * 128:(k + 1) * 128].rearrange(
                                "p (i s) -> p s i", i=16), AX.X, ALU.add))
                    t2sel = b.S("ve").n
                    t2ts = b.op("pe", lambda e: e.transpose(
                        pb[6][0:32, 0:128], s_sm32[:, 0:32], self.s_identf[:, :]),
                        waits=[(b.S("ve"), t2sel), (b.S(dxn), tsd)], inc=b.S("pe"))
                    t2ev = b.op("ve", lambda e: e.tensor_copy(
                        s_tr32[0:32, :], pb[6][0:32, 0:128]), waits=[(b.S("pe"), t2ts)])
                    for sl in range(npk):
                        for j in range(2):
                            tsd = b.dma("sy", dxn, s_sls2[sl][j:j + 1, :],
                                        s_tr32[(2 * sl + j) * 4:(2 * sl + j) * 4 + 4, :],
                                        waits=[(b.S("ve"), t2ev)])
                    for sl in range(npk):
                        b.op("ve", TTo(s_rrs[sl], s_sls[sl], s_sls2[sl], ALU.min),
                             waits=[(b.S(dxn), tsd), (b.S("pe"), b.S("pe").n)])
                        b.op("ve", TTo(s_sls2[sl], s_sls[sl], s_sls2[sl], ALU.max))
                        b.op("ve", lambda e, sl=sl: e.tensor_copy(
                            s_sls[sl][:, :], s_rrs[sl][:, :]))
                    for c, sel_r in ((0, s_sls), (1, s_sls2)):
                        for sl in range(npk):
                            b.op("ve", TTo(s_rrs[sl], s_brs[sl], sel_r[sl], ALU.mult))
                            tbc1 = b.op("pe", lambda e, sl=sl: e.matmul(
                                pb[3][:, 0:512], self.s_ind2[0:2, :], s_rrs[sl][:, :],
                                start=True, stop=True),
                                waits=[(b.S("ve"), b.S("ve").n)], inc=b.S("pe"))
                            b.op("ve", lambda e: e.tensor_copy(
                                s_E[:, :], pb[3][:, 0:512]), waits=[(b.S("pe"), tbc1)])
                            b.op("ve", lambda e: e.reciprocal(s_ET[:, :], s_E[:, :]))
                            b.op("ve", lambda e, sl=sl: e.tensor_tensor(
                                s_r[:, :], s_wt[sl][:, :], s_ET[:, :], ALU.mult))
                            b.op("ve", lambda e: e.tensor_scalar(
                                s_r[:, :], s_r[:, :], MAGIC, MAGIC, ALU.add, ALU.subtract))
                            b.op("ve", lambda e: e.tensor_scalar(
                                s_r[:, :], s_r[:, :], 15.0, -15.0, ALU.min, ALU.max))
                            b.op("ve", lambda e: e.tensor_tensor(
                                s_r2[:, :], s_r[:, :], s_E[:, :], ALU.mult))
                            b.op("ve", lambda e, sl=sl: e.tensor_tensor(
                                s_r2[:, :], s_r2[:, :], s_wt[sl][:, :], ALU.subtract))
                            td2 = b.op("ve", lambda e: e.tensor_tensor(
                                s_r2[:, :], s_r2[:, :], s_r2[:, :], ALU.mult))
                            ttrs = []
                            for q in range(4):
                                ttrs.append(b.op("pe", lambda e, q=q: e.transpose(
                                    pb[5][:, q * 128:(q + 1) * 128],
                                    s_r2[:, q * 128:(q + 1) * 128], self.s_identf[:, :]),
                                    waits=[(b.S("ve"), td2)], inc=b.S("pe")))
                            for q in range(4):
                                c0 = c * 16 + 2 * sl * 4 + q
                                b.op("ve", lambda e, q=q, c0=c0: e.tensor_reduce(
                                    s_EN[:, c0:c0 + 5:4],
                                    pb[5][:, q * 128:(q + 1) * 128].rearrange(
                                        "p (j f) -> p j f", j=2), AX.X, ALU.add),
                                    waits=[(b.S("pe"), ttrs[3])])
                    b.op("ve", lambda e: e.tensor_tensor(
                        s_sm32[:, 0:16], s_EN[:, 0:16], s_EN[:, 16:32], ALU.min))
                    tww = b.op("ve", lambda e: e.tensor_tensor(
                        s_sm32[:, 0:16], s_sm32[:, 0:16], s_EN[:, 0:16], ALU.not_equal))
                    tw2 = b.op("pe", lambda e: e.transpose(
                        pb[6][0:16, 0:128], s_sm32[:, 0:16], self.s_identf[:, :]),
                        waits=[(b.S("ve"), tww), (b.S(dxn), b.S(dxn).n)],
                        inc=b.S("pe"))
                    tew = b.op("ve", lambda e: e.tensor_copy(
                        s_tr32[0:16, :], pb[6][0:16, 0:128]), waits=[(b.S("pe"), tw2)])
                    for sl in range(npk):
                        for j in range(2):
                            tsd = b.dma("sy", dxn, s_ils[sl][j:j + 1, :],
                                        s_tr32[(2 * sl + j) * 4:(2 * sl + j) * 4 + 4, :],
                                        waits=[(b.S("ve"), tew)])
                    for sl in range(npk):
                        b.op("ve", TTo(s_rrs[sl], s_sls2[sl], s_sls[sl], ALU.subtract),
                             waits=[(b.S(dxn), tsd)])
                        b.op("ve", TTo(s_rrs[sl], s_rrs[sl], s_ils[sl], ALU.mult))
                        b.op("ve", TTo(s_sls[sl], s_sls[sl], s_rrs[sl], ALU.add))
                for sl in range(npk):
                    tm2 = b.op("ve", lambda e, sl=sl: e.tensor_tensor(
                        s_rrs[sl][:, :], s_brs[sl][:, :], s_sls[sl][:, :], ALU.mult),
                        waits=[(b.S(dxn), tsd)])
                    tsb = b.op("pe", lambda e, sl=sl: e.matmul(
                        pb[3][:, 0:512], self.s_ind2[0:2, :], s_rrs[sl][:, :],
                        start=True, stop=True),
                        waits=[(b.S("ve"), b.S("ve").n)], inc=b.S("pe"))
                    dq_t = s_dqh if half else s_dq
                    if not half:
                        tib = b.op("pe", lambda e, sl=sl: e.matmul(
                            pb[2][:, 0:512], self.s_ind2[0:2, :], s_ils[sl][:, :],
                            start=True, stop=True),
                            waits=[(b.S("ve"), b.S("ve").n)], inc=b.S("pe"))
                        b.op("ve", lambda e, sl=sl: e.tensor_tensor(
                            s_t1[:, :], s_u[sl][:, :], pb[2][:, 0:512], ALU.mult),
                            waits=[(b.S("pe"), tib)])
                        b.op("ve", lambda e: e.tensor_scalar(
                            s_t1[:, :], s_t1[:, :], MAGIC, MAGIC, ALU.add, ALU.subtract))
                        b.op("ve", lambda e: e.tensor_scalar(
                            s_t1[:, :], s_t1[:, :], 15.0, -15.0, ALU.min, ALU.max))
                        tdq = b.op("ve", lambda e, dq_t=dq_t: e.tensor_tensor(
                            dq_t[:, :], s_t1[:, :], pb[3][:, 0:512], ALU.mult),
                            waits=[(b.S("pe"), tsb), (b.S("dsh"), b.S("dsh").n)])
                    else:
                        TT = lambda o, a, c, op: (lambda e: e.tensor_tensor(
                            o[:, :], a[:, :], c[:, :], op))
                        TS = lambda o, a, s1, s2, op1, op2=None: (
                            (lambda e: e.tensor_scalar(o[:, :], a[:, :], s1, s2,
                                                       op1, op2)) if op2 is not None
                            else (lambda e: e.tensor_scalar(o[:, :], a[:, :], s1,
                                                            s2, op1)))
                        b.op("ve", lambda e: e.tensor_copy(s_E[:, :], pb[3][:, 0:512]),
                             waits=[(b.S("pe"), tsb)])
                        b.op("ve", lambda e: e.reciprocal(s_ET[:, :], s_E[:, :]))
                        b.op("ve", TT(s_r2, s_ET, s_E, ALU.mult))
                        b.op("ve", TS(s_t1, s_E, 4097.0, None, ALU.mult))
                        b.op("ve", TT(s_u[0], s_t1, s_E, ALU.subtract))
                        b.op("ve", TT(s_rv, s_t1, s_u[0], ALU.subtract))
                        b.op("ve", TT(s_W, s_E, s_rv, ALU.subtract))
                        b.op("ve", TS(s_t1, s_ET, 4097.0, None, ALU.mult))
                        b.op("ve", TT(s_u[0], s_t1, s_ET, ALU.subtract))
                        b.op("ve", TT(s_u[1], s_t1, s_u[0], ALU.subtract))
                        b.op("ve", TT(s_u[0], s_ET, s_u[1], ALU.subtract))
                        b.op("ve", TT(s_v[0], s_u[1], s_rv, ALU.mult))
                        b.op("ve", TT(s_v[0], s_v[0], s_r2, ALU.subtract))
                        b.op("ve", TT(s_v[1], s_u[1], s_W, ALU.mult))
                        b.op("ve", TT(s_v[0], s_v[0], s_v[1], ALU.add))
                        b.op("ve", TT(s_v[1], s_u[0], s_rv, ALU.mult))
                        b.op("ve", TT(s_v[0], s_v[0], s_v[1], ALU.add))
                        b.op("ve", TT(s_v[1], s_u[0], s_W, ALU.mult))
                        b.op("ve", TT(s_v[0], s_v[0], s_v[1], ALU.add))
                        b.op("ve", TS(s_v[1], s_r2, -1.0, 1.0, ALU.mult, ALU.add))
                        b.op("ve", TT(s_v[1], s_v[1], s_v[0], ALU.subtract))
                        b.op("ve", TT(s_v[1], s_v[1], s_ET, ALU.mult))
                        b.op("ve", TT(s_ET, s_ET, s_v[1], ALU.add))
                        b.op("ve", TT(s_r, s_wt[sl], s_ET, ALU.mult))
                        b.op("ve", TS(s_r, s_r, MAGIC, MAGIC, ALU.add, ALU.subtract))
                        b.op("ve", TS(s_r, s_r, 15.0, -15.0, ALU.min, ALU.max))
                        tdq = b.op("ve", lambda e, dq_t=dq_t: e.tensor_tensor(
                            dq_t[:, :], s_r[:, :], s_E[:, :], ALU.mult),
                            waits=[(b.S("dsh"), b.S("dsh").n)])
                    rc = rcs[sl]
                    tlast = b.op("sy", lambda e, rc=rc, sl_f=sl_f, dq_t=dq_t: e.dma_start(
                        out=dst.ap()[rc * 128:(rc + 1) * 128, sl_f], in_=dq_t[:, :]),
                        waits=[(b.S("ve"), tdq)], inc=b.S("dsh"), inck=16)
                return tlast, prevE

            nfc = Fw // 512
            war = {0: b.S("ve").n, 1: b.S("ve").n}
            loads = {0: issue_loads(0, b.S("ve").n)}
            prevE = 0
            for fc in range(nfc):
                par = fc % 2
                if fc + 1 < nfc:
                    loads[fc + 1] = issue_loads(fc + 1, war[(fc + 1) % 2])
                tw, trr = loads.pop(fc)
                tlast, prevE = chunk(fc, s_wtp[par], s_brsp[par], s_rrsp[par],
                                     tw, trr, prevE)
                war[par] = b.S("ve").n
            return tlast

        t_fc = pack([0, 1], FH, self.wf_sh, self.d_wf_s, 0)
        self.t_ag_f = b.op("gp", lambda e: e.collective_compute(
            "AllGather", ALU.bypass, replica_groups=[list(range(8))],
            ins=[self.d_wf_s.ap().opt()], outs=[self.d_wf.ap().opt()]),
            waits=[(b.S("dsh"), t_fc)], inc=b.S("cc"))
        pack([0, 1], C, self.wj_sh, self.d_wj_s, 4, half=True)
        pack([2, 3], C, self.wj_sh, self.d_wj_s, 8, half=True)
        pack([4, 5], C, self.wj_sh, self.d_wj_s, 12, half=True)
        t_pj = pack([6, 7], C, self.wj_sh, self.d_wj_s, 16, half=True)
        self.t_ag_j = b.op("gp", lambda e: e.collective_compute(
            "AllGather", ALU.bypass, replica_groups=[list(range(8))],
            ins=[self.d_wj_s.ap().opt()], outs=[self.d_wj.ap().opt()]),
            waits=[(b.S("dsh"), t_pj)], inc=b.S("cc"))

    def stage_lambda_tiles(self):
        nc, b, pb = self.nc, self.b, self.pb
        s_l16 = self.s_rev16
        tb1 = b.op("ve", lambda e: e.tensor_scalar(
            s_l16[:, :], self.s_iota16[:, :], self.s_scol[0:16, 0:1], -1.0,
            ALU.mult, ALU.mult), waits=[(b.S("dx"), self.t_scol)])
        tb2 = b.op("sc", lambda e: e.activation(s_l16[:, :], s_l16[:, :], ACTF.Exp),
                   waits=[(b.S("ve"), tb1)])
        tb3 = b.dma("sy", "dx", self.s_rowb[0:1, 0:2048], s_l16[0:16, :],
                    waits=[(b.S("ac"), tb2)])
        for i in range(4):
            tg = b.op("pe", lambda e, i=i: e.matmul(
                pb[5][:, 0:512], self.s_ones_row[0:1, :],
                self.s_rowb[0:1, i * 512:(i + 1) * 512], start=True, stop=True),
                waits=[(b.S("dx"), tb3), (b.S("ve"), b.S("ve").n)], inc=b.S("pe"))
            b.op("ve", lambda e, i=i: e.tensor_copy(
                self.s_lt[:, i * 512:(i + 1) * 512], pb[5][:, 0:512]),
                waits=[(b.S("pe"), tg)])
        tc1 = b.op("ve", lambda e: e.tensor_scalar(
            self.s_rowa[0:1, 0:16], self.s_srow[0:1, :], -128.0, None, ALU.mult))
        tc2 = b.op("sc", lambda e: e.activation(
            self.s_rowa[0:1, 0:16], self.s_rowa[0:1, 0:16], ACTF.Exp),
            waits=[(b.S("ve"), tc1)])
        tc3 = b.op("ve", lambda e: e.tensor_copy(
            self.s_rowb[0:1, 0:2048].rearrange("o (h t) -> o h t", h=16),
            self.s_rowa[0:1, 0:16].unsqueeze(2).broadcast_to([1, 16, 128])),
            waits=[(b.S("ac"), tc2)])
        for i in range(4):
            tg = b.op("pe", lambda e, i=i: e.matmul(
                pb[5][:, 0:512], self.s_ones_row[0:1, :],
                self.s_rowb[0:1, i * 512:(i + 1) * 512], start=True, stop=True),
                waits=[(b.S("ve"), b.S("ve").n)], inc=b.S("pe"))
            b.op("ve", lambda e, i=i: e.tensor_copy(
                self.s_lTf[:, i * 512:(i + 1) * 512], pb[5][:, 0:512]),
                waits=[(b.S("pe"), tg)])
        b.op("ve", lambda e: e.memset(self.s_S[:, :], 0.0))
        b.op("ve", lambda e: e.memset(self.s_Sb[:, :], 0.0))

    def stage_qkv(self, wes):
        nc, b, pb = self.nc, self.b, self.pb
        SBw = lambda n, shp, dt=F32: wes.enter_context(nc.sbuf_tensor(n, shp, dt))
        s_h = SBw("s_h", [128, KC * T], BF16)
        s_wres = SBw("s_wres", [128, KC * T], BF16)
        s_x = [SBw(f"s_x{i}", [128, 512], F32) for i in range(2)]
        s_xsq = SBw("s_xsq", [128, 512], F32)
        s_rr0 = self.s_rowb
        s_ev = [SBw(f"s_ev{i}", [128, 512], BF16) for i in range(2)]

        tstat = []
        for q4 in range(4):
            tq = None
            for kc in range(KC):
                tx = b.dma("sy", "dx", s_x[kc % 2][:, :],
                           self.xT[kc * 128:(kc + 1) * 128, q4 * 512:(q4 + 1) * 512],
                           waits=[(b.S("ac"), b.S("ac").n), (b.S("pe"), b.S("pe").n)]
                           if kc >= 2 else [])
                ts = b.op("sc", lambda e, kc=kc: e.activation(
                    s_xsq[:, :], s_x[kc % 2][:, :], ACTF.Square),
                    waits=[(b.S("dx"), tx), (b.S("pe"), b.S("pe").n)])
                tq = b.op("pe", lambda e, kc=kc, q4=q4: e.matmul(
                    pb[6][0:1, 0:512], self.s_ones_col[:, :], s_xsq[:, :],
                    start=(kc == 0), stop=(kc == KC - 1)),
                    waits=[(b.S("ac"), ts)] + ([(b.S("ve"), b.S("ve").n)]
                                               if kc == 0 else []), inc=b.S("pe"))
            tr1 = b.op("ve", lambda e, q4=q4: e.tensor_scalar(
                self.s_rowa[0:1, 0:512], pb[6][0:1, 0:512], 1.0 / C, EPS_F32,
                ALU.mult, ALU.add), waits=[(b.S("pe"), tq)])
            b.op("ve", lambda e: e.reciprocal(self.s_rowa[0:1, 512:1024],
                                              self.s_rowa[0:1, 0:512]))
            tstat.append(b.op("sc", lambda e, q4=q4: e.activation(
                s_rr0[0:1, q4 * 512:(q4 + 1) * 512], self.s_rowa[0:1, 512:1024],
                ACTF.Sqrt), waits=[(b.S("ve"), b.S("ve").n)]))

        for kc in range(KC):
            for q4 in range(4):
                tx = b.dma("sy", "dx", s_x[q4 % 2][:, :],
                           self.xT[kc * 128:(kc + 1) * 128, q4 * 512:(q4 + 1) * 512],
                           waits=[(b.S("ve"), b.S("ve").n), (b.S("ac"), b.S("ac").n)])
                tout = b.op("pe", lambda e, kc=kc, q4=q4: e.matmul(
                    pb[7][:, 0:512], self.s_g1row[0:1, kc * 128:(kc + 1) * 128],
                    s_rr0[0:1, q4 * 512:(q4 + 1) * 512], start=True, stop=True),
                    waits=[(b.S("ac"), tstat[-1]), (b.S("ve"), b.S("ve").n)],
                    inc=b.S("pe"))
                b.op("ve", lambda e, kc=kc, q4=q4: e.tensor_tensor(
                    s_h[:, kc * T + q4 * 512: kc * T + (q4 + 1) * 512],
                    s_x[q4 % 2][:, :], pb[7][:, 0:512], ALU.mult),
                    waits=[(b.S("pe"), tout), (b.S("dx"), tx)])

        tev_st = {0: 0, 1: 0}

        def proj_phase(col0, dst):
            tw = None
            for kc in range(KC):
                tw = b.dma("sy", "dw", s_wres[:, kc * T:(kc + 1) * T],
                           self.d_wq.ap()[kc * 128:(kc + 1) * 128, col0:col0 + C],
                           waits=[(b.S("cc"), self.t_ag_q), (b.S("pe"), b.S("pe").n)])
            for m in range(16):
                pset = [pb[(m % 2) * 4 + i] for i in range(4)]
                tmm = None
                for kc in range(KC):
                    for fch in range(4):
                        tmm = b.op("pe", lambda e, m=m, kc=kc, fch=fch, pset=pset:
                            e.matmul(
                                pset[fch][:, 0:512],
                                s_wres[:, kc * T + m * 128: kc * T + (m + 1) * 128],
                                s_h[:, kc * T + fch * 512: kc * T + (fch + 1) * 512],
                                start=(kc == 0), stop=(kc == KC - 1)),
                            waits=[(b.S("dw"), tw), (b.S("ve"), b.S("ve").n)]
                            if kc == 0 and fch == 0 else [], inc=b.S("pe"))
                for fch in range(4):
                    tev = b.op("ve", lambda e, fch=fch, pset=pset: e.tensor_copy(
                        s_ev[fch % 2][:, :], pset[fch][:, 0:512]),
                        waits=[(b.S("pe"), tmm), (b.S("dzq"), tev_st[fch % 2])])
                    tev_st[fch % 2] = b.op("sy", lambda e, m=m, fch=fch: e.dma_start(
                        out=dst.ap()[:, m * T + fch * 512: m * T + (fch + 1) * 512],
                        in_=s_ev[fch % 2][:, :]),
                        waits=[(b.S("ve"), tev)], inc=b.S("dzq"), inck=16)
        proj_phase(0, self.d_q)
        proj_phase(C, self.d_k)
        proj_phase(2 * C, self.d_v)

    def stage_scan(self, ses):
        nc, b, pb = self.nc, self.b, self.pb
        SBs = lambda n, shp, dt=BF16: ses.enter_context(nc.sbuf_tensor(n, shp, dt))
        s_maskT = ses.enter_context(nc.sbuf_tensor("s_maskT", [128, H * 128], F32))
        for h in range(H):
            ta = b.op("ve", lambda e, h=h: e.tensor_scalar(
                s_maskT[:, h * 128:(h + 1) * 128], self.s_diffT[:, :],
                self.s_snegb[:, h:h + 1], 88.0, ALU.mult, ALU.min))
            tb2 = b.op("sc", lambda e, h=h: e.activation(
                s_maskT[:, h * 128:(h + 1) * 128], s_maskT[:, h * 128:(h + 1) * 128],
                ACTF.Exp), waits=[(b.S("ve"), ta)])
            b.op("ve", lambda e, h=h: e.tensor_tensor(
                s_maskT[:, h * 128:(h + 1) * 128], s_maskT[:, h * 128:(h + 1) * 128],
                self.s_causT[:, :], ALU.mult), waits=[(b.S("ac"), tb2)])

        s_qc = [SBs(f"s_qc{i}", [128, 2048]) for i in range(2)]
        s_kc = [SBs(f"s_kc{i}", [128, 2048]) for i in range(2)]
        s_vc = [SBs(f"s_vc{i}", [128, 2048]) for i in range(2)]
        s_qt = ses.enter_context(nc.sbuf_tensor("s_qt", [128, 2048], FP16))
        s_kt = ses.enter_context(nc.sbuf_tensor("s_kt", [128, 512], FP16))
        s_vt = ses.enter_context(nc.sbuf_tensor("s_vt", [128, 2048], FP16))
        s_sm = ses.enter_context(nc.sbuf_tensor("s_sm", [128, 512], FP16))
        s_oc = ses.enter_context(nc.sbuf_tensor("s_oc", [128, 2048], FP16))
        tprev_o = 0
        to_prev = 0

        def scan_loads(c, war_ve, war_pe):
            bufc = c % 2
            sem = "dq" if bufc == 0 else "dg"
            ap3 = lambda d: d.ap()[:, :].rearrange("p (h t) -> p h t", h=H)[
                :, :, c * 128:(c + 1) * 128]
            tq = b.dma("sy", sem, s_qc[bufc][:, :].rearrange("p (h t) -> p h t", t=128),
                       ap3(self.d_q), waits=[(b.S("ve"), war_ve), (b.S("pe"), war_pe)])
            tk = b.dma("sy", sem, s_kc[bufc][:, :].rearrange("p (h t) -> p h t", t=128),
                       ap3(self.d_k))
            tv = b.dma("sy", sem, s_vc[bufc][:, :].rearrange("p (h t) -> p h t", t=128),
                       ap3(self.d_v))
            return sem, tq, tk, tv

        sc_pend = {0: scan_loads(0, b.S("ve").n, b.S("pe").n)}
        sc_war = {0: (0, 0), 1: (0, 0)}
        for c in range(NCH):
            bufc = c % 2
            if c + 1 < NCH:
                sc_pend[c + 1] = scan_loads(c + 1, *sc_war[(c + 1) % 2])
            dqn, tq, tk, tv = sc_pend.pop(c)
            tqt = b.op("ve", lambda e, bufc=bufc: e.tensor_tensor(
                s_qt[:, :], s_qc[bufc][:, :], self.s_lt[:, :], ALU.mult),
                waits=[(b.S(dqn), tq), (b.S("pe"), b.S("pe").n)])
            for g4 in range(4):
                pv = pb[g4 % 2][:, 0:512].bitcast(BF16)
                for hh in range(4):
                    h = g4 * 4 + hh
                    tt = b.op("pe", lambda e, h=h, hh=hh, bufc=bufc, pv=pv: e.transpose(
                        pv[:, hh * 128:(hh + 1) * 128],
                        s_vc[bufc][:, h * 128:(h + 1) * 128], self.s_identb[:, :]),
                        waits=[(b.S(dqn), tv), (b.S("ve"), b.S("ve").n)], inc=b.S("pe"))
                tve = b.op("ve", lambda e, g4=g4, pv=pv: e.tensor_copy(
                    s_vt[:, g4 * 512:(g4 + 1) * 512], pv[:, 0:512]),
                    waits=[(b.S("pe"), tt)])
            for g4 in range(4):
                pkv = pb[2][:, 0:512].bitcast(BF16)
                for hh in range(4):
                    h = g4 * 4 + hh
                    ttk = b.op("pe", lambda e, h=h, hh=hh, bufc=bufc, pkv=pkv: e.transpose(
                        pkv[:, hh * 128:(hh + 1) * 128],
                        s_kc[bufc][:, h * 128:(h + 1) * 128], self.s_identb[:, :]),
                        waits=[(b.S(dqn), tk), (b.S("ve"), b.S("ve").n)], inc=b.S("pe"))
                for hh in range(4):
                    h = g4 * 4 + hh
                    b.op("ve", lambda e, h=h, hh=hh, pkv=pkv: e.tensor_scalar(
                        s_kt[:, hh * 128:(hh + 1) * 128], pkv[:, hh * 128:(hh + 1) * 128],
                        self.s_lrev[:, h:h + 1], None, ALU.mult),
                        waits=[(b.S("pe"), ttk)])
                tkt = b.S("ve").n
                for hh in range(4):
                    h = g4 * 4 + hh
                    tsc = b.op("pe", lambda e, h=h, hh=hh, bufc=bufc: e.matmul(
                        pb[3][:, hh * 128:(hh + 1) * 128],
                        s_kc[bufc][:, h * 128:(h + 1) * 128],
                        s_qc[bufc][:, h * 128:(h + 1) * 128],
                        start=True, stop=True),
                        waits=[(b.S("ve"), tkt)], inc=b.S("pe"))
                tms = b.op("ve", lambda e, g4=g4: e.tensor_tensor(
                    s_sm[:, :], pb[3][:, 0:512], s_maskT[:, g4 * 512:(g4 + 1) * 512],
                    ALU.mult), waits=[(b.S("pe"), tsc)])
                po = pb[4 + (g4 % 2)]
                for hh in range(4):
                    h = g4 * 4 + hh
                    tin = b.op("pe", lambda e, h=h, hh=hh, po=po: e.matmul(
                        po[:, hh * 128:(hh + 1) * 128],
                        s_vt[:, (h // 4) * 512 + (h % 4) * 128:
                             (h // 4) * 512 + (h % 4) * 128 + 128],
                        s_sm[:, hh * 128:(hh + 1) * 128], start=True, stop=False),
                        waits=[(b.S("ve"), tms)] + ([(b.S("ac"), b.S("ac").n)]
                                                    if hh == 0 else []),
                        inc=b.S("pe"))
                    tcx = b.op("pe", lambda e, h=h, hh=hh, po=po: e.matmul(
                        po[:, hh * 128:(hh + 1) * 128],
                        self.s_Sb[:, h * 128:(h + 1) * 128],
                        s_qt[:, h * 128:(h + 1) * 128], start=False, stop=True),
                        waits=[(b.S("ve"), tqt), (b.S("ve"), tprev_o)], inc=b.S("pe"))
                tev = b.op("sc", lambda e, g4=g4, po=po: e.activation(
                    s_oc[:, g4 * 512:(g4 + 1) * 512], po[:, 0:512], ACTF.Identity,
                    scale=SCALE_ATTN),
                    waits=[(b.S("pe"), tcx), (b.S("dzq"), to_prev)])
                pk = pb[6 + (g4 % 2)]
                for hh in range(4):
                    h = g4 * 4 + hh
                    tkv = b.op("pe", lambda e, h=h, hh=hh, pk=pk: e.matmul(
                        pk[:, hh * 128:(hh + 1) * 128],
                        s_kt[:, hh * 128:(hh + 1) * 128],
                        s_vt[:, (h // 4) * 512 + (h % 4) * 128:
                             (h // 4) * 512 + (h % 4) * 128 + 128],
                        start=True, stop=True),
                        waits=[(b.S("ve"), b.S("ve").n)] if hh == 0 else [],
                        inc=b.S("pe"))
                sl = slice(g4 * 512, (g4 + 1) * 512)
                b.op("ve", lambda e, sl=sl, pk=pk: e.tensor_tensor(
                    self.s_S[:, sl], self.s_S[:, sl], self.s_lTf[:, sl], ALU.mult),
                    waits=[(b.S("pe"), tkv)])
                b.op("ve", lambda e, sl=sl, pk=pk: e.tensor_tensor(
                    self.s_S[:, sl], self.s_S[:, sl], pk[:, 0:512], ALU.add))
                tprev_o = b.op("ve", lambda e, sl=sl: e.tensor_copy(
                    self.s_Sb[:, sl], self.s_S[:, sl]))
            to_prev = b.op("sy", lambda e, c=c: e.dma_start(
                out=self.d_o.ap()[:, :].rearrange("p (h t) -> p h t", h=H)[
                    :, :, c * 128:(c + 1) * 128],
                in_=s_oc[:, :].rearrange("p (h t) -> p h t", t=128)),
                waits=[(b.S("ac"), b.S("ac").n)], inc=b.S("dzq"), inck=16)
            sc_war[c % 2] = (b.S("ve").n, b.S("pe").n)

    def stage_states(self):
        nc, b, pb = self.nc, self.b, self.pb
        tso = b.op("gp", lambda e: e.dma_start(out=self.d_st_i.ap(), in_=self.s_S[:, :]),
                   waits=[(b.S("ve"), b.S("ve").n)], inc=b.S("dg"), inck=16)
        tag = b.op("gp", lambda e: e.collective_compute(
            "AllGather", ALU.bypass,
            replica_groups=[[0, 1, 2, 3], [4, 5, 6, 7]],
            ins=[self.d_st_i.ap().opt()], outs=[self.d_st_o.ap().opt()]),
            waits=[(b.S("dg"), tso)], inc=b.S("cc"))
        b.op("ve", lambda e: e.memset(self.s_S[:, :], 0.0),
             waits=[(b.S("cc"), tag), (b.S("dg"), tso)])
        for j in range(3):
            tp = b.dma("sy", "dq", self.s_P[:, :],
                       self.d_st_o.ap()[j * 128:(j + 1) * 128, :],
                       waits=[(b.S("cc"), tag), (b.S("ve"), b.S("ve").n)])
            t1 = b.op("ve", lambda e, j=j: e.tensor_scalar(
                self.s_rowa[0:1, 0:16], self.s_srow[0:1, :],
                self.s_cvec[0:1, 3 + j:4 + j], -2048.0, ALU.mult, ALU.mult))
            t2 = b.op("sc", lambda e: e.activation(
                self.s_rowa[0:1, 0:16], self.s_rowa[0:1, 0:16], ACTF.Exp),
                waits=[(b.S("ve"), t1)])
            b.op("ve", lambda e, j=j: e.tensor_scalar(
                self.s_rowa[0:1, 0:16], self.s_rowa[0:1, 0:16],
                self.s_cvec[0:1, j:j + 1], None, ALU.mult), waits=[(b.S("ac"), t2)])
            t4 = b.op("ve", lambda e: e.tensor_copy(
                self.s_rowb[0:1, 0:2048].rearrange("o (h t) -> o h t", h=16),
                self.s_rowa[0:1, 0:16].unsqueeze(2).broadcast_to([1, 16, 128])))
            for i in range(4):
                sl = slice(i * 512, (i + 1) * 512)
                tg = b.op("pe", lambda e, i=i: e.matmul(
                    pb[5][:, 0:512], self.s_ones_row[0:1, :],
                    self.s_rowb[0:1, i * 512:(i + 1) * 512], start=True, stop=True),
                    waits=[(b.S("ve"), b.S("ve").n), (b.S("dq"), tp)], inc=b.S("pe"))
                b.op("ve", lambda e, sl=sl: e.tensor_tensor(
                    self.s_sc512[:, :], self.s_P[:, sl], pb[5][:, 0:512], ALU.mult),
                    waits=[(b.S("pe"), tg)])
                b.op("ve", lambda e, sl=sl: e.tensor_tensor(
                    self.s_S[:, sl], self.s_S[:, sl], self.s_sc512[:, :], ALU.add))
        b.op("ve", lambda e: e.tensor_copy(self.s_Sb[:, :], self.s_S[:, :]))

    def stage_corr_rms(self, a2s):
        nc, b, pb = self.nc, self.b, self.pb
        SBa = lambda n, shp, dt=F32: a2s.enter_context(nc.sbuf_tensor(n, shp, dt))
        s_qc = [SBa(f"s_aqc{i}", [128, 2048], BF16) for i in range(2)]
        s_oc = [SBa(f"s_aoc{i}", [128, 2048], FP16) for i in range(2)]
        s_qt = SBa("s_aqt", [128, 2048], FP16)
        s_o2 = SBa("s_o2", [128, 2048], F32)
        s_o2sq = SBa("s_o2sq", [128, 2048], F32)
        s_zc = SBa("s_zc", [128, 2048], BF16)
        s_g2row = SBa("s_g2row", [1, C], F32)
        s_g3row = SBa("s_g3row", [1, C], F32)
        s_g23row = SBa("s_g23row", [1, C], F32)
        s_g2sqc = SBa("s_g2sqc", [128, H], F32)
        tg2 = b.dma("sy", "dx", s_g2row[0:1, :], self.g2)
        tg3 = b.dma("sy", "dx", s_g3row[0:1, :], self.g3)
        b.op("ve", lambda e: e.tensor_tensor(
            s_g23row[0:1, :], s_g2row[0:1, :], s_g3row[0:1, :], ALU.mult),
            waits=[(b.S("dx"), tg3)])
        tgc = None
        for h in range(H):
            tgc = b.dma("sy", "dx", s_g2sqc[0:128, h:h + 1],
                        s_g2row[0:1, h * 128:(h + 1) * 128],
                        waits=[(b.S("dx"), tg2)])
        b.op("ve", lambda e: e.tensor_tensor(
            s_g2sqc[:, :], s_g2sqc[:, :], s_g2sqc[:, :], ALU.mult),
            waits=[(b.S("dx"), tgc)])

        tzst_prev = 0
        def corr_loads(c, war_ve):
            bufc = c % 2
            sem = "dq" if bufc == 0 else "dg"
            ap3 = lambda d: d.ap()[:, :].rearrange("p (h t) -> p h t", h=H)[
                :, :, c * 128:(c + 1) * 128]
            tq = b.dma("sy", sem, s_qc[bufc][:, :].rearrange("p (h t) -> p h t", t=128),
                       ap3(self.d_q), waits=[(b.S("ve"), war_ve)])
            to = b.dma("sy", sem, s_oc[bufc][:, :].rearrange("p (h t) -> p h t", t=128),
                       ap3(self.d_o))
            return sem, tq, to

        cr_pend = {0: corr_loads(0, b.S("ve").n)}
        cr_war = {0: 0, 1: 0}
        for c in range(NCH):
            bufc = c % 2
            if c + 1 < NCH:
                cr_pend[c + 1] = corr_loads(c + 1, cr_war[(c + 1) % 2])
            dqn, tq, to = cr_pend.pop(c)
            tqt = b.op("ve", lambda e, bufc=bufc: e.tensor_tensor(
                s_qt[:, :], s_qc[bufc][:, :], self.s_lt[:, :], ALU.mult),
                waits=[(b.S(dqn), tq), (b.S("pe"), b.S("pe").n)])
            tcr = None
            for h in range(H):
                tcr = b.op("pe", lambda e, h=h: e.matmul(
                    pb[h // 4][:, (h % 4) * 128:(h % 4 + 1) * 128],
                    self.s_Sb[:, h * 128:(h + 1) * 128],
                    s_qt[:, h * 128:(h + 1) * 128], start=True, stop=True),
                    waits=[(b.S("ve"), tqt)], inc=b.S("pe"))
            for i in range(4):
                b.op("ve", lambda e, i=i, bufc=bufc: e.scalar_tensor_tensor(
                    s_o2[:, i * 512:(i + 1) * 512], pb[i][:, 0:512], SCALE_ATTN,
                    s_oc[bufc][:, i * 512:(i + 1) * 512], ALU.mult, ALU.add),
                    waits=[(b.S("pe"), tcr), (b.S(dqn), to), (b.S("ac"), b.S("ac").n)])
            for i in range(4):
                sl = slice(i * 512, (i + 1) * 512)
                b.op("ve", lambda e, sl=sl: e.tensor_tensor(
                    self.s_S[:, sl], self.s_S[:, sl], self.s_lTf[:, sl], ALU.mult))
            b.op("ve", lambda e: e.tensor_copy(self.s_Sb[:, :], self.s_S[:, :]))
            tsq = b.op("sc", lambda e: e.activation(s_o2sq[:, :], s_o2[:, :], ACTF.Square),
                       waits=[(b.S("ve"), b.S("ve").n)])
            tst = tst2 = None
            for h in range(H):
                tst = b.op("pe", lambda e, h=h: e.matmul(
                    pb[4][0:1, 0:128], self.s_ones_col[:, :],
                    s_o2sq[:, h * 128:(h + 1) * 128], start=(h == 0), stop=(h == H - 1)),
                    waits=[(b.S("ac"), tsq), (b.S("ve"), b.S("ve").n)], inc=b.S("pe"))
            for h in range(H):
                tst2 = b.op("pe", lambda e, h=h: e.matmul(
                    pb[4][0:1, 128:256], s_g2sqc[:, h:h + 1],
                    s_o2sq[:, h * 128:(h + 1) * 128], start=(h == 0), stop=(h == H - 1)),
                    inc=b.S("pe"))
            b.op("ve", lambda e: e.tensor_scalar(
                self.s_rowa[0:1, 0:128], pb[4][0:1, 0:128], 1.0 / C, 1e-6,
                ALU.mult, ALU.add), waits=[(b.S("pe"), tst2)])
            b.op("ve", lambda e: e.reciprocal(self.s_rowa[0:1, 128:256],
                                              self.s_rowa[0:1, 0:128]))
            tr2 = b.op("sc", lambda e: e.activation(
                self.s_rowa[0:1, 256:384], self.s_rowa[0:1, 128:256], ACTF.Sqrt),
                waits=[(b.S("ve"), b.S("ve").n)])
            b.op("ve", lambda e: e.tensor_scalar(
                self.s_rowa[0:1, 384:512], pb[4][0:1, 128:256], 1.0 / C, None,
                ALU.mult), waits=[(b.S("ac"), tr2)])
            b.op("ve", lambda e: e.tensor_tensor(
                self.s_rowa[0:1, 512:640], self.s_rowa[0:1, 256:384],
                self.s_rowa[0:1, 256:384], ALU.mult))
            b.op("ve", lambda e: e.tensor_tensor(
                self.s_rowa[0:1, 640:768], self.s_rowa[0:1, 512:640],
                self.s_rowa[0:1, 384:512], ALU.mult))
            b.op("ve", lambda e: e.tensor_scalar(
                self.s_rowa[0:1, 640:768], self.s_rowa[0:1, 640:768], EPS_F32, None,
                ALU.add))
            b.op("ve", lambda e: e.reciprocal(self.s_rowa[0:1, 768:896],
                                              self.s_rowa[0:1, 640:768]))
            tr3 = b.op("sc", lambda e: e.activation(
                self.s_rowa[0:1, 896:1024], self.s_rowa[0:1, 768:896], ACTF.Sqrt))
            tr23 = b.op("ve", lambda e: e.tensor_tensor(
                self.s_rowa[0:1, 1024:1152], self.s_rowa[0:1, 256:384],
                self.s_rowa[0:1, 896:1024], ALU.mult), waits=[(b.S("ac"), tr3)])
            tgo = None
            for h in range(H):
                tgo = b.op("pe", lambda e, h=h: e.matmul(
                    pb[4 + h // 4][:, (h % 4) * 128:(h % 4 + 1) * 128],
                    s_g23row[0:1, h * 128:(h + 1) * 128],
                    self.s_rowa[0:1, 1024:1152], start=True, stop=True),
                    waits=[(b.S("ve"), tr23)], inc=b.S("pe"))
            for i in range(4):
                b.op("ve", lambda e, i=i: e.tensor_tensor(
                    s_zc[:, i * 512:(i + 1) * 512], s_o2[:, i * 512:(i + 1) * 512],
                    pb[4 + i][:, 0:512], ALU.mult),
                    waits=[(b.S("pe"), tgo), (b.S("dzq"), tzst_prev)])
            tzst_prev = b.op("sy", lambda e, c=c: e.dma_start(
                out=self.d_z.ap()[:, :].rearrange("p (h t) -> p h t", h=H)[
                    :, :, c * 128:(c + 1) * 128],
                in_=s_zc[:, :].rearrange("p (h t) -> p h t", t=128)),
                waits=[(b.S("ve"), b.S("ve").n)], inc=b.S("dzq"), inck=16)
            cr_war[c % 2] = b.S("ve").n

    def stage_proj_ln1(self, pjs):
        nc, b, pb = self.nc, self.b, self.pb
        SBp = lambda n, shp, dt=F32: pjs.enter_context(nc.sbuf_tensor(n, shp, dt))
        s_wp = SBp("s_wp", [128, KC * C], BF16)
        s_zfp = [SBp(f"s_zf{p}", [128, KC * 512], BF16) for p in range(2)]
        s_xt = [SBp(f"s_pxt{i}", [128, 512], F32) for i in range(2)]
        s_y = SBp("s_y", [128, KC * 512], F32)
        s_ysq = SBp("s_ysq", [128, 512], F32)
        s_z1 = [SBp(f"s_z1_{i}", [128, 512], F32) for i in range(2)]
        s_z1bt = [SBp(f"s_z1bt{i}", [128, 512], BF16) for i in range(2)]
        tw = None
        for kc in range(KC):
            tw = b.dma("sy", "dw", s_wp[:, kc * C:(kc + 1) * C],
                       self.d_wp.ap()[kc * 128:(kc + 1) * 128, :],
                       waits=[(b.S("cc"), self.t_ag_p)])
        tz1cast = {0: 0, 1: 0}
        tz1ca = {0: 0, 1: 0}

        def zf_loads(fch, war_pe):
            par = fch % 2
            sem = "dq" if par == 0 else "dw1"
            tzf = None
            for kc in range(KC):
                tzf = b.dma("sy", sem, s_zfp[par][:, kc * 512:(kc + 1) * 512],
                            self.d_z.ap()[:, kc * T + fch * 512: kc * T + (fch + 1) * 512],
                            waits=[(b.S("pe"), war_pe)] if kc == 0 else [])
            return sem, tzf

        zf_pend = {0: zf_loads(0, b.S("pe").n)}
        zf_war = {0: 0, 1: 0}
        for fch in range(4):
            if fch + 1 < 4:
                zf_pend[fch + 1] = zf_loads(fch + 1, zf_war[(fch + 1) % 2])
            dqn_f, tzf = zf_pend.pop(fch)
            s_zf = s_zfp[fch % 2]
            for m in range(KC):
                pm = pb[m % 2]
                tmm = None
                for kc in range(KC):
                    tmm = b.op("pe", lambda e, m=m, kc=kc, pm=pm, s_zf=s_zf: e.matmul(
                        pm[:, 0:512],
                        s_wp[:, kc * C + m * 128: kc * C + (m + 1) * 128],
                        s_zf[:, kc * 512:(kc + 1) * 512],
                        start=(kc == 0), stop=(kc == KC - 1)),
                        waits=[(b.S("dw"), tw), (b.S(dqn_f), tzf),
                               (b.S("ve"), b.S("ve").n)] if kc == 0 else [],
                        inc=b.S("pe"))
                txt = b.dma("sy", "dx", s_xt[m % 2][:, :],
                            self.xT[m * 128:(m + 1) * 128, fch * 512:(fch + 1) * 512],
                            waits=[(b.S("ve"), b.S("ve").n)])
                b.op("ve", lambda e, m=m, pm=pm: e.tensor_tensor(
                    s_y[:, m * 512:(m + 1) * 512], pm[:, 0:512], s_xt[m % 2][:, :],
                    ALU.add), waits=[(b.S("pe"), tmm), (b.S("dx"), txt),
                                     (b.S("ac"), b.S("ac").n)])
            tsy = tsq = None
            for m in range(KC):
                tsy = b.op("pe", lambda e, m=m: e.matmul(
                    pb[2][0:1, 0:512], self.s_ones_col[:, :],
                    s_y[:, m * 512:(m + 1) * 512], start=(m == 0), stop=(m == KC - 1)),
                    waits=[(b.S("ve"), b.S("ve").n)], inc=b.S("pe"))
            for m in range(KC):
                ts = b.op("sc", lambda e, m=m: e.activation(
                    s_ysq[:, :], s_y[:, m * 512:(m + 1) * 512], ACTF.Square),
                    waits=[(b.S("pe"), tsy if m == 0 else b.S("pe").n)])
                tsq = b.op("pe", lambda e, m=m: e.matmul(
                    pb[3][0:1, 0:512], self.s_ones_col[:, :], s_ysq[:, :],
                    start=(m == 0), stop=(m == KC - 1)),
                    waits=[(b.S("ac"), ts)], inc=b.S("pe"))
            b.op("ve", lambda e: e.tensor_scalar(
                self.s_rowa[0:1, 0:512], pb[2][0:1, 0:512], 1.0 / C, None, ALU.mult),
                waits=[(b.S("pe"), tsq)])
            b.op("ve", lambda e: e.tensor_scalar(
                self.s_rowa[0:1, 512:1024], pb[3][0:1, 0:512], 1.0 / C, None, ALU.mult))
            b.op("ve", lambda e: e.tensor_tensor(
                self.s_rowb[0:1, 0:512], self.s_rowa[0:1, 0:512],
                self.s_rowa[0:1, 0:512], ALU.mult))
            b.op("ve", lambda e: e.tensor_tensor(
                self.s_rowb[0:1, 0:512], self.s_rowa[0:1, 512:1024],
                self.s_rowb[0:1, 0:512], ALU.subtract))
            b.op("ve", lambda e: e.tensor_scalar(
                self.s_rowb[0:1, 0:512], self.s_rowb[0:1, 0:512], 1e-5, None, ALU.add))
            b.op("ve", lambda e: e.reciprocal(self.s_rowb[0:1, 512:1024],
                                              self.s_rowb[0:1, 0:512]))
            trr = b.op("sc", lambda e: e.activation(
                self.s_rowb[0:1, 0:512], self.s_rowb[0:1, 512:1024], ACTF.Sqrt))
            tmt = b.op("ve", lambda e: e.tensor_tensor(
                self.s_rowa[0:1, 1024:1536], self.s_rowa[0:1, 0:512],
                self.s_rowb[0:1, 0:512], ALU.mult), waits=[(b.S("ac"), trr)])
            if fch == 0:
                b.op("sy", lambda e: e.dma_start(
                    out=self.d_dbg.ap()[0:1, 0:512], in_=self.s_rowa[0:1, 0:512]),
                    waits=[(b.S("ve"), tmt)], inc=b.S("dst"), inck=16)
                b.op("sy", lambda e: e.dma_start(
                    out=self.d_dbg.ap()[1:2, 0:512], in_=self.s_rowa[0:1, 512:1024]),
                    inc=b.S("dst"), inck=16)
                b.op("sy", lambda e: e.dma_start(
                    out=self.d_dbg.ap()[2:3, 0:512], in_=self.s_rowb[0:1, 0:512]),
                    inc=b.S("dst"), inck=16)
                b.op("sy", lambda e: e.dma_start(
                    out=self.d_dbg.ap()[3:4, 0:512], in_=s_ysq[0:1, 0:512]),
                    inc=b.S("dst"), inck=16)
            tB = b.op("pe", lambda e: e.matmul(
                pb[4][:, 0:512], self.s_ones_row[0:1, :], self.s_rowb[0:1, 0:512],
                start=True, stop=True), waits=[(b.S("ve"), tmt)], inc=b.S("pe"))
            tC = b.op("pe", lambda e: e.matmul(
                pb[5][:, 0:512], self.s_ones_row[0:1, :], self.s_rowa[0:1, 1024:1536],
                start=True, stop=True), inc=b.S("pe"))
            tz1st = {0: 0, 1: 0}
            for m in range(KC):
                pbuf = m % 2
                b.op("ve", lambda e, m=m, pbuf=pbuf: e.tensor_tensor(
                    s_z1[pbuf][:, :], s_y[:, m * 512:(m + 1) * 512], pb[4][:, 0:512],
                    ALU.mult), waits=[(b.S("pe"), tC), (b.S("ac"), tz1ca[pbuf]),
                                      (b.S("dzq"), tz1st[pbuf])])
                tz1 = b.op("ve", lambda e, pbuf=pbuf: e.tensor_tensor(
                    s_z1[pbuf][:, :], s_z1[pbuf][:, :], pb[5][:, 0:512], ALU.subtract))
                tz1st[pbuf] = b.op("sy", lambda e, m=m, fch=fch, pbuf=pbuf: e.dma_start(
                    out=self.d_z1.ap()[:, m * T + fch * 512: m * T + (fch + 1) * 512],
                    in_=s_z1[pbuf][:, :]), waits=[(b.S("ve"), tz1)],
                    inc=b.S("dzq"), inck=16)
                tz1ca[pbuf] = b.op("sc", lambda e, pbuf=pbuf: e.activation(
                    s_z1bt[pbuf][:, :], s_z1[pbuf][:, :], ACTF.Copy),
                    waits=[(b.S("ve"), tz1), (b.S("dg"), tz1cast[pbuf])])
                tz1cast[pbuf] = b.op("sy", lambda e, m=m, fch=fch, pbuf=pbuf: e.dma_start(
                    out=self.d_z1b.ap()[:, m * T + fch * 512: m * T + (fch + 1) * 512],
                    in_=s_z1bt[pbuf][:, :]), waits=[(b.S("ac"), tz1ca[pbuf])],
                    inc=b.S("dg"), inck=16)
            zf_war[fch % 2] = b.S("pe").n

    def stage_ffn(self, ffs):
        nc, b, pb = self.nc, self.b, self.pb
        SBf = lambda n, shp, dt=F32: ffs.enter_context(nc.sbuf_tensor(n, shp, dt))
        s_z1b = SBf("s_fz1b", [128, KC * 1024], BF16)
        s_wfp = [SBf(f"s_fwf{p}", [128, KC * 512], BF16) for p in range(2)]
        s_hsq = SBf("s_fhsq", [128, 8 * 1024], FP16)
        s_y2 = SBf("s_fy2", [128, KC * 1024], F32)
        s_wj = [SBf(f"s_fwj{i}", [128, 8 * 128], FP16) for i in range(2)]
        s_ysq = SBf("s_fysq", [128, 1024], F32)
        s_z2 = [SBf(f"s_fz2_{i}", [128, 1024], F32) for i in range(2)]
        self.s_rowc = SBf("s_rowc", [1, 2048], F32)

        war_wf = {0: 0, 1: 0}
        twf_pend = {}

        def load_wf(ghi, war_tok):
            par = ghi % 2
            sem = "dw" if par == 0 else "dw1"
            c0 = (ghi % 16) * 512
            twf = None
            for kc in range(KC):
                twf = b.dma("sy", sem, s_wfp[par][:, kc * 512:(kc + 1) * 512],
                            self.d_wf.ap()[kc * 128:(kc + 1) * 128, c0:c0 + 512],
                            waits=[(b.S("cc"), self.t_ag_f),
                                   (b.S("pe"), war_tok)] if kc == 0 else [])
            return twf

        for ft in range(2):
            tz = ty = None
            for kc in range(KC):
                tz = b.dma("sy", "dq", s_z1b[:, kc * 1024:(kc + 1) * 1024],
                           self.d_z1b.ap()[:, kc * T + ft * 1024: kc * T + (ft + 1) * 1024],
                           waits=[(b.S("gpc"), b.S("gpc").n), (b.S("dg"), b.S("dg").n),
                                  (b.S("pe"), b.S("pe").n)] if kc == 0 else [])
                ty = b.dma("sy", "dx", s_y2[:, kc * 1024:(kc + 1) * 1024],
                           self.d_z1.ap()[:, kc * T + ft * 1024: kc * T + (ft + 1) * 1024],
                           waits=[(b.S("ve"), b.S("ve").n)] if kc == 0 else [])
            for g in range(8):
                for hh in range(2):
                    ghi = ft * 16 + g * 2 + hh
                    if ghi == 0:
                        twf_pend[0] = load_wf(0, b.S("pe").n)
                    if ghi + 1 < 32:
                        twf_pend[ghi + 1] = load_wf(ghi + 1, war_wf[(ghi + 1) % 2])
                    twf = twf_pend.pop(ghi)
                    swf = s_wfp[ghi % 2]
                    dwn_f = "dw" if ghi % 2 == 0 else "dw1"
                    for hx in range(4):
                        hc = hh * 4 + hx
                        pp0, pp1 = pb[2 * (hc % 2)], pb[2 * (hc % 2) + 1]
                        tmm = None
                        for kc in range(KC):
                            b.op("pe", lambda e, kc=kc, hx=hx, pp0=pp0, swf=swf: e.matmul(
                                pp0[:, 0:512],
                                swf[:, kc * 512 + hx * 128: kc * 512 + (hx + 1) * 128],
                                s_z1b[:, kc * 1024: kc * 1024 + 512],
                                start=(kc == 0), stop=(kc == KC - 1)),
                                waits=[(b.S(dwn_f), twf), (b.S("dq"), tz),
                                       (b.S("ve"), b.S("ve").n)] if kc == 0 else [],
                                inc=b.S("pe"))
                            tmm = b.op("pe", lambda e, kc=kc, hx=hx, pp1=pp1, swf=swf: e.matmul(
                                pp1[:, 0:512],
                                swf[:, kc * 512 + hx * 128: kc * 512 + (hx + 1) * 128],
                                s_z1b[:, kc * 1024 + 512: kc * 1024 + 1024],
                                start=(kc == 0), stop=(kc == KC - 1)), inc=b.S("pe"))
                        for half, ppx in ((0, pp0), (1, pp1)):
                            b.op("ve", lambda e, ppx=ppx: e.tensor_scalar(
                                s_ysq[:, 0:512], ppx[:, 0:512], 0.0, None, ALU.max),
                                waits=[(b.S("pe"), tmm), (b.S("ac"), b.S("ac").n)])
                            b.op("sc", lambda e, hc=hc, half=half: e.activation(
                                s_hsq[:, hc * 1024 + half * 512: hc * 1024 + (half + 1) * 512],
                                s_ysq[:, 0:512], ACTF.Square), waits=[(b.S("ve"), b.S("ve").n)])
                    war_wf[ghi % 2] = b.S("pe").n
                for m in range(KC):
                    twj = b.dma("sy", "dw", s_wj[m % 2][:, :],
                                self.d_wj.ap()[g * 1024:(g + 1) * 1024,
                                               m * 128:(m + 1) * 128].rearrange(
                                    "(hc p) o -> p hc o", p=128),
                                waits=[(b.S("pe"), b.S("pe").n),
                                       (b.S("cc"), self.t_ag_j)])
                    pm0, pm1 = pb[4 + 2 * (m % 2)], pb[4 + 2 * (m % 2) + 1]
                    tpj = None
                    for hc in range(8):
                        b.op("pe", lambda e, hc=hc, m=m, pm0=pm0: e.matmul(
                            pm0[:, 0:512], s_wj[m % 2][:, hc * 128:(hc + 1) * 128],
                            s_hsq[:, hc * 1024: hc * 1024 + 512],
                            start=(hc == 0), stop=(hc == 7)),
                            waits=[(b.S("dw"), twj), (b.S("ac"), b.S("ac").n),
                                   (b.S("ve"), b.S("ve").n)]
                            if hc == 0 else [], inc=b.S("pe"))
                        tpj = b.op("pe", lambda e, hc=hc, m=m, pm1=pm1: e.matmul(
                            pm1[:, 0:512], s_wj[m % 2][:, hc * 128:(hc + 1) * 128],
                            s_hsq[:, hc * 1024 + 512: hc * 1024 + 1024],
                            start=(hc == 0), stop=(hc == 7)), inc=b.S("pe"))
                    b.op("ve", lambda e, m=m, pm0=pm0: e.tensor_tensor(
                        s_y2[:, m * 1024: m * 1024 + 512],
                        s_y2[:, m * 1024: m * 1024 + 512], pm0[:, 0:512], ALU.add),
                        waits=[(b.S("pe"), tpj), (b.S("dx"), ty)])
                    b.op("ve", lambda e, m=m, pm1=pm1: e.tensor_tensor(
                        s_y2[:, m * 1024 + 512: m * 1024 + 1024],
                        s_y2[:, m * 1024 + 512: m * 1024 + 1024], pm1[:, 0:512],
                        ALU.add))
            if ft == 1:
                b.op("sy", lambda e: e.dma_start(out=self.d_dbg2.ap(), in_=s_y2[:, :]),
                     waits=[(b.S("ve"), b.S("ve").n)], inc=b.S("dst"), inck=16)
                b.op("sy", lambda e: e.dma_start(out=self.d_dbg3.ap(), in_=s_hsq[:, :]),
                     waits=[(b.S("ac"), b.S("ac").n)], inc=b.S("dst"), inck=16)
            tsy = tsq = None
            for m in range(KC):
                tsy = b.op("pe", lambda e, m=m: e.matmul(
                    pb[0][0:1, 0:512], self.s_ones_col[:, :],
                    s_y2[:, m * 1024: m * 1024 + 512], start=(m == 0), stop=False if m < KC - 1 else True),
                    waits=[(b.S("ve"), b.S("ve").n)], inc=b.S("pe"))
                tsy = b.op("pe", lambda e, m=m: e.matmul(
                    pb[1][0:1, 0:512], self.s_ones_col[:, :],
                    s_y2[:, m * 1024 + 512: m * 1024 + 1024], start=(m == 0),
                    stop=False if m < KC - 1 else True), inc=b.S("pe"))
            for m in range(KC):
                ts = b.op("sc", lambda e, m=m: e.activation(
                    s_ysq[:, :], s_y2[:, m * 1024:(m + 1) * 1024], ACTF.Square),
                    waits=[(b.S("pe"), tsy if m == 0 else b.S("pe").n)])
                tsq = b.op("pe", lambda e, m=m: e.matmul(
                    pb[2][0:1, 0:512], self.s_ones_col[:, :], s_ysq[:, 0:512],
                    start=(m == 0), stop=(m == KC - 1)),
                    waits=[(b.S("ac"), ts)], inc=b.S("pe"))
                tsq = b.op("pe", lambda e, m=m: e.matmul(
                    pb[3][0:1, 0:512], self.s_ones_col[:, :], s_ysq[:, 512:1024],
                    start=(m == 0), stop=(m == KC - 1)), inc=b.S("pe"))
            for hf, pbm, pbq in ((0, pb[0], pb[2]), (1, pb[1], pb[3])):
                o = hf * 512
                b.op("ve", lambda e, pbm=pbm, o=o: e.tensor_scalar(
                    self.s_rowa[0:1, o:o + 512], pbm[0:1, 0:512], 1.0 / C, None,
                    ALU.mult), waits=[(b.S("pe"), tsq)])
                b.op("ve", lambda e, pbq=pbq, o=o: e.tensor_scalar(
                    self.s_rowb[0:1, o:o + 512], pbq[0:1, 0:512], 1.0 / C, None,
                    ALU.mult))
                b.op("ve", lambda e, o=o: e.tensor_tensor(
                    self.s_rowc[0:1, o:o + 512], self.s_rowa[0:1, o:o + 512],
                    self.s_rowa[0:1, o:o + 512], ALU.mult))
                b.op("ve", lambda e, o=o: e.tensor_tensor(
                    self.s_rowc[0:1, o:o + 512], self.s_rowb[0:1, o:o + 512],
                    self.s_rowc[0:1, o:o + 512], ALU.subtract))
                b.op("ve", lambda e, o=o: e.tensor_scalar(
                    self.s_rowc[0:1, o:o + 512], self.s_rowc[0:1, o:o + 512], 1e-5,
                    None, ALU.add))
                b.op("ve", lambda e, o=o: e.reciprocal(
                    self.s_rowb[0:1, o:o + 512], self.s_rowc[0:1, o:o + 512]))
                trr = b.op("sc", lambda e, o=o: e.activation(
                    self.s_rowc[0:1, o:o + 512], self.s_rowb[0:1, o:o + 512],
                    ACTF.Sqrt))
                b.op("ve", lambda e, o=o: e.tensor_tensor(
                    self.s_rowa[0:1, o:o + 512], self.s_rowa[0:1, o:o + 512],
                    self.s_rowc[0:1, o:o + 512], ALU.mult), waits=[(b.S("ac"), trr)])
            touts = []
            for hf in range(2):
                o = hf * 512
                tB = b.op("pe", lambda e, o=o, hf=hf: e.matmul(
                    pb[4][:, 0:512] if hf == 0 else pb[5][:, 0:512],
                    self.s_ones_row[0:1, :], self.s_rowc[0:1, o:o + 512],
                    start=True, stop=True), waits=[(b.S("ve"), b.S("ve").n)],
                    inc=b.S("pe"))
                tC = b.op("pe", lambda e, o=o, hf=hf: e.matmul(
                    pb[6][:, 0:512] if hf == 0 else pb[7][:, 0:512],
                    self.s_ones_row[0:1, :], self.s_rowa[0:1, o:o + 512],
                    start=True, stop=True), inc=b.S("pe"))
                touts.append(tC)
            tz2st = {0: 0, 1: 0}
            for m in range(KC):
                pbuf = m % 2
                for hf in range(2):
                    pB = pb[4] if hf == 0 else pb[5]
                    pC = pb[6] if hf == 0 else pb[7]
                    b.op("ve", lambda e, m=m, hf=hf, pbuf=pbuf, pB=pB: e.tensor_tensor(
                        s_z2[pbuf][:, hf * 512:(hf + 1) * 512],
                        s_y2[:, m * 1024 + hf * 512: m * 1024 + (hf + 1) * 512],
                        pB[:, 0:512], ALU.mult),
                        waits=[(b.S("pe"), touts[-1]), (b.S("dst"), tz2st[pbuf])])
                    b.op("ve", lambda e, hf=hf, pbuf=pbuf, pC=pC: e.tensor_tensor(
                        s_z2[pbuf][:, hf * 512:(hf + 1) * 512],
                        s_z2[pbuf][:, hf * 512:(hf + 1) * 512], pC[:, 0:512],
                        ALU.subtract))
                tzv = b.S("ve").n
                tz2st[pbuf] = b.op("sy", lambda e, m=m, ft=ft, pbuf=pbuf: e.dma_start(
                    out=self.out[m * 128:(m + 1) * 128, ft * 1024:(ft + 1) * 1024],
                    in_=s_z2[pbuf][:, :]), waits=[(b.S("ve"), tzv)],
                    inc=b.S("dst"), inck=16)


TAP_DEFS = {
    "t_q": ([128, H * T], BF16), "t_k": ([128, H * T], BF16),
    "t_v": ([128, H * T], BF16), "t_o": ([128, H * T], FP16),
    "t_z": ([128, H * T], BF16), "t_z1": ([128, KC * T], F32),
    "t_z1b": ([128, KC * T], BF16), "t_wq": ([C, QF], BF16),
    "t_wp": ([C, C], BF16), "t_wf": ([C, FH], BF16), "t_wj": ([FH, C], FP16),
    "t_brow": ([20, FH], F32), "t_rrow": ([20, FH], F32),
    "t_sto": ([512, H * D], F32),
    "t_dbg": ([8, 2048], F32),
    "t_dbg2": ([128, KC * 1024], F32),
    "t_dbg3": ([128, 8 * 1024], FP16),
}


def kernel(**inputs):
    in_maps = _host_shard(**inputs)
    nc = Builder(taps=TAPS).build()
    res = run_bass_kernel_spmd(nc, in_maps, core_ids=list(range(N_CORES)))
    full = np.empty((2, 8192, 2048), dtype=np.float32)
    for c in range(N_CORES):
        bb, seg = c // 4, c % 4
        full[bb, seg * T:(seg + 1) * T, :] = res.results[c]["out"].T
    if TAPS:
        kernel.tap_results = [{t: res.results[c][t] for t in TAPS}
                              for c in range(N_CORES)]
    return full
